# version 10
# baseline (speedup 1.0000x reference)
import math
import numpy as np
import ml_dtypes
import concourse.bass as bass
import concourse.mybir as mybir
from concourse.bass_utils import run_bass_kernel_spmd

F32 = mybir.dt.float32
BF16 = mybir.dt.bfloat16
AF = mybir.ActivationFunctionType
ALU = mybir.AluOpType
AXL = mybir.AxisListType

NCORES = 8
T = 2048
HALO = 66
TIN = T + HALO
VH = 63
TV = T + VH
H = 4
DK = 256
DV = 256
CH = 128
NCH = T // CH
ST = 512
NST = T // ST
D = 1024
KT = D // 128
GH = 2048
P5 = 5

DEBUG = []
CVALS = [1e-6, 1e-5, 0.5, -1.0, 1.0, 0.0, 0.2, 2.0]


class Prog:
    ENGINES = ("pe", "dve", "act", "pool", "sp")

    def __init__(self):
        self.steps = {e: [] for e in self.ENGINES}
        self.counts = {}

    def on(self, eng, fn, waits=(), inc=None, amt=1):
        tok = None
        if inc is not None:
            self.counts[inc] = self.counts.get(inc, 0) + amt
            tok = (inc, self.counts[inc])
        self.steps[eng].append((fn, list(waits), inc, amt))
        return tok

    def dma(self, eng, fn, waits=(), inc=None):
        return self.on(eng, fn, waits, inc, amt=16)

    def bump(self, sem, amt):
        self.counts[sem] = self.counts.get(sem, 0) + amt
        return (sem, self.counts[sem])

    def emit(self, nc, sems):
        self.sems = sems
        block_attr = {"pe": "tensor", "dve": "vector", "act": "scalar",
                      "pool": "gpsimd", "sp": "sync"}
        with nc.Block() as block:
            for eng_name in self.ENGINES:
                steps = self.steps[eng_name]
                if not steps:
                    continue
                def make_body(steps):
                    def body(eng):
                        for fn, waits, inc, amt in steps:
                            for (sem, cnt) in waits:
                                eng.wait_ge(sems[sem], cnt)
                            last = fn(eng)
                            if inc is not None:
                                assert last is not None, f"step for {inc} returned no instruction"
                                last.then_inc(sems[inc], amt)
                    return body
                getattr(block, block_attr[eng_name])(make_body(steps))


def ceil_div(a, b):
    return (a + b - 1) // b


def build_nc():
    nc = bass.Bass("TRN2", target_bir_lowering=False, debug=False)
    p = Prog()

    din = {}
    def inp(name, shape, dtype=F32):
        din[name] = nc.dram_tensor(name, shape, dtype, kind="ExternalInput").ap()
        return din[name]

    hsT = inp("hsT", [D, TIN])
    wq = inp("wq", [D, D]); wk = inp("wk", [D, D]); wv = inp("wv", [D, D])
    wb = inp("wb", [D, H])
    conv_q = inp("conv_q", [D, 4]); conv_k = inp("conv_k", [D, 4]); conv_v = inp("conv_v", [D, 4])
    fir_s = inp("fir_s", [D, 3]); fir_m = inp("fir_m", [D, 15]); fir_l = inp("fir_l", [D, 64])
    gw1_slice = inp("gw1_slice", [20 * 128, GH])
    gw1_hs = inp("gw1_hs", [D, GH])
    gb1 = inp("gb1", [128, 16]); gw2 = inp("gw2", [GH, 20]); gb2 = inp("gb2", [20])
    log_tau = inp("log_tau", [H]); eps_logit = inp("eps_logit", [H, 5])
    norm_w = inp("norm_w", [128, 2]); wo = inp("wo", [D, D])
    cn_w1 = inp("cn_w1", [D, D]); cn_b1 = inp("cn_b1", [128, 8])
    cn_w2 = inp("cn_w2", [D, 512]); cn_b2 = inp("cn_b2", [128, 4])
    cn_w3 = inp("cn_w3", [512, 256]); cn_b3 = inp("cn_b3", [128, 2])
    cn_w4 = inp("cn_w4", [256, 1]); cn_b4 = inp("cn_b4", [1])
    gc_w1 = inp("gc_w1", [D, 512]); gc_b1 = inp("gc_b1", [128, 4])
    gc_w2 = inp("gc_w2", [512, 256]); gc_b2 = inp("gc_b2", [128, 2])
    gc_w3 = inp("gc_w3", [256, 1]); gc_b3 = inp("gc_b3", [1])
    adaptive_range = inp("adaptive_range", [1])
    eye16 = inp("eye16", [128, 128], BF16)
    eyef = inp("eyef", [128, 128])
    mask3 = inp("mask3", [128, 384])
    sel20 = inp("sel20", [20, 20, 128])
    sel_m = inp("sel_m", [128, 20, 80], BF16)
    sel_v = inp("sel_v", [128, 20, 80], BF16)
    sel_a = inp("sel_a", [128, 20, 80], BF16)
    ones1 = inp("ones1", [1, 128], BF16)
    ones1f = inp("ones1f", [1, 128])
    onesc = inp("onesc", [128, 1], BF16)
    onescf = inp("onescf", [128, 1])
    g0c = inp("g0c", [128, 2, 512], BF16)
    gsel = inp("gsel", [128, 20, 10])
    pref_mask = inp("pref_mask", [128, 8])
    pref_inv = inp("pref_inv", [128, 8])
    cvals = inp("cvals", [128, 8])
    cvals16 = inp("cvals16", [128, 8], BF16)

    out = nc.dram_tensor("out", [T, D], F32, kind="ExternalOutput").ap()

    dbg = {}
    for name in DEBUG:
        shp = DEBUG_SHAPES[name]
        dbg[name] = nc.dram_tensor("dbg_" + name, shp, F32, kind="ExternalOutput").ap()

    cc_g_in = nc.dram_tensor("cc_g_in", [H * 2 * 128, 512], F32).ap()
    cc_g_out = nc.dram_tensor("cc_g_out", [8 * H * 2 * 128, 512], F32).ap()
    cc_gw_in = nc.dram_tensor("cc_gw_in", [10, GH], F32).ap()
    cc_gw_out = nc.dram_tensor("cc_gw_out", [80, GH], F32).ap()
    cc_mean_in = nc.dram_tensor("cc_mean_in", [128, 8], F32).ap()
    cc_mean_out = nc.dram_tensor("cc_mean_out", [128, 8], F32).ap()
    x1_dram = nc.dram_tensor("x1_dram", [D, T], F32).ap()

    sem_names = ["ld", "pe", "dve", "act", "pool", "sp", "lw", "st", "cc", "ld2", "ld3",
                 "lda", "ldb", "lwa", "lwb", "pld"]

    import contextlib
    guards = {}
    tensors = {}

    def salloc(name, shape, dtype):
        g = nc.sbuf_tensor(name, shape, dtype)
        t = g.__enter__()
        guards[name] = g
        tensors[name] = t
        return t

    def sfree(*names):
        for n in names:
            guards.pop(n).__exit__(None, None, None)

    def palloc(name, shape, dtype=F32):
        g = nc.psum_tensor(name, shape, dtype)
        t = g.__enter__()
        guards[name] = g
        tensors[name] = t
        return t

    build_program(nc, p, din, out, dbg,
                  dict(cc_g_in=cc_g_in, cc_g_out=cc_g_out, cc_gw_in=cc_gw_in,
                       cc_gw_out=cc_gw_out, cc_mean_in=cc_mean_in,
                       cc_mean_out=cc_mean_out, x1_dram=x1_dram),
                  salloc, sfree, palloc)

    with contextlib.ExitStack() as ctx:
        sems = {s: ctx.enter_context(nc.semaphore(s)) for s in sem_names}
        p.emit(nc, sems)
    for n in reversed(list(guards)):
        guards.pop(n).__exit__(None, None, None)
    return nc


DEBUG_SHAPES = {
    "o16": [128, KT, T],
    "x1dump": [D, T],
    "probs": [NST, 128, 4, 20],
    "logitsdump": [NST, 128, 4, 20],
    "rowsdump": [1, 128],
    "gwout": [80, GH],
    "gredump": [10, GH],
    "statsdump": [NST, 128, 512],
    "epsbcdump": [128, 40],
    "gw2dump": [128, 16, 20],
    "ocomb": [NST, 128, KT, 512],
    "stats80": [128, T],
    "hgate": [128, 16, 512],
    "gout": [8 * H * 2 * 128, 512],
    "fst": [128, H * NCH, 128],
    "sinit": [128, 2 * H, 256],
    "q16": [128, KT, T],
    "k16": [128, KT, T],
    "v16": [128, KT, TV],
    "beta": [128, NCH, H],
    "rsq": [1, 8, T],
}


def build_program(nc, p, din, out, dbg, dram, salloc, sfree, palloc):
    st = {}
    st.update(nc=nc, p=p, din=din, out=out, dbg=dbg, dram=dram,
              salloc=salloc, sfree=sfree, palloc=palloc)
    phase15_gw1(st)
    phase1_front(st)
    phase2_delta(st)
    phase3_gate(st)
    phase4_ffn(st)
    finish_debug(st)


def phase1_front(st):
    nc, p, din = st["nc"], st["p"], st["din"]
    salloc, palloc = st["salloc"], st["palloc"]

    cw = salloc("cw", [128, KT, 12], F32)
    s_eye16 = salloc("s_eye16", [128, 128], BF16)
    s_eyef = salloc("s_eyef", [128, 128], F32)
    s_onesc = salloc("s_onesc", [128, 1], BF16)
    s_wb = salloc("s_wb", [128, KT, H], BF16)
    s_ones1f = salloc("s_ones1f", [1, 128], F32)
    s_ones1 = salloc("s_ones1", [1, 128], BF16)
    s_onescf = salloc("s_onescf", [128, 1], F32)
    s_mask3 = salloc("s_mask3", [128, 384], F32)
    s_cvals = salloc("s_cvals", [128, 8], F32)
    s_cvals16 = salloc("s_cvals16", [128, 8], BF16)
    for i, v in enumerate(CVALS):
        nc.const_aps.register(F32, v, s_cvals[:, i:i + 1])
        nc.const_aps.register(BF16, v, s_cvals16[:, i:i + 1])

    v16 = salloc("v16", [128, KT, TV], BF16)
    o16 = salloc("o16", [128, KT, T], BF16)
    q16 = salloc("q16", [128, KT, T], BF16)
    k16 = salloc("k16", [128, KT, T], BF16)
    beta_tm = salloc("beta_tm", [128, NCH, H], F32)
    hs16 = salloc("hs16", [128, KT, TIN], BF16)
    st["o16"] = o16
    rsq = salloc("rsq", [1, 576], F32)
    st.update(hs16=hs16, q16=q16, k16=k16, v16=v16, beta_tm=beta_tm,
              s_eye16=s_eye16, s_eyef=s_eyef, s_onesc=s_onesc, rsq=rsq,
              s_ones1f=s_ones1f, s_ones1=s_ones1, s_onescf=s_onescf,
              s_mask3=s_mask3, s_cvals=s_cvals, s_cvals16=s_cvals16)

    hsf = salloc("hsf", [128, 2, KT, 544], F32)

    pp = [palloc("pp0", [128, 512], F32), palloc("pp1", [128, 512], F32)]

    NC_IN = ceil_div(TIN, 512)
    def chunk_rng(i, total):
        lo = 512 * i
        return lo, min(512, total - lo)

    castok = {}
    for i in range(NC_IN):
        lo, ln = chunk_rng(i, TIN)
        bi = i % 2
        def ld(eng, lo=lo, ln=ln, bi=bi):
            return eng.dma_start(out=hsf[:, bi, :, 0:ln],
                                 in_=din["hsT"].rearrange("(a p) t -> p a t", p=128)[:, :, lo:lo + ln])
        waits = [castok[i - 2]] if i >= 2 else []
        ldtok = p.dma("sp", ld, waits=waits, inc="lda" if bi == 0 else "ldb")
        def cast(eng, lo=lo, ln=ln, bi=bi):
            return eng.tensor_copy(hs16[:, :, lo:lo + ln], hsf[:, bi, :, 0:ln])
        castok[i] = p.on("dve", cast, waits=[ldtok], inc="dve")
    hs16_ready = castok[NC_IN - 1]
    st["hs16_ready"] = hs16_ready

    def ld_misc(eng):
        sem = p.sems["ld2"]
        eng.dma_start(out=cw[:, :, 0:4], in_=din["conv_q"].rearrange("(a p) j -> p a j", p=128)).then_inc(sem, 16)
        eng.dma_start(out=cw[:, :, 4:8], in_=din["conv_k"].rearrange("(a p) j -> p a j", p=128)).then_inc(sem, 16)
        eng.dma_start(out=cw[:, :, 8:12], in_=din["conv_v"].rearrange("(a p) j -> p a j", p=128)).then_inc(sem, 16)
        eng.dma_start(out=s_eye16[:], in_=din["eye16"]).then_inc(sem, 16)
        eng.dma_start(out=s_eyef[:], in_=din["eyef"]).then_inc(sem, 16)
        eng.dma_start(out=s_mask3[:], in_=din["mask3"]).then_inc(sem, 16)
        eng.dma_start(out=s_ones1f[:], in_=din["ones1f"]).then_inc(sem, 16)
        eng.dma_start(out=s_ones1[:], in_=din["ones1"]).then_inc(sem, 16)
        eng.dma_start(out=s_onescf[:], in_=din["onescf"]).then_inc(sem, 16)
        eng.dma_start(out=s_cvals[:], in_=din["cvals"]).then_inc(sem, 16)
        eng.dma_start(out=s_cvals16[:], in_=din["cvals16"]).then_inc(sem, 16)
        eng.dma_start(out=s_onesc[:], in_=din["onesc"]).then_inc(sem, 16)
        return None
    p.on("sp", ld_misc, waits=[])
    misc_tok = p.bump("ld2", 16 * 12)
    st["misc_tok"] = misc_tok

    def ld_wb(eng):
        return eng.dma_start(out=hsf[:, 0, :, 512:512 + H],
                             in_=din["wb"].rearrange("(a p) h -> p a h", p=128))
    wb_tok = p.dma("sp", ld_wb, waits=[castok[NC_IN - 2]], inc="ld2")
    def cast_wb(eng):
        return eng.tensor_copy(s_wb[:], hsf[:, 0, :, 512:512 + H])
    wb16_tok = p.on("dve", cast_wb, waits=[wb_tok], inc="dve")

    st["sfree"]("hsf")
    wbuf = salloc("wbuf", [128, 2, KT, 128], F32)
    wbuf16 = salloc("wbuf16", [128, 2, KT, 128], BF16)
    raw = salloc("raw", [128, 2, TIN], BF16)
    cacc = salloc("cacc", [128, 544], F32)

    projs = [("wq", 0, q16, T, 66), ("wk", 1, k16, T, 66), ("wv", 2, v16, TV, 3)]
    prev_cast = {}
    prev_silu = None
    ev_toks = []
    raw_free = {}
    step = 0
    for wi, (wname, cidx, dst, tlen, off) in enumerate(projs):
        for m in range(KT):
            gi = wi * KT + m
            bi = gi % 2
            def ldw(eng, wname=wname, m=m, bi=bi):
                return eng.dma_start(
                    out=wbuf[:, bi],
                    in_=din[wname].rearrange("(a p) n -> p a n", p=128)[:, :, 128 * m:128 * (m + 1)])
            waits = [prev_cast[bi]] if prev_cast.get(bi) else [hs16_ready, wb16_tok]
            lw_tok = p.dma("sp", ldw, waits=waits, inc="lwa" if bi == 0 else "lwb")
            def castw(eng, bi=bi):
                return eng.tensor_copy(wbuf16[:, bi], wbuf[:, bi])
            cw_tok = p.on("dve", castw, waits=[lw_tok], inc="dve")
            prev_cast[bi] = cw_tok
            rwaits = [raw_free[bi]] if bi in raw_free else []
            for i in range(NC_IN):
                lo, ln = chunk_rng(i, TIN)
                ps = pp[step % 2]
                def mm(eng, bi=bi, lo=lo, ln=ln, ps=ps):
                    ins = None
                    for kt in range(KT):
                        ins = eng.matmul(ps[:, 0:ln], wbuf16[:, bi, kt, :],
                                         hs16[:, kt, lo:lo + ln], start=(kt == 0), stop=(kt == KT - 1))
                    return ins
                waits = [cw_tok, hs16_ready]
                if step >= 2:
                    waits.append(ev_toks[step - 2])
                mm_tok = p.on("pe", mm, waits=waits, inc="pe")
                def ev(eng, bi=bi, lo=lo, ln=ln, ps=ps):
                    return eng.copy(raw[:, bi, lo:lo + ln], ps[:, 0:ln])
                ev_toks.append(p.on("act", ev, waits=[mm_tok] + (rwaits if i == 0 else []), inc="act"))
                step += 1
            raw_ready = ev_toks[-1]
            NCC = ceil_div(tlen, 512)
            cv_tok = None
            for i in range(NCC):
                lo, ln = chunk_rng(i, tlen)
                base = off - 3 + lo
                def conv(eng, m=m, bi=bi, lo=lo, ln=ln, base=base, cidx=cidx):
                    eng.tensor_scalar(out=cacc[:, 0:ln], in0=raw[:, bi, base:base + ln],
                                      scalar1=cw[:, m, 4 * cidx:4 * cidx + 1], scalar2=None,
                                      op0=ALU.mult)
                    ins = None
                    for j in range(1, 4):
                        ins = eng.scalar_tensor_tensor(
                            out=cacc[:, 0:ln], in0=raw[:, bi, base + j:base + j + ln],
                            scalar=cw[:, m, 4 * cidx + j:4 * cidx + j + 1],
                            in1=cacc[:, 0:ln], op0=ALU.mult, op1=ALU.add)
                    return ins
                cv_tok = p.on("dve", conv, waits=[raw_ready, misc_tok] + ([prev_silu] if prev_silu else []), inc="dve")
                def silu(eng, m=m, lo=lo, ln=ln, dst=dst):
                    return eng.activation(out=dst[:, m, lo:lo + ln], in_=cacc[:, 0:ln], func=AF.Silu)
                prev_silu = p.on("act", silu, waits=[cv_tok], inc="act")
            raw_free[bi] = cv_tok
    silu_all = ("act", p.counts["act"])
    st["qkv_ready"] = silu_all

    ppb = palloc("ppb", [128, H], F32)
    beta_ev = None
    for c in range(NCH):
        lo = HALO + CH * c
        def mmb(eng, lo=lo):
            ins = None
            for kt in range(KT):
                ins = eng.matmul(ppb[:], hs16[:, kt, lo:lo + CH], s_wb[:, kt, :],
                                 start=(kt == 0), stop=(kt == KT - 1))
            return ins
        tok = p.on("pe", mmb, waits=[hs16_ready, wb16_tok] + ([beta_ev] if beta_ev else []), inc="pe")
        def sig(eng, c=c):
            return eng.activation(out=beta_tm[:, c, :], in_=ppb[:], func=AF.Sigmoid)
        beta_ev = p.on("act", sig, waits=[tok], inc="act")
    st["beta_ready"] = beta_ev

    st["l2_done"] = l2norm(st, pp, q16, k16, s_onesc, silu_all)

    emit_debug(st, dict(q16=q16, k16=k16, v16=v16, beta=beta_tm))
    st["sfree"]("cacc", "raw", "wbuf16", "wbuf")
    st["sfree"]("rsq")
    for nm in ("pbc", "prs", "ppb", "pp1", "pp0"):
        st["sfree"](nm)
    st["sfree"]("hs16")


def l2norm(st, pp, q16, k16, s_onesc, qkv_ready):
    nc, p = st["nc"], st["p"]
    salloc, palloc = st["salloc"], st["palloc"]
    rsq = st["rsq"]
    sq = salloc("sq", [128, 2, 512], BF16)
    prs = palloc("prs", [1, 512], F32)
    pbc = palloc("pbc", [128, 512], F32)
    bc = salloc("bc", [128, 512], F32)
    last_sc = None
    prev_sr = None
    prev_bc = None
    for ti, tens in ((0, q16), (1, k16)):
        for h in range(H):
            for i in range(NST):
                lo = ST * i
                def sqf(eng, tens=tens, h=h, lo=lo):
                    eng.tensor_mul(sq[:, 0, :], tens[:, 2 * h, lo:lo + ST], tens[:, 2 * h, lo:lo + ST])
                    return eng.tensor_mul(sq[:, 1, :], tens[:, 2 * h + 1, lo:lo + ST], tens[:, 2 * h + 1, lo:lo + ST])
                sq_tok = p.on("dve", sqf, waits=[qkv_ready, st["misc_tok"]] + ([last_sc] if last_sc else []), inc="dve")
                def mm(eng):
                    eng.matmul(prs[:], s_onesc[:], sq[:, 0, :], start=True, stop=False)
                    return eng.matmul(prs[:], s_onesc[:], sq[:, 1, :], start=False, stop=True)
                mm_tok = p.on("pe", mm, waits=[sq_tok] + ([prev_sr] if prev_sr else []), inc="pe")
                def sqr(eng):
                    return eng.activation(out=rsq[:, 0:ST], in_=prs[:], func=AF.Sqrt, bias=1e-6)
                sr_tok = p.on("act", sqr, waits=[mm_tok] + ([prev_bc] if prev_bc else []), inc="act")
                prev_sr = sr_tok
                def rec(eng):
                    return eng.reciprocal(rsq[:, 0:ST], rsq[:, 0:ST])
                rc_tok = p.on("dve", rec, waits=[sr_tok], inc="dve")
                def bcm(eng):
                    return eng.matmul(pbc[:], st["s_ones1f"][:], rsq[:, 0:ST],
                                      start=True, stop=True)
                bc_tok = p.on("pe", bcm, waits=[rc_tok] + ([last_sc] if last_sc else []), inc="pe")
                prev_bc = bc_tok
                def evbc(eng):
                    return eng.tensor_copy(bc[:], pbc[:])
                ev_tok = p.on("dve", evbc, waits=[bc_tok], inc="dve")
                def scale(eng, tens=tens, h=h, lo=lo):
                    eng.tensor_mul(tens[:, 2 * h, lo:lo + ST], tens[:, 2 * h, lo:lo + ST], bc[:])
                    return eng.tensor_mul(tens[:, 2 * h + 1, lo:lo + ST], tens[:, 2 * h + 1, lo:lo + ST], bc[:])
                last_sc = p.on("dve", scale, waits=[ev_tok], inc="dve")
    st["sfree"]("bc", "sq")
    return last_sc


def emit_debug(st, mapping):
    p, dbg = st["p"], st["dbg"]
    for name, tens in mapping.items():
        if name in dbg:
            def store(eng, tens=tens, name=name):
                return eng.dma_start(out=dbg[name], in_=tens[:])
            waits = [(s, c) for s, c in p.counts.items()]
            p.dma("pool", store, waits=waits, inc="st")


def finish_debug(st):
    p = st["p"]
    def fin(eng):
        return eng.nop()
    waits = [(s, c) for s, c in p.counts.items()]
    p.on("sp", fin, waits=waits)



def make_consts():
    c = {}
    c["eye16"] = np.eye(128).astype(ml_dtypes.bfloat16)
    c["eyef"] = np.eye(128, dtype=np.float32)
    i = np.arange(128)
    mb = np.zeros((128, 384), np.float32)
    mb[:, 0:128] = -(i[:, None] < i[None, :]).astype(np.float32)
    mb[:, 128:256] = -(i[:, None] > i[None, :]).astype(np.float32)
    mb[:, 256:384] = (i[:, None] <= i[None, :]).astype(np.float32)
    c["mask3"] = mb
    sel = np.zeros((20, 20, 128), np.float32)
    for r in range(20):
        sel[r, r, :] = 1.0
    c["sel20"] = sel
    c["ones1"] = np.ones((1, 128), ml_dtypes.bfloat16)
    c["ones1f"] = np.ones((1, 128), np.float32)
    c["onesc"] = np.ones((128, 1), ml_dtypes.bfloat16)
    c["onescf"] = np.ones((128, 1), np.float32)
    g0 = np.zeros((128, 2, 512), np.float32)
    for t in range(2):
        g0[:, t, 128 * t:128 * (t + 1)] = np.eye(128)
    c["g0c"] = g0.astype(ml_dtypes.bfloat16)
    gs = np.zeros((128, 20, 10), np.float32)
    for t in range(20):
        gs[:, t, t // 2] = 1.0
    c["gsel"] = gs
    for nm, entries in (("sel_m", [(1, 1.0 / 256)]), ("sel_v", [(2, 1.0 / 256), (0, 1.0)]),
                        ("sel_a", [(3, 1.0 / 256)])):
        sl = np.zeros((128, 20, 80), np.float32)
        for pth in range(5):
            for hh in range(4):
                for blk, val in entries:
                    sl[:, pth * 4 + hh, 20 * blk + pth * 4 + hh] = val
        c[nm] = sl.astype(ml_dtypes.bfloat16)
    cv = np.broadcast_to(np.asarray(CVALS, np.float32), (128, 8)).copy()
    c["cvals"] = cv
    c["cvals16"] = cv.astype(ml_dtypes.bfloat16)
    return c


def kernel(**inputs):
    hs = np.ascontiguousarray(np.asarray(inputs["L0_attn_hidden_states"], dtype=np.float32).reshape(16384, D))
    consts = make_consts()
    weights = {
        "wq": "L0_attn_Wq", "wk": "L0_attn_Wk", "wv": "L0_attn_Wv", "wb": "L0_attn_Wb",
        "conv_q": "L0_attn_conv_q", "conv_k": "L0_attn_conv_k", "conv_v": "L0_attn_conv_v",
        "fir_s": "L0_attn_fir_s", "fir_m": "L0_attn_fir_m", "fir_l": "L0_attn_fir_l",
        "gb1": "L0_attn_gb1", "gw2": "L0_attn_gW2", "gb2": "L0_attn_gb2",
        "log_tau": "L0_attn_log_tau", "eps_logit": "L0_attn_eps_logit",
        "norm_w": "L0_attn_norm_w", "wo": "L0_attn_Wo",
        "cn_w1": "L0_ffn_cn_w1", "cn_b1": "L0_ffn_cn_b1", "cn_w2": "L0_ffn_cn_w2",
        "cn_b2": "L0_ffn_cn_b2", "cn_w3": "L0_ffn_cn_w3", "cn_b3": "L0_ffn_cn_b3",
        "cn_w4": "L0_ffn_cn_w4", "cn_b4": "L0_ffn_cn_b4",
        "gc_w1": "L0_ffn_gc_w1", "gc_b1": "L0_ffn_gc_b1", "gc_w2": "L0_ffn_gc_w2",
        "gc_b2": "L0_ffn_gc_b2", "gc_w3": "L0_ffn_gc_w3", "gc_b3": "L0_ffn_gc_b3",
        "adaptive_range": "L0_ffn_adaptive_range",
    }
    shared = {k: np.ascontiguousarray(np.asarray(inputs[v], dtype=np.float32)) for k, v in weights.items()}
    for nm, nt in (("gb1", 16), ("norm_w", 2), ("cn_b1", 8), ("cn_b2", 4), ("cn_b3", 2),
                   ("gc_b1", 4), ("gc_b2", 2)):
        shared[nm] = np.ascontiguousarray(shared[nm].reshape(nt, 128).T)
    gW1 = np.ascontiguousarray(np.asarray(inputs["L0_attn_gW1"], dtype=np.float32))
    shared["gw1_hs"] = gW1[0:D]

    nc = build_nc()
    in_maps = []
    for c in range(NCORES):
        m = dict(shared)
        m.update(consts)
        lo = T * c - HALO
        if lo < 0:
            sl = np.concatenate([np.zeros((HALO, D), np.float32), hs[0:T * c + T]], axis=0)
        else:
            sl = hs[lo:T * (c + 1)]
        m["hsT"] = np.ascontiguousarray(sl.T)
        m["gw1_slice"] = np.ascontiguousarray(gW1[D + 2560 * c: D + 2560 * (c + 1)])
        pm = np.zeros((128, 8), np.float32)
        pm[:, 0:c] = 1.0
        m["pref_mask"] = pm
        m["pref_inv"] = 1.0 - pm
        in_maps.append(m)
    res = run_bass_kernel_spmd(nc, in_maps, core_ids=list(range(NCORES)))
    outs = [res.results[c]["out"] for c in range(NCORES)]
    full = np.concatenate(outs, axis=0).reshape(1, 16384, D)
    if DEBUG:
        kernel.debug_results = res.results
    return full


def phase2_delta(st):
    nc, p, din, dram = st["nc"], st["p"], st["din"], st["dram"]
    salloc, sfree, palloc = st["salloc"], st["sfree"], st["palloc"]
    q16, k16, v16, beta = st["q16"], st["k16"], st["v16"], st["beta_tm"]
    eye16, eyef, ones1f, mask3 = st["s_eye16"], st["s_eyef"], st["s_ones1f"], st["s_mask3"]

    Fst = salloc("Fst", [128, H * NCH, 128], BF16)
    o16 = st["o16"]
    ktm = salloc("ktm", [128, 256], BF16)
    vtm = salloc("vtm", [128, 256], BF16)
    nkvb = salloc("nkvb", [128, 512], BF16)
    kbf = salloc("kbf", [128, 256], BF16)
    brow = salloc("brow", [1, 128], F32)
    bbc = salloc("bbc", [128, 128], BF16)
    BAL = salloc("BAL", [128, 384], BF16)
    BALf = salloc("BALf", [128, 384], F32)
    PBf = salloc("PBf", [128, 256], F32)
    ngs = salloc("ngs", [128, 256], BF16)
    lcl = salloc("lcl", [128, 128], BF16)
    us = salloc("us", [128, 512], BF16)
    Gs = salloc("Gs", [128, 2, 512], BF16)
    goutf = salloc("goutf", [128, 2, 512], F32)
    s_pref = salloc("s_pref", [128, 16], F32)
    gpref = salloc("gpref", [128, 2, 512], F32)
    gj16 = salloc("gj16", [128, 2, 512], BF16)
    mjt = salloc("mjt", [128, 2, 256], BF16)
    Sinit = salloc("Sinit", [128, 2 * H, 256], BF16)
    Ss = salloc("Ss", [128, 2, 256], BF16)
    u16s = salloc("u16s", [128, 256], BF16)
    otm = salloc("otm", [128, 256], BF16)
    g0s = salloc("g0s", [128, 2, 512], BF16)

    pG = palloc("pG", [128, 2, 512], F32)
    pU = palloc("pU", [128, 512], F32)
    pM = palloc("pM", [128, 384], F32)
    pF = palloc("pF", [128, 128], F32)
    pNG = palloc("pNG", [128, 256], F32)
    pT = palloc("pT", [128, 256], BF16)
    pR = palloc("pR", [1, 128], F32)

    ch = st["l2_done"]

    first = [True]
    def sstep(eng, fn, extra=()):
        nonlocal ch
        w = [ch] + list(extra)
        if first[0]:
            w.append(st["gw1_done"])
            first[0] = False
        ch = p.on(eng, fn, waits=w, inc=eng if eng in ("pe", "dve", "act", "pool") else "sp",
                  amt=16 if eng in () else 1)
        return ch

    def sdma(eng, fn, extra=()):
        nonlocal ch
        ch = p.dma(eng, fn, waits=[ch] + list(extra), inc="ld3")
        return ch

    def ld_pref(eng):
        sem = p.sems["ld3"]
        eng.dma_start(out=s_pref[:, 0:8], in_=din["pref_mask"]).then_inc(sem, 16)
        eng.dma_start(out=s_pref[:, 8:16], in_=din["pref_inv"]).then_inc(sem, 16)
        eng.dma_start(out=g0s[:], in_=din["g0c"]).then_inc(sem, 16)
        return None
    p.on("sp", ld_pref, waits=[ch])
    pref_tok = p.bump("ld3", 48)

    beta_ready = st["beta_ready"]

    def prep_chunk(h, c, mode):
        cb = CH * c
        fidx = h * NCH + c
        def tr_k(eng):
            eng.transpose(pT[:, 0:128], k16[:, 2 * h, cb:cb + 128], eye16[:])
            return eng.transpose(pT[:, 128:256], k16[:, 2 * h + 1, cb:cb + 128], eye16[:])
        sstep("pe", tr_k)
        sstep("act", lambda eng: eng.copy(ktm[:], pT[:]))
        def tr_v(eng):
            eng.transpose(pT[:, 0:128], v16[:, 2 * h, VH + cb:VH + cb + 128], eye16[:])
            return eng.transpose(pT[:, 128:256], v16[:, 2 * h + 1, VH + cb:VH + cb + 128], eye16[:])
        sstep("pe", tr_v)
        sstep("act", lambda eng: eng.copy(vtm[:], pT[:]))
        def mk_nkvb(eng):
            eng.tensor_scalar(out=nkvb[:, 0:256], in0=ktm[:], scalar1=beta[:, c, h:h + 1],
                              scalar2=-1.0, op0=ALU.mult, op1=ALU.mult)
            return eng.tensor_scalar(out=nkvb[:, 256:512], in0=vtm[:], scalar1=beta[:, c, h:h + 1],
                                     scalar2=None, op0=ALU.mult)
        sstep("dve", mk_nkvb, extra=[beta_ready])
        if mode == "A":
            def brow_mm(eng):
                return eng.matmul(pR[:], beta[:, c, h:h + 1], eyef[:], start=True, stop=True)
            sstep("pe", brow_mm)
            sstep("act", lambda eng: eng.copy(brow[:], pR[:]))
            def bbc_mm(eng):
                return eng.matmul(pU[:, 0:128], ones1f[:], brow[:], start=True, stop=True)
            sstep("pe", bbc_mm)
            sstep("dve", lambda eng: eng.tensor_copy(bbc[:], pU[:, 0:128]))
            def mk_kbf(eng):
                eng.tensor_mul(kbf[:, 0:128], k16[:, 2 * h, cb:cb + 128], bbc[:])
                return eng.tensor_mul(kbf[:, 128:256], k16[:, 2 * h + 1, cb:cb + 128], bbc[:])
            sstep("dve", mk_kbf)
            def grams(eng):
                for t in range(2):
                    eng.matmul(pM[:, 0:128], k16[:, 2 * h + t, cb:cb + 128], kbf[:, 128 * t:128 * t + 128],
                               start=(t == 0), stop=(t == 1))
                for t in range(2):
                    eng.matmul(pM[:, 128:256], kbf[:, 128 * t:128 * t + 128], k16[:, 2 * h + t, cb:cb + 128],
                               start=(t == 0), stop=(t == 1))
                return eng.matmul(pM[:, 256:384], eye16[:], eye16[:], start=True, stop=True)
            def grams2(eng):
                ins = None
                for t in range(2):
                    ins = eng.matmul(pM[:, 0:128], k16[:, 2 * h + t, cb:cb + 128], kbf[:, 128 * t:128 * t + 128],
                                     start=(t == 0), stop=(t == 1))
                for t in range(2):
                    ins = eng.matmul(pM[:, 128:256], kbf[:, 128 * t:128 * t + 128], k16[:, 2 * h + t, cb:cb + 128],
                                     start=(t == 0), stop=(t == 1))
                return ins
            sstep("pe", grams2)
            def maskba(eng):
                return eng.tensor_mul(BALf[:, 0:256], pM[:, 0:256], mask3[:, 0:256])
            sstep("dve", maskba)
            def f0(eng):
                return eng.tensor_add(BALf[:, 256:384], BALf[:, 0:128], eyef[:])
            sstep("dve", f0)
            def finit(eng):
                return eng.matmul(pF[:], eyef[:], BALf[:, 256:384], start=True, stop=False)
            sstep("pe", finit)
            fsb_cur = BALf[:, 256:384]
            pb_cur = BALf[:, 0:256]
            for lvl in range(1, 7):
                def sq(eng, pb_cur=pb_cur):
                    eng.matmul(pM[:, 0:128], pb_cur[:, 128:256], pb_cur[:, 0:128], start=True, stop=True)
                    return eng.matmul(pM[:, 128:256], pb_cur[:, 0:128], pb_cur[:, 128:256], start=True, stop=True)
                sstep("pe", sq)
                def evpb(eng):
                    return eng.tensor_copy(PBf[:], pM[:, 0:256])
                sstep("dve", evpb)
                def fcomb(eng, fsb_cur=fsb_cur, last=(lvl == 6)):
                    return eng.matmul(pF[:], PBf[:, 128:256], fsb_cur, start=False, stop=last)
                sstep("pe", fcomb)
                if lvl == 6:
                    def evf(eng, fidx=fidx):
                        return eng.tensor_copy(Fst[:, fidx, :], pF[:])
                    sstep("dve", evf)
                else:
                    def evf(eng):
                        return eng.tensor_copy(BALf[:, 256:384], pF[:])
                    sstep("dve", evf)
                fsb_cur = BALf[:, 256:384]
                pb_cur = PBf
        def ngmm(eng):
            eng.matmul(pNG[:, 0:128], nkvb[:, 0:128], Fst[:, fidx, :], start=True, stop=True)
            return eng.matmul(pNG[:, 128:256], nkvb[:, 128:256], Fst[:, fidx, :], start=True, stop=True)
        sstep("pe", ngmm)
        sstep("act", lambda eng: eng.copy(ngs[:], pNG[:]))
        if mode == "B":
            def gram3(eng):
                ins = None
                for t in range(2):
                    ins = eng.matmul(pM[:, 256:384], k16[:, 2 * h + t, cb:cb + 128],
                                     q16[:, 2 * h + t, cb:cb + 128], start=(t == 0), stop=(t == 1))
                return ins
            sstep("pe", gram3)
            def masklcl(eng):
                return eng.tensor_mul(lcl[:], pM[:, 256:384], mask3[:, 256:384])
            sstep("dve", masklcl)

    for h in range(H):
        for c in range(NCH):
            prep_chunk(h, c, "A")
            cb = CH * c
            fidx = h * NCH + c
            if c == 0:
                def umm(eng, fidx=fidx):
                    return eng.matmul(pU[:], Fst[:, fidx, :], nkvb[:], start=True, stop=True)
            else:
                def umm(eng, fidx=fidx):
                    eng.matmul(pU[:], ngs[:, 0:128], Gs[:, 0, :], start=True, stop=False)
                    eng.matmul(pU[:], ngs[:, 128:256], Gs[:, 1, :], start=False, stop=False)
                    return eng.matmul(pU[:, 256:512], Fst[:, fidx, :], nkvb[:, 256:512],
                                      start=False, stop=True)
            sstep("pe", umm)
            sstep("dve", lambda eng: eng.tensor_copy(us[:], pU[:]))
            if c == 0:
                def gmm(eng):
                    ins = None
                    for t in range(2):
                        eng.matmul(pG[:, t, :], ktm[:, 128 * t:128 * t + 128], us[:], start=True, stop=False)
                        ins = eng.matmul(pG[:, t, :], eye16[:], g0s[:, t, :], start=False, stop=True)
                    return ins
            else:
                def gmm(eng):
                    ins = None
                    for t in range(2):
                        ins = eng.matmul(pG[:, t, :], ktm[:, 128 * t:128 * t + 128], us[:],
                                         start=False, stop=(c == NCH - 1))
                    return ins
            sstep("pe", gmm, extra=[pref_tok] if c == 0 else [])
            if c < NCH - 1:
                sstep("dve", lambda eng: eng.tensor_copy(Gs[:], pG[:]))
            else:
                sstep("dve", lambda eng: eng.tensor_copy(goutf[:], pG[:]))
                def st_g(eng, h=h):
                    return eng.dma_start(
                        out=dram["cc_g_in"].rearrange("(a p) n -> p a n", p=128)[:, 2 * h:2 * h + 2, :],
                        in_=goutf[:])
                sdma("pool", st_g)

    def cc_fn(eng):
        return eng.collective_compute(
            "AllGather", ALU.bypass, replica_groups=[list(range(8))],
            ins=[dram["cc_g_in"].opt()], outs=[dram["cc_g_out"].opt()])
    nonlocal_ch = None
    ch = p.on("pool", cc_fn, waits=[ch], inc="cc", amt=1)

    gv = dram["cc_g_out"].rearrange("(r a p) n -> p r a n", p=128, r=8)
    for j in range(7):
        for h in range(H):
            def ld_g(eng, j=j, h=h):
                return eng.dma_start(out=gpref[:], in_=gv[:, j, 2 * h:2 * h + 2, :])
            sdma("pool", ld_g)
            if j == 0:
                def mk_s0(eng):
                    return eng.tensor_scalar(out=Sinit[:, 2 * h:2 * h + 2, :] if False else Sinit[:, 2 * h:2 * h + 2, :],
                                             in0=gpref[:, :, 256:512], scalar1=s_pref[:, 0:1],
                                             scalar2=None, op0=ALU.mult)
                def mk_s0b(eng, h=h):
                    return eng.tensor_scalar(out=Sinit[:, 2 * h:2 * h + 2, :], in0=gpref[:, :, 256:512],
                                             scalar1=s_pref[:, 0:1], scalar2=None, op0=ALU.mult)
                sstep("dve", mk_s0b)
            else:
                def mk_gj(eng, j=j):
                    eng.tensor_scalar(out=gj16[:], in0=gpref[:], scalar1=s_pref[:, j:j + 1],
                                      scalar2=None, op0=ALU.mult)
                    ins = None
                    for t in range(2):
                        ins = eng.scalar_tensor_tensor(
                            out=gj16[:, t, 128 * t:128 * t + 128], in0=eyef[:],
                            scalar=s_pref[:, 8 + j:8 + j + 1], in1=gj16[:, t, 128 * t:128 * t + 128],
                            op0=ALU.mult, op1=ALU.add)
                    return ins
                sstep("dve", mk_gj)
                def tr_m(eng):
                    ins = None
                    for a2 in range(2):
                        for b2 in range(2):
                            ins = eng.transpose(pT[:, 128 * a2:128 * a2 + 128] if b2 == 0 else pT[:, 128 * a2:128 * a2 + 128],
                                                gj16[:, a2, 128 * b2:128 * b2 + 128], eye16[:])
                    return ins
                def tr_m2(eng):
                    ins = None
                    for b2 in range(2):
                        for a2 in range(2):
                            ins = eng.transpose(pT[:, 128 * a2:128 * a2 + 128],
                                                gj16[:, a2, 128 * b2:128 * b2 + 128], eye16[:])
                    return ins
                for b2 in range(2):
                    def tr_b(eng, b2=b2):
                        ins = None
                        for a2 in range(2):
                            ins = eng.transpose(pT[:, 128 * a2:128 * a2 + 128],
                                                gj16[:, a2, 128 * b2:128 * b2 + 128], eye16[:])
                        return ins
                    sstep("pe", tr_b)
                    def ev_b(eng, b2=b2):
                        return eng.tensor_copy(mjt[:, b2, :], pT[:])
                    sstep("dve", ev_b)
                def smm(eng, h=h):
                    ins = None
                    for a2 in range(2):
                        for b2 in range(2):
                            eng.matmul(pG[:, a2, 0:256], mjt[:, b2, 128 * a2:128 * a2 + 128],
                                       Sinit[:, 2 * h + b2, :], start=(b2 == 0), stop=False)
                        ins = eng.matmul(pG[:, a2, 0:256], eye16[:], gj16[:, a2, 256:512],
                                         start=False, stop=True)
                    return ins
                sstep("pe", smm)
                def ev_s(eng, h=h):
                    return eng.tensor_copy(Sinit[:, 2 * h:2 * h + 2, :], pG[:, :, 0:256])
                sstep("dve", ev_s)

    for h in range(H):
        for c in range(NCH):
            prep_chunk(h, c, "B")
            cb = CH * c
            fidx = h * NCH + c
            sread = Sinit[:, 2 * h:2 * h + 2, :] if c == 0 else Ss
            if c == 0:
                def sinitmm(eng, h=h):
                    ins = None
                    for t in range(2):
                        ins = eng.matmul(pG[:, t, 0:256], eye16[:], Sinit[:, 2 * h + t, :],
                                         start=True, stop=False)
                    return ins
                sstep("pe", sinitmm)
            def ummb(eng, fidx=fidx, sread=sread):
                eng.matmul(pU[:, 0:256], Fst[:, fidx, :], nkvb[:, 256:512], start=True, stop=False)
                eng.matmul(pU[:, 0:256], ngs[:, 0:128], sread[:, 0, :], start=False, stop=False)
                return eng.matmul(pU[:, 0:256], ngs[:, 128:256], sread[:, 1, :], start=False, stop=True)
            sstep("pe", ummb)
            sstep("dve", lambda eng: eng.tensor_copy(u16s[:], pU[:, 0:256]))
            def omm(eng, h=h, cb=cb, sread=sread):
                eng.matmul(pU[:, 256:512], q16[:, 2 * h, cb:cb + 128], sread[:, 0, :], start=True, stop=False)
                eng.matmul(pU[:, 256:512], q16[:, 2 * h + 1, cb:cb + 128], sread[:, 1, :], start=False, stop=False)
                return eng.matmul(pU[:, 256:512], lcl[:], u16s[:], start=False, stop=True)
            sstep("pe", omm)
            sstep("act", lambda eng: eng.copy(otm[:], pU[:, 256:512]))
            def smm2(eng):
                ins = None
                for t in range(2):
                    ins = eng.matmul(pG[:, t, 0:256], ktm[:, 128 * t:128 * t + 128], u16s[:],
                                     start=False, stop=False)
                return ins
            sstep("pe", smm2)
            sstep("dve", lambda eng: eng.tensor_copy(Ss[:], pG[:, :, 0:256]))
            def tr_o(eng):
                ins = None
                for t in range(2):
                    ins = eng.transpose(pT[:, 128 * t:128 * t + 128], otm[:, 128 * t:128 * t + 128], eye16[:])
                return ins
            sstep("pe", tr_o)
            def ev_o(eng, h=h, cb=cb):
                return eng.tensor_copy(o16[:, 2 * h:2 * h + 2, cb:cb + 128],
                                       pT[:].rearrange("p (a b) -> p a b", a=2))
            sstep("dve", ev_o)

    st["delta_done"] = ch
    emit_debug(st, dict(o16=o16, sinit=Sinit, fst=Fst))
    if "gout" in st["dbg"]:
        def store_g(eng):
            return eng.dma_start(out=st["dbg"]["gout"], in_=st["dram"]["cc_g_out"])
        p.dma("pool", store_g, waits=[ch], inc="st")
    for nm in ("pR", "pT", "pNG", "pF", "pM", "pU", "pG"):
        sfree(nm)
    for nm in ("g0s", "otm", "u16s", "Ss", "Sinit", "mjt", "gj16", "gpref", "s_pref",
               "goutf", "Gs", "us", "lcl", "ngs", "PBf", "BALf", "BAL", "bbc", "brow", "kbf",
               "nkvb", "vtm", "ktm", "Fst"):
        sfree(nm)
    sfree("hs16") if False else None
    sfree("beta_tm", "k16", "q16")


def phase15_gw1(st):
    nc, p, din, dram = st["nc"], st["p"], st["din"], st["dram"]
    salloc, sfree, palloc = st["salloc"], st["sfree"], st["palloc"]
    gwb = salloc("gwb", [128, 2, GH], F32)
    gsl = salloc("gsl", [128, 20, 10], F32)
    gred = salloc("gred", [10, GH], F32)
    pRD = palloc("pRD", [10, 4, 512], F32)
    ch = None
    def sstep(eng, fn, extra=()):
        nonlocal ch
        ch = p.on(eng, fn, waits=([ch] if ch else []) + list(extra), inc=eng)
        return ch
    def sdma(eng, fn):
        nonlocal ch
        ch = p.dma(eng, fn, waits=[ch] if ch else [], inc="pld" if eng == "pool" else "ld3")
        return ch
    sdma("sp", lambda e: e.dma_start(out=gsl[:], in_=din["gsel"]))
    gv = din["gw1_slice"].rearrange("(a p) n -> p a n", p=128)
    for kt in range(20):
        bi = kt % 2
        def ld(e, kt=kt, bi=bi):
            return e.dma_start(out=gwb[:, bi, :], in_=gv[:, kt, :])
        sdma("sp", ld)
        def mm(e, kt=kt, bi=bi):
            ins = None
            for nb in range(4):
                ins = e.matmul(pRD[:, nb, :], gsl[:, kt, :], gwb[:, bi, 512 * nb:512 * nb + 512],
                               start=(kt == 0), stop=(kt == 19))
            return ins
        sstep("pe", mm)
    sstep("dve", lambda e: e.tensor_copy(gred[:], pRD[:].rearrange("p a b -> p (a b)")))
    if "gredump" in st["dbg"]:
        def dgr(e):
            return e.dma_start(out=st["dbg"]["gredump"], in_=gred[:])
        ch = p.dma("pool", dgr, waits=[ch], inc="st")
    sdma("pool", lambda e: e.dma_start(out=dram["cc_gw_in"], in_=gred[:]))
    def cc_fn(e):
        return e.collective_compute(
            "AllGather", ALU.bypass, replica_groups=[list(range(8))],
            ins=[dram["cc_gw_in"].opt()], outs=[dram["cc_gw_out"].opt()])
    ch = p.on("pool", cc_fn, waits=[ch], inc="cc", amt=1)
    st["gw1_done"] = ch
    sfree("pRD")
    sfree("gred", "gsl", "gwb")


def phase3_gate(st):
    nc, p, din, dram = st["nc"], st["p"], st["din"], st["dram"]
    salloc, sfree, palloc = st["salloc"], st["sfree"], st["palloc"]
    v16, o16 = st["v16"], st["o16"]
    eye16, eyef, ones1f, onescf = st["s_eye16"], st["s_eyef"], st["s_ones1f"], st["s_onescf"]

    rows = salloc("rows", [1, 128], F32)
    tmp512 = salloc("tmp512", [128, 512], F32)
    rowbc = salloc("rowbc", [128, 2, 512], F32)
    pacc = salloc("pacc", [128, KT], F32)
    firs = salloc("firs", [128, KT, 3 * 512], BF16)
    stats80 = salloc("stats80", [128, 512], F32)
    fw = salloc("fw", [128, KT, 82], F32)
    sel = salloc("sel", [128, 20, 80], BF16)
    s_sel20 = salloc("s_sel20", [20, 20, 128], F32)
    hsst = salloc("hsst", [128, KT, 512], F32)
    hs16st = salloc("hs16st", [128, KT, 512], BF16)
    sgw = salloc("sgw", [128, 9, 128], BF16)
    wo16 = salloc("wo16", [128, KT, D], BF16)
    hgate = salloc("hgate", [128, 16, 512], BF16)
    big512 = salloc("big512", [128, KT, 512], F32)
    sgwf = big512[:].rearrange("p a n -> p (a n)")[:, 0:1152].rearrange("p (a n) -> p a n", n=128)
    onrm = salloc("onrm", [128, KT, 512], BF16)
    gw2s = salloc("gw2s", [128, 16, 20], BF16)
    gb1s = salloc("gb1s", [128, 16], F32)
    normw = salloc("normw", [128, 2], F32)
    ltm = salloc("ltm", [128, 4, 20], F32)
    mx5 = salloc("mx5", [128, 4, 8], F32)
    prow = salloc("prow", [20, 512], F32)
    pb16 = salloc("pb16", [128, 512], F32)
    epsbc = salloc("epsbc", [128, 40], F32)
    sq512 = salloc("sq512", [128, 512], BF16)
    stats16 = salloc("stats16", [128, 512], BF16)
    gb2_16 = salloc("gb2_16", [1, 20], BF16)

    pA = palloc("pA", [128, 512], F32)
    pB = palloc("pB", [128, 512], F32)
    pC = palloc("pC", [128, 512], F32)
    pD = palloc("pD", [128, 64], F32)
    pE1 = palloc("pE1", [1, 512], F32)
    pE2 = palloc("pE2", [1, 512], F32)
    st["p3"] = dict(pA=pA, pB=pB, pC=pC, pD=pD, pE1=pE1, pE2=pE2, rowbc=rowbc,
                    tmp512=tmp512, big512=big512, sq512=sq512, pacc=pacc, rows=rows,
                    hsst=hsst, hs16st=hs16st)

    ch = st["delta_done"]
    def sstep(eng, fn, extra=()):
        nonlocal ch
        ch = p.on(eng, fn, waits=[ch] + list(extra), inc=eng)
        return ch
    def sdma(eng, fn):
        nonlocal ch
        ch = p.dma(eng, fn, waits=[ch], inc="ld3")
        return ch

    def ld1(e):
        sem = p.sems["ld3"]
        e.dma_start(out=fw[:, :, 0:3], in_=din["fir_s"].rearrange("(a p) j -> p a j", p=128)).then_inc(sem, 16)
        e.dma_start(out=fw[:, :, 3:18], in_=din["fir_m"].rearrange("(a p) j -> p a j", p=128)).then_inc(sem, 16)
        e.dma_start(out=fw[:, :, 18:82], in_=din["fir_l"].rearrange("(a p) j -> p a j", p=128)).then_inc(sem, 16)
        e.dma_start(out=s_sel20[:], in_=din["sel20"]).then_inc(sem, 16)
        e.dma_start(out=gb1s[:], in_=din["gb1"]).then_inc(sem, 16)
        e.dma_start(out=normw[:], in_=din["norm_w"]).then_inc(sem, 16)
        e.dma_start(out=rows[:, 0:4], in_=din["log_tau"].rearrange("(o h) -> o h", o=1)).then_inc(sem, 16)
        e.dma_start(out=rows[:, 8:28], in_=din["eps_logit"].rearrange("(o h) p -> o (h p)", o=1)).then_inc(sem, 16)
        e.dma_start(out=rows[:, 72:92], in_=din["gb2"].rearrange("(o j) -> o j", o=1)).then_inc(sem, 16)
        return None
    p.on("sp", ld1, waits=[ch])
    ch = p.bump("ld3", 16 * 9)
    for m in range(KT):
        def ldw(e, m=m):
            return e.dma_start(out=big512[:, :, 0:128],
                               in_=din["wo"].rearrange("(a p) n -> p a n", p=128)[:, :, 128 * m:128 * (m + 1)])
        sdma("sp", ldw)
        def cst(e, m=m):
            return e.tensor_copy(wo16[:, :, 128 * m:128 * (m + 1)], big512[:, :, 0:128])
        sstep("dve", cst)
    gw2stage = big512[:].rearrange("p a n -> p (a n)")[:, 0:320].rearrange("p (a j) -> p a j", j=20)
    sdma("sp", lambda e: e.dma_start(out=gw2stage, in_=din["gw2"].rearrange("(a p) j -> p a j", p=128)))
    sstep("dve", lambda e: e.tensor_copy(gw2s[:], gw2stage))

    sstep("dve", lambda e: e.tensor_scalar(out=rows[:, 4:8], in0=rows[:, 0:4], scalar1=-1.0,
                                           scalar2=None, op0=ALU.mult))
    sstep("dve", lambda e: e.tensor_tensor(out=rows[:, 4:8], in0=rows[:, 0:4], in1=rows[:, 4:8], op=ALU.max))
    sstep("dve", lambda e: e.tensor_single_scalar(out=rows[:, 0:4], in_=rows[:, 0:4], scalar=0.0, op=ALU.max))
    sstep("act", lambda e: e.activation(out=rows[:, 4:8], in_=rows[:, 4:8], func=AF.Exp, scale=-1.0))
    sstep("act", lambda e: e.activation(out=rows[:, 4:8], in_=rows[:, 4:8], func=AF.Ln, bias=1.0))
    sstep("dve", lambda e: e.tensor_tensor(out=rows[:, 0:4], in0=rows[:, 0:4], in1=rows[:, 4:8], op=ALU.add))
    sstep("dve", lambda e: e.tensor_scalar(out=rows[:, 0:4], in0=rows[:, 0:4], scalar1=0.5,
                                           scalar2=None, op0=ALU.add))
    sstep("dve", lambda e: e.reciprocal(rows[:, 0:4], rows[:, 0:4]))
    sstep("act", lambda e: e.activation(out=rows[:, 8:28], in_=rows[:, 8:28], func=AF.Sigmoid))
    sstep("dve", lambda e: e.tensor_scalar(out=rows[:, 8:28], in0=rows[:, 8:28], scalar1=0.2,
                                           scalar2=None, op0=ALU.mult))
    sstep("dve", lambda e: e.reduce_sum(rows[:, 28:32],
                                        rows[:, 8:28].rearrange("p (h q) -> p h q", h=4), axis=AXL.X))
    sstep("dve", lambda e: e.tensor_scalar(out=rows[:, 28:32], in0=rows[:, 28:32], scalar1=-1.0,
                                           scalar2=1.0, op0=ALU.mult, op1=ALU.add))
    def eps_fin(e):
        ins = None
        for q in range(5):
            ins = e.tensor_copy(rows[:, 32:52].rearrange("p (h q) -> p h q", q=5)[:, :, q], rows[:, 28:32])
        return ins
    sstep("dve", eps_fin)
    def itau_exp(e):
        ins = None
        for q in range(5):
            ins = e.tensor_copy(rows[:, 52:72].rearrange("p (h q) -> p h q", q=5)[:, :, q], rows[:, 0:4])
        return ins
    sstep("dve", itau_exp)
    sstep("pe", lambda e: e.matmul(pD[:, 0:20], ones1f[:], rows[:, 52:72], start=True, stop=True))
    sstep("dve", lambda e: e.tensor_copy(epsbc[:, 0:20], pD[:, 0:20]))
    def gw2_scale(e):
        ins = None
        for kt in range(16):
            ins = e.tensor_tensor(out=gw2s[:, kt, :], in0=gw2s[:, kt, :], in1=epsbc[:, 0:20], op=ALU.mult)
        return ins
    sstep("dve", gw2_scale)
    sstep("dve", lambda e: e.tensor_tensor(out=rows[:, 72:92], in0=rows[:, 72:92], in1=rows[:, 52:72], op=ALU.mult))
    sstep("dve", lambda e: e.tensor_copy(gb2_16[:], rows[:, 72:92]))
    sstep("pe", lambda e: e.matmul(pD[:, 0:20], ones1f[:], rows[:, 32:52], start=True, stop=True))
    sstep("dve", lambda e: e.tensor_copy(epsbc[:, 0:20], pD[:, 0:20]))
    sstep("pe", lambda e: e.matmul(pD[:, 0:20], ones1f[:], rows[:, 8:28], start=True, stop=True))
    sstep("dve", lambda e: e.tensor_copy(epsbc[:, 20:40], pD[:, 0:20]))
    sstep("dve", lambda e: e.memset(stats80[:], 0.0))
    sstep("dve", lambda e: e.memset(stats16[:], 0.0))
    if "gwout" in st["dbg"]:
        def dgw(e):
            return e.dma_start(out=st["dbg"]["gwout"], in_=dram["cc_gw_out"])
        ch = p.dma("pool", dgw, waits=[ch], inc="st")
    if "rowsdump" in st["dbg"]:
        def drows(e):
            return e.dma_start(out=st["dbg"]["rowsdump"], in_=rows[:])
        ch = p.dma("pool", drows, waits=[ch], inc="st")
    if "epsbcdump" in st["dbg"]:
        def depsbc(e):
            return e.dma_start(out=st["dbg"]["epsbcdump"], in_=epsbc[:])
        ch = p.dma("pool", depsbc, waits=[ch], inc="st")
    if "gw2dump" in st["dbg"]:
        def dgw2(e):
            return e.dma_start(out=st["dbg"]["gw2dump"], in_=gw2s[:])
        ch = p.dma("pool", dgw2, waits=[ch], inc="st")

    gveff = dram["cc_gw_out"].rearrange("(pp hh ss) n -> ss (pp hh) n", pp=5, hh=4)
    SBLK = [1, 2, 3, 0]
    FIRK = [(0, 3, 0), (1, 15, 3), (2, 64, 18)]
    def path_tile(pth, gg, t0):
        if pth < 3:
            return firs[:, gg, 512 * pth:512 * pth + 512]
        if pth == 3:
            return o16[:, gg, t0:t0 + ST]
        return v16[:, gg, 63 + t0:63 + t0 + ST]

    for s in range(NST):
        t0 = ST * s
        sdma("sp", lambda e, t0=t0: e.dma_start(
            out=hsst[:], in_=din["hsT"].rearrange("(a p) t -> p a t", p=128)[:, :, HALO + t0:HALO + t0 + ST]))
        sstep("dve", lambda e: e.tensor_copy(hs16st[:], hsst[:]))
        for (pth, KW, off) in FIRK:
            for gg in range(KT):
                def fir(e, pth=pth, KW=KW, off=off, gg=gg, t0=t0):
                    base = 63 + t0 - (KW - 1)
                    e.tensor_scalar(out=tmp512[:], in0=v16[:, gg, base:base + ST],
                                    scalar1=fw[:, gg, off:off + 1], scalar2=None, op0=ALU.mult)
                    for j in range(1, KW):
                        e.scalar_tensor_tensor(
                            out=tmp512[:], in0=v16[:, gg, base + j:base + j + ST],
                            scalar=fw[:, gg, off + j:off + j + 1], in1=tmp512[:],
                            op0=ALU.mult, op1=ALU.add)
                    return e.tensor_copy(firs[:, gg, 512 * pth:512 * pth + 512], tmp512[:])
                sstep("dve", fir)
        passes = [("sel_m", pB, None), ("sel_v", pA, "sq"), ("sel_a", pC, "abs")]
        for (selname, ppsum, xform) in passes:
            sdma("sp", lambda e, selname=selname: e.dma_start(out=sel[:], in_=din[selname]))
            for pth in range(P5):
                for gg in range(KT):
                    kt = pth * 8 + gg
                    if xform == "sq":
                        def xf(e, pth=pth, gg=gg, t0=t0):
                            tl = path_tile(pth, gg, t0)
                            return e.tensor_mul(sq512[:], tl, tl)
                        sstep("dve", xf)
                        rhs_t = sq512
                    elif xform == "abs":
                        def xf(e, pth=pth, gg=gg, t0=t0):
                            tl = path_tile(pth, gg, t0)
                            e.tensor_scalar(out=sq512[:], in0=tl, scalar1=-1.0, scalar2=None, op0=ALU.mult)
                            return e.tensor_tensor(out=sq512[:], in0=sq512[:], in1=tl, op=ALU.max)
                        sstep("dve", xf)
                        rhs_t = sq512
                    else:
                        rhs_t = None
                    def mmp(e, pth=pth, gg=gg, kt=kt, t0=t0, ppsum=ppsum, rhs_t=rhs_t):
                        rhs = rhs_t[:] if rhs_t is not None else path_tile(pth, gg, t0)
                        return e.matmul(ppsum[0:80, :], sel[:, pth * 4 + gg // 2, :], rhs,
                                        start=(kt == 0), stop=(kt == 39))
                    sstep("pe", mmp)
        sstep("dve", lambda e: e.tensor_copy(stats80[0:80, :], pA[0:80, :]))
        sstep("dve", lambda e: e.tensor_tensor(out=stats80[0:80, :], in0=stats80[0:80, :],
                                               in1=pB[0:80, :], op=ALU.add))
        sstep("dve", lambda e: e.tensor_tensor(out=stats80[0:80, :], in0=stats80[0:80, :],
                                               in1=pC[0:80, :], op=ALU.add))
        sstep("act", lambda e: e.activation(out=stats80[0:20, :], in_=stats80[0:20, :], func=AF.Sqrt))
        sstep("dve", lambda e: e.tensor_copy(stats16[0:80, :], stats80[0:80, :]))
        if "statsdump" in st["dbg"]:
            def dstat(e, s=s):
                return e.dma_start(out=st["dbg"]["statsdump"][s], in_=stats80[:])
            ch = p.dma("pool", dstat, waits=[ch], inc="st")
        for m in range(16):
            def ldm2(e, m=m):
                sem = p.sems["ld3"]
                e.dma_start(out=sgwf[:, 0:8, :],
                            in_=din["gw1_hs"].rearrange("(a p) n -> p a n", p=128)[:, :, 128 * m:128 * (m + 1)]).then_inc(sem, 16)
                for sv in range(4):
                    blk = SBLK[sv]
                    e.dma_start(out=sgwf[20 * blk:20 * blk + 20, 8, :],
                                in_=gveff[sv, :, 128 * m:128 * (m + 1)]).then_inc(sem, 16)
                return None
            p.on("sp", ldm2, waits=[ch])
            ch = p.bump("ld3", 16 * 5)
            sstep("dve", lambda e: e.tensor_copy(sgw[:], sgwf[:]))
            def gmm(e, m=m, t0=t0):
                for kt in range(KT):
                    e.matmul(pA[:], sgw[:, kt, :], hs16st[:, kt, :], start=(kt == 0), stop=False)
                return e.matmul(pA[:], sgw[:, 8, :], stats16[:], start=False, stop=True)
            sstep("pe", gmm)
            sstep("act", lambda e, m=m: e.activation(out=hgate[:, m, :], in_=pA[:], func=AF.Gelu,
                                                     bias=gb1s[:, m:m + 1]))
        for tt in range(4):
            def lmm(e, tt=tt):
                for kt in range(16):
                    e.matmul(pD[:, 0:20], hgate[:, kt, 128 * tt:128 * tt + 128], gw2s[:, kt, :],
                             start=(kt == 0), stop=False)
                return e.matmul(pD[:, 0:20], st["s_ones1"][:], gb2_16[:], start=False, stop=True)
            sstep("pe", lmm)
            sstep("dve", lambda e, tt=tt: e.tensor_copy(ltm[:, tt, :], pD[:, 0:20]))
        if "logitsdump" in st["dbg"]:
            def dlog(e, s=s):
                return e.dma_start(out=st["dbg"]["logitsdump"][s], in_=ltm[:])
            ch = p.dma("pool", dlog, waits=[ch], inc="st")
        if True:
            pass
        mxv = mx5[:].rearrange("p a b -> p (a b)")
        def smax1(e):
            return e.reduce_max(mxv[:, 0:16].rearrange("p (a h) -> p a h", a=4),
                                ltm[:].rearrange("p a (h q) -> p a h q", q=5), axis=AXL.X)
        sstep("dve", smax1)
        def smax2(e):
            ins = None
            lv = ltm[:].rearrange("p a (h q) -> p a h q", q=5)
            mv = mxv[:, 0:16].rearrange("p (a h) -> p a h", a=4)
            for q in range(5):
                ins = e.tensor_tensor(out=lv[:, :, :, q], in0=lv[:, :, :, q], in1=mv, op=ALU.subtract)
            return ins
        sstep("dve", smax2)
        sstep("act", lambda e: e.activation(out=ltm[:], in_=ltm[:], func=AF.Exp))
        def ssum(e):
            return e.reduce_sum(mxv[:, 16:32].rearrange("p (a h) -> p a h", a=4),
                                ltm[:].rearrange("p a (h q) -> p a h q", q=5), axis=AXL.X)
        sstep("dve", ssum)
        sstep("dve", lambda e: e.reciprocal(mxv[:, 16:32], mxv[:, 16:32]))
        def pnorm(e):
            ins = None
            lv = ltm[:].rearrange("p a (h q) -> p a h q", q=5)
            mv = mxv[:, 16:32].rearrange("p (a h) -> p a h", a=4)
            for q in range(5):
                ins = e.tensor_tensor(out=lv[:, :, :, q], in0=lv[:, :, :, q], in1=mv, op=ALU.mult)
            return ins
        sstep("dve", pnorm)
        def peps1(e):
            ins = None
            for tt in range(4):
                ins = e.tensor_tensor(out=ltm[:, tt, :], in0=ltm[:, tt, :], in1=epsbc[:, 0:20], op=ALU.mult)
            return ins
        sstep("dve", peps1)
        def peps2(e):
            ins = None
            for tt in range(4):
                ins = e.tensor_tensor(out=ltm[:, tt, :], in0=ltm[:, tt, :], in1=epsbc[:, 20:40], op=ALU.add)
            return ins
        sstep("dve", peps2)
        if "probs" in st["dbg"]:
            def dprobs(e, s=s):
                return e.dma_start(out=st["dbg"]["probs"].rearrange("a p b c -> a p b c")[s], in_=ltm[:])
            ch = p.dma("pool", dprobs, waits=[ch], inc="st")
        sstep("dve", lambda e: e.tensor_copy(tmp512[:, 0:80], ltm[:].rearrange("p a j -> p (a j)")))
        for tt in range(4):
            def trp(e, tt=tt):
                return e.transpose(pC[0:20, 128 * tt:128 * tt + 128], tmp512[:, 20 * tt:20 * tt + 20], eyef[:])
            sstep("pe", trp)
        sstep("dve", lambda e: e.tensor_copy(prow[:], pC[0:20, :]))
        for h in range(H):
            for pth in range(P5):
                j = h * P5 + pth
                def bcp(e, j=j):
                    return e.matmul(pB[:], s_sel20[:, j, :], prow[:], start=True, stop=True)
                sstep("pe", bcp)
                sstep("dve", lambda e: e.tensor_copy(pb16[:], pB[:]))
                for t2 in range(2):
                    gg = 2 * h + t2
                    def comb(e, pth=pth, gg=gg, t0=t0, first=(pth == 0)):
                        tl = path_tile(pth, gg, t0)
                        if first:
                            return e.tensor_tensor(out=big512[:, gg, :], in0=tl, in1=pb16[:], op=ALU.mult)
                        e.tensor_tensor(out=tmp512[:], in0=tl, in1=pb16[:], op=ALU.mult)
                        return e.tensor_tensor(out=big512[:, gg, :], in0=big512[:, gg, :],
                                               in1=tmp512[:], op=ALU.add)
                    sstep("dve", comb)
        if "ocomb" in st["dbg"]:
            def docomb(e, s=s):
                return e.dma_start(out=st["dbg"]["ocomb"][s], in_=big512[:])
            ch = p.dma("pool", docomb, waits=[ch], inc="st")
        for h in range(H):
            for t2 in range(2):
                gg = 2 * h + t2
                sstep("dve", lambda e, gg=gg: e.tensor_mul(tmp512[:], big512[:, gg, :], big512[:, gg, :]))
                sstep("pe", lambda e, t2=t2: e.matmul(pE1[:], onescf[:], tmp512[:],
                                                      start=(t2 == 0), stop=(t2 == 1)))
            sstep("act", lambda e: e.activation(out=tmp512[0:1, :], in_=pE1[:], func=AF.Sqrt,
                                                scale=1.0 / 256.0, bias=1e-5))
            sstep("dve", lambda e: e.reciprocal(tmp512[0:1, :], tmp512[0:1, :]))
            sstep("pe", lambda e: e.matmul(pB[:], ones1f[:], tmp512[0:1, :], start=True, stop=True))
            sstep("dve", lambda e: e.tensor_copy(rowbc[:, 0, :], pB[:]))
            for t2 in range(2):
                gg = 2 * h + t2
                def appl(e, gg=gg, t2=t2):
                    return e.scalar_tensor_tensor(out=onrm[:, gg, :], in0=big512[:, gg, :],
                                                  scalar=normw[:, t2:t2 + 1], in1=rowbc[:, 0, :],
                                                  op0=ALU.mult, op1=ALU.mult)
                sstep("dve", appl)
        for m in range(KT):
            def womm(e, m=m):
                ins = None
                for kt in range(KT):
                    ins = e.matmul(pA[:], wo16[:, kt, 128 * m:128 * (m + 1)], onrm[:, kt, :],
                                   start=(kt == 0), stop=(kt == KT - 1))
                return ins
            sstep("pe", womm)
            sstep("dve", lambda e, m=m: e.tensor_tensor(out=big512[:, m, :], in0=pA[:],
                                                        in1=hsst[:, m, :], op=ALU.add))
        ln_emit(st, sstep, big512)
        ch = st["_ln_ch"]
        def macc(e, s=s):
            ins = None
            dst = pacc if s == 0 else tmp512[:, 0:KT].rearrange("p (a o) -> p a o", o=1)
            for gg in range(KT):
                if s == 0:
                    ins = e.reduce_sum(pacc[:, gg:gg + 1], big512[:, gg, :], axis=AXL.X)
                else:
                    ins = e.reduce_sum(tmp512[:, gg:gg + 1], big512[:, gg, :], axis=AXL.X)
            return ins
        sstep("dve", macc)
        if s > 0:
            sstep("dve", lambda e: e.tensor_tensor(out=pacc[:], in0=pacc[:], in1=tmp512[:, 0:KT], op=ALU.add))
        sdma("pool", lambda e, t0=t0: e.dma_start(
            out=dram["x1_dram"].rearrange("(a p) t -> p a t", p=128)[:, :, t0:t0 + ST], in_=big512[:]))

    emit_debug(st, dict(stats80=stats80, hgate=hgate))
    if "x1dump" in st["dbg"]:
        def stx1(eng):
            return eng.dma_start(out=st["dbg"]["x1dump"], in_=dram["x1_dram"])
        ch = p.dma("pool", stx1, waits=[ch], inc="st")
    st["chain3"] = ch
    for nm in ("pE2", "pE1", "pD", "pC", "pB", "pA"):
        sfree(nm)
    for nm in ("gb2_16", "stats16", "sq512", "epsbc", "pb16", "prow", "mx5", "ltm", "normw", "gb1s",
               "gw2s", "onrm", "big512", "hgate", "wo16", "sgw", "hs16st", "hsst",
               "s_sel20", "sel", "fw", "stats80", "firs"):
        sfree(nm)


def ln_emit(st, sstep, xbuf):
    p = st["p"]
    d = st["p3"]
    pE1, pE2, tmp512, rowbc = d["pE1"], d["pE2"], d["tmp512"], d["rowbc"]
    onescf, ones1f = st["s_onescf"], st["s_ones1f"]
    for gg in range(KT):
        sstep("pe", lambda e, gg=gg: e.matmul(pE1[:], onescf[:], xbuf[:, gg, :],
                                              start=(gg == 0), stop=(gg == KT - 1)))
    for gg in range(KT):
        sstep("dve", lambda e, gg=gg: e.tensor_mul(tmp512[:], xbuf[:, gg, :], xbuf[:, gg, :]))
        sstep("pe", lambda e, gg=gg: e.matmul(pE2[:], onescf[:], tmp512[:],
                                              start=(gg == 0), stop=(gg == KT - 1)))
    sstep("dve", lambda e: e.tensor_scalar(out=tmp512[0:1, :], in0=pE1[:], scalar1=1.0 / D,
                                           scalar2=None, op0=ALU.mult))
    sstep("dve", lambda e: e.tensor_mul(rowbc[0:1, 1, :], tmp512[0:1, :], tmp512[0:1, :]))
    sstep("dve", lambda e: e.tensor_scalar(out=rowbc[0:1, 1, :], in0=rowbc[0:1, 1, :], scalar1=-1.0,
                                           scalar2=None, op0=ALU.mult))
    sstep("dve", lambda e: e.scalar_tensor_tensor(out=rowbc[0:1, 1, :], in0=pE2[:], scalar=1.0 / D,
                                                  in1=rowbc[0:1, 1, :], op0=ALU.mult, op1=ALU.add))
    sstep("act", lambda e: e.activation(out=rowbc[0:1, 1, :], in_=rowbc[0:1, 1, :], func=AF.Sqrt, bias=1e-5))
    sstep("dve", lambda e: e.reciprocal(rowbc[0:1, 1, :], rowbc[0:1, 1, :]))
    sstep("pe", lambda e: e.matmul(d["pB"][:], ones1f[:], tmp512[0:1, :], start=True, stop=True))
    sstep("dve", lambda e: e.tensor_copy(rowbc[:, 0, :], d["pB"][:]))
    sstep("pe", lambda e: e.matmul(d["pB"][:], ones1f[:], rowbc[0:1, 1, :], start=True, stop=True))
    sstep("dve", lambda e: e.tensor_copy(rowbc[:, 1, :], d["pB"][:]))
    for gg in range(KT):
        def appl(e, gg=gg):
            e.tensor_tensor(out=xbuf[:, gg, :], in0=xbuf[:, gg, :], in1=rowbc[:, 0, :], op=ALU.subtract)
            return e.tensor_tensor(out=xbuf[:, gg, :], in0=xbuf[:, gg, :], in1=rowbc[:, 1, :], op=ALU.mult)
        tok = sstep("dve", appl)
    st["_ln_ch"] = tok


def broadcast_pair(a, b):
    import concourse.bass as _b
    return _b.broadcast_tensor_aps(a, b)


def phase4_ffn(st):
    nc, p, din, dram, out = st["nc"], st["p"], st["din"], st["dram"], st["out"]
    salloc, sfree, palloc = st["salloc"], st["sfree"], st["palloc"]
    eyef, ones1f, onescf = st["s_eyef"], st["s_ones1f"], st["s_onescf"]
    d3 = st["p3"]
    rows, pacc, tmp512, rowbc, hsst = d3["rows"], d3["pacc"], d3["tmp512"], d3["rowbc"], d3["hsst"]

    ch = st["chain3"]
    def sstep(eng, fn, extra=()):
        nonlocal ch
        ch = p.on(eng, fn, waits=[ch] + list(extra), inc=eng)
        return ch
    def sdma(eng, fn):
        nonlocal ch
        ch = p.dma(eng, fn, waits=[ch], inc="pld" if eng == "pool" else "ld3")
        return ch

    w1s = salloc("w1s", [128, KT, D], BF16)
    w2s = salloc("w2s", [128, KT, 512], BF16)
    w3s = salloc("w3s", [128, 4, 256], BF16)
    w4s = salloc("w4s", [128, 2, 1], BF16)
    g1s = salloc("g1s", [128, KT, 512], BF16)
    g2s = salloc("g2s", [128, 4, 256], BF16)
    g3s = salloc("g3s", [128, 2, 1], BF16)
    bias = salloc("bias", [128, 16], F32)
    xs = salloc("xs", [128, KT, 512], F32)
    xs2 = salloc("xs2", [128, KT, 512], F32)
    xs16 = salloc("xs16", [128, KT, 512], BF16)
    t1 = salloc("t1", [128, KT, 512], BF16)
    t2b = salloc("t2b", [128, 4, 512], BF16)
    t3b = salloc("t3b", [128, 2, 512], BF16)
    gcx = salloc("gcx", [128, KT], BF16)
    gch = salloc("gch", [128, 4], BF16)
    obuf = salloc("obuf", [128, D], F32)

    pA = palloc("pA4", [128, 512], F32)
    pB = palloc("pB4", [128, 512], F32)
    pE1 = palloc("pE14", [1, 512], F32)
    pE2 = palloc("pE24", [1, 512], F32)
    pD = palloc("pD4", [128, 16], F32)
    st["p3"]["pA"] = pA
    st["p3"]["pB"] = pB
    st["p3"]["pE1"] = pE1
    st["p3"]["pE2"] = pE2

    def ldw(e):
        sem = p.sems["ld3"]
        e.dma_start(out=xs[:, :, 0:512].rearrange("p a n -> p (a n)")[:, 0:4096].rearrange("p (a n) -> p a n", a=KT),
                    in_=din["cn_w1"].rearrange("(a p) n -> p a n", p=128)[:, :, 0:512]).then_inc(sem, 16)
        return None
    def ld_piece(dst, src_name, cols, stage_cols):
        def f(e, dst=dst, cols=cols):
            sem = p.sems["ld3"]
            e.dma_start(out=xs[:, :, 0:cols[1] - cols[0]] if False else xs.ap() if False else xs[:, :, :],
                        in_=din[src_name]).then_inc(sem, 16)
            return None
        return f
    def l1(e):
        sem = p.sems["ld3"]
        e.dma_start(out=xs[:], in_=din["cn_w1"].rearrange("(a p) n -> p a n", p=128)[:, :, 0:512]).then_inc(sem, 16)
        return None
    p.on("sp", l1, waits=[ch]); ch = p.bump("ld3", 16)
    sstep("dve", lambda e: e.tensor_copy(w1s[:, :, 0:512], xs[:]))
    def l2(e):
        sem = p.sems["ld3"]
        e.dma_start(out=xs[:], in_=din["cn_w1"].rearrange("(a p) n -> p a n", p=128)[:, :, 512:1024]).then_inc(sem, 16)
        return None
    p.on("sp", l2, waits=[ch]); ch = p.bump("ld3", 16)
    sstep("dve", lambda e: e.tensor_copy(w1s[:, :, 512:1024], xs[:]))
    def l3(e):
        sem = p.sems["ld3"]
        e.dma_start(out=xs[:], in_=din["cn_w2"].rearrange("(a p) n -> p a n", p=128)).then_inc(sem, 16)
        return None
    p.on("sp", l3, waits=[ch]); ch = p.bump("ld3", 16)
    sstep("dve", lambda e: e.tensor_copy(w2s[:], xs[:]))
    def l4(e):
        sem = p.sems["ld3"]
        e.dma_start(out=xs[:, 0:4, 0:256], in_=din["cn_w3"].rearrange("(a p) n -> p a n", p=128)).then_inc(sem, 16)
        e.dma_start(out=xs[:, 4:6, 0:1], in_=din["cn_w4"].rearrange("(a p) n -> p a n", p=128),
                    allow_slow_non_contiguous=True).then_inc(sem, 16)
        e.dma_start(out=xs[:, 6:7, 0:512].rearrange("p a n -> p (a n)"), in_=din["gc_w1"].rearrange("(a p) n -> p a n", p=128).rearrange("p a n -> p (a n)") if False else din["gc_w1"].rearrange("(a p) n -> p (a n)", p=128)).then_inc(sem, 16)
        return None
    def l4b(e):
        sem = p.sems["ld3"]
        e.dma_start(out=xs[:, 0:4, 0:256], in_=din["cn_w3"].rearrange("(a p) n -> p a n", p=128)).then_inc(sem, 16)
        e.dma_start(out=xs[:, 4:6, 0:1], in_=din["cn_w4"].rearrange("(a p) n -> p a n", p=128),
                    allow_slow_non_contiguous=True).then_inc(sem, 16)
        return None
    p.on("sp", l4b, waits=[ch]); ch = p.bump("ld3", 32)
    sstep("dve", lambda e: e.tensor_copy(w3s[:], xs[:, 0:4, 0:256]))
    sstep("dve", lambda e: e.tensor_copy(w4s[:], xs[:, 4:6, 0:1]))
    def l5(e):
        sem = p.sems["ld3"]
        e.dma_start(out=xs[:, 0:8, 0:512], in_=din["gc_w1"].rearrange("(a p) n -> p a n", p=128)).then_inc(sem, 16)
        return None
    p.on("sp", l5, waits=[ch]); ch = p.bump("ld3", 16)
    sstep("dve", lambda e: e.tensor_copy(g1s[:], xs[:, 0:8, 0:512]))
    def l6(e):
        sem = p.sems["ld3"]
        e.dma_start(out=xs[:, 0:4, 0:256], in_=din["gc_w2"].rearrange("(a p) n -> p a n", p=128)).then_inc(sem, 16)
        e.dma_start(out=xs[:, 4:6, 0:1], in_=din["gc_w3"].rearrange("(a p) n -> p a n", p=128),
                    allow_slow_non_contiguous=True).then_inc(sem, 16)
        e.dma_start(out=xs[0:1, 6, 0:1], in_=din["adaptive_range"].rearrange("(o a) -> o a", o=1)).then_inc(sem, 16)
        e.dma_start(out=xs[:, 7, 0:8], in_=din["cn_b1"]).then_inc(sem, 16)
        e.dma_start(out=xs[:, 7, 8:12], in_=din["cn_b2"]).then_inc(sem, 16)
        e.dma_start(out=xs[:, 7, 12:14], in_=din["cn_b3"]).then_inc(sem, 16)
        e.dma_start(out=xs[0:1, 7, 14:15], in_=din["cn_b4"].rearrange("(o a) -> o a", o=1)).then_inc(sem, 16)
        e.dma_start(out=xs[:, 6, 8:12], in_=din["gc_b1"]).then_inc(sem, 16)
        e.dma_start(out=xs[:, 6, 12:14], in_=din["gc_b2"]).then_inc(sem, 16)
        e.dma_start(out=xs[0:1, 6, 15:16], in_=din["gc_b3"].rearrange("(o a) -> o a", o=1)).then_inc(sem, 16)
        return None
    p.on("sp", l6, waits=[ch]); ch = p.bump("ld3", 160)
    sstep("dve", lambda e: e.tensor_copy(g2s[:], xs[:, 0:4, 0:256]))
    sstep("dve", lambda e: e.tensor_copy(g3s[:], xs[:, 4:6, 0:1]))
    def cpb(e):
        e.tensor_copy(bias[:, 0:8], xs[:, 7, 0:8])
        e.tensor_copy(bias[:, 8:12], xs[:, 7, 8:12])
        e.tensor_copy(bias[:, 12:14], xs[:, 7, 12:14])
        e.tensor_copy(rows[:, 100:101], xs[0:1, 7, 14:15])
        e.tensor_copy(rows[:, 101:102], xs[0:1, 6, 15:16])
        e.tensor_copy(rows[:, 102:103], xs[0:1, 6, 0:1])
        e.tensor_copy(bias[:, 14:15].rearrange("p a -> p a") if False else rowbc[:, 0, 0:4], xs[:, 6, 8:12])
        return e.tensor_copy(rowbc[:, 1, 0:2], xs[:, 6, 12:14])
    sstep("dve", cpb)

    sdma("pool", lambda e: e.dma_start(out=dram["cc_mean_in"], in_=pacc[:]))
    def cc_fn(e):
        return e.collective_compute(
            "AllReduce", ALU.add, replica_groups=[list(range(8))],
            ins=[dram["cc_mean_in"].opt()], outs=[dram["cc_mean_out"].opt()])
    ch = p.on("pool", cc_fn, waits=[ch], inc="cc", amt=1)
    sdma("pool", lambda e: e.dma_start(out=pacc[:], in_=dram["cc_mean_out"]))
    sstep("dve", lambda e: e.tensor_scalar(out=gcx[:], in0=pacc[:], scalar1=1.0 / 16384.0,
                                           scalar2=None, op0=ALU.mult))
    def gc1(e):
        for m in range(4):
            for kt in range(KT):
                e.matmul(pD[:, m:m + 1], g1s[:, kt, 128 * m:128 * (m + 1)], gcx[:, kt:kt + 1],
                         start=(kt == 0), stop=(kt == KT - 1))
        return e.matmul(pD[:, 4:5], g1s[:, 0, 0:128], gcx[:, 0:1], start=True, stop=True)
    def gc1b(e):
        ins = None
        for m in range(4):
            for kt in range(KT):
                ins = e.matmul(pD[:, m:m + 1], g1s[:, kt, 128 * m:128 * (m + 1)], gcx[:, kt:kt + 1],
                               start=(kt == 0), stop=(kt == KT - 1))
        return ins
    sstep("pe", gc1b)
    def gc1act(e):
        ins = None
        for m in range(4):
            ins = e.activation(out=gch[:, m:m + 1], in_=pD[:, m:m + 1], func=AF.Gelu,
                               bias=rowbc[:, 0, m:m + 1])
        return ins
    sstep("act", gc1act)
    def gc2(e):
        ins = None
        for m in range(2):
            for kt in range(4):
                ins = e.matmul(pD[:, 8 + m:9 + m], g2s[:, kt, 128 * m:128 * (m + 1)], gch[:, kt:kt + 1],
                               start=(kt == 0), stop=(kt == 3))
        return ins
    sstep("pe", gc2)
    def gc2act(e):
        ins = None
        for m in range(2):
            ins = e.activation(out=gch[:, 4 + m:5 + m] if False else gch[:, m:m + 1], in_=pD[:, 8 + m:9 + m],
                               func=AF.Gelu, bias=rowbc[:, 1, m:m + 1])
        return ins
    sstep("act", gc2act)
    def gc3(e):
        ins = None
        for kt in range(2):
            ins = e.matmul(pD[0:1, 12:13], g3s[:, kt, 0:1], gch[:, kt:kt + 1],
                           start=(kt == 0), stop=(kt == 1))
        return ins
    sstep("pe", gc3)
    def gc3act(e):
        return e.activation(out=rows[:, 103:104], in_=pD[0:1, 12:13], func=AF.Sigmoid,
                            bias=rows[:, 101:102])
    sstep("act", gc3act)
    def couplc(e):
        e.tensor_scalar(out=rows[:, 104:105], in0=rows[:, 103:104], scalar1=0.7, scalar2=None, op0=ALU.mult)
        e.tensor_scalar(out=rows[:, 105:106], in0=rows[:, 102:103], scalar1=2.0, scalar2=None, op0=ALU.mult)
        return e.tensor_scalar(out=rows[:, 106:107], in0=rows[:, 102:103], scalar1=-1.0, scalar2=0.1,
                               op0=ALU.mult, op1=ALU.add)
    sstep("dve", couplc)

    twist_map = [(0, 4, 1), (1, 6, -1), (2, 0, -1), (3, 2, 1)]
    for s in range(NST):
        t0 = ST * s
        sdma("sp", lambda e, t0=t0: e.dma_start(
            out=xs[:], in_=dram["x1_dram"].rearrange("(a p) t -> p a t", p=128)[:, :, t0:t0 + ST]))
        cur, nxt = xs, xs2
        for cyc in range(3):
            sstep("dve", lambda e, cur=cur: e.tensor_copy(xs16[:], cur[:]))
            for m in range(KT):
                def m1(e, m=m):
                    ins = None
                    for kt in range(KT):
                        ins = e.matmul(pA[:], w1s[:, kt, 128 * m:128 * (m + 1)], xs16[:, kt, :],
                                       start=(kt == 0), stop=(kt == KT - 1))
                    return ins
                sstep("pe", m1)
                sstep("act", lambda e, m=m: e.activation(out=t1[:, m, :], in_=pA[:], func=AF.Gelu,
                                                         bias=bias[:, m:m + 1]))
            for m in range(4):
                def m2(e, m=m):
                    ins = None
                    for kt in range(KT):
                        ins = e.matmul(pA[:], w2s[:, kt, 128 * m:128 * (m + 1)], t1[:, kt, :],
                                       start=(kt == 0), stop=(kt == KT - 1))
                    return ins
                sstep("pe", m2)
                sstep("act", lambda e, m=m: e.activation(out=t2b[:, m, :], in_=pA[:], func=AF.Gelu,
                                                         bias=bias[:, 8 + m:9 + m]))
            for m in range(2):
                def m3(e, m=m):
                    ins = None
                    for kt in range(4):
                        ins = e.matmul(pA[:], w3s[:, kt, 128 * m:128 * (m + 1)], t2b[:, kt, :],
                                       start=(kt == 0), stop=(kt == 3))
                    return ins
                sstep("pe", m3)
                sstep("act", lambda e, m=m: e.activation(out=t3b[:, m, :], in_=pA[:], func=AF.Gelu,
                                                         bias=bias[:, 12 + m:13 + m]))
            def m4(e):
                ins = None
                for kt in range(2):
                    ins = e.matmul(pE1[:], t3b[:, kt, :].rearrange("p n -> p n") if False else w4s[:, kt, :],
                                   t3b[:, kt, :], start=(kt == 0), stop=(kt == 1))
                return ins
            sstep("pe", m4)
            sstep("act", lambda e: e.activation(out=tmp512[0:1, :], in_=pE1[:], func=AF.Sigmoid,
                                                bias=rows[:, 100:101]))
            sstep("dve", lambda e: e.tensor_scalar(out=tmp512[0:1, :], in0=tmp512[0:1, :], scalar1=0.3,
                                                   scalar2=rows[:, 104:105], op0=ALU.mult, op1=ALU.add))
            sstep("dve", lambda e: e.tensor_scalar(out=tmp512[0:1, :], in0=tmp512[0:1, :],
                                                   scalar1=rows[:, 105:106], scalar2=rows[:, 106:107],
                                                   op0=ALU.mult, op1=ALU.add))
            sstep("pe", lambda e: e.matmul(pB[:], ones1f[:], tmp512[0:1, :], start=True, stop=True))
            sstep("dve", lambda e: e.tensor_copy(rowbc[:, 0, :], pB[:]))
            for (qd, gsrc, sign) in twist_map:
                for t2_ in range(2):
                    gd = 2 * qd + t2_
                    gs = gsrc + t2_
                    def tw(e, gd=gd, gs=gs, sign=sign, cur=cur, nxt=nxt):
                        e.tensor_mul(tmp512[:], cur[:, gs, :], rowbc[:, 0, :])
                        return e.tensor_tensor(out=nxt[:, gd, :], in0=cur[:, gd, :], in1=tmp512[:],
                                               op=ALU.add if sign > 0 else ALU.subtract)
                    sstep("dve", tw)
            cur, nxt = nxt, cur
        def resln(e, cur=cur):
            ins = None
            for gg in range(KT):
                ins = e.engine_nop()
            return ins
        sdma("sp", lambda e, t0=t0, nxt=nxt: e.dma_start(
            out=nxt[:], in_=dram["x1_dram"].rearrange("(a p) t -> p a t", p=128)[:, :, t0:t0 + ST]))
        for gg in range(KT):
            sstep("dve", lambda e, gg=gg, cur=cur, nxt=nxt: e.tensor_tensor(
                out=cur[:, gg, :], in0=cur[:, gg, :], in1=nxt[:, gg, :], op=ALU.add))
        ln_emit(st, sstep, cur)
        ch = st["_ln_ch"]
        for tt in range(4):
            for gg in range(KT):
                def trx(e, gg=gg, tt=tt, cur=cur):
                    return e.transpose(pB[:, 128 * (gg % 4):128 * (gg % 4) + 128], cur[:, gg, 128 * tt:128 * tt + 128], eyef[:])
                def trx2(e, gg=gg, tt=tt, cur=cur):
                    return e.transpose(pB[:, 0:128], cur[:, gg, 128 * tt:128 * tt + 128], eyef[:])
                sstep("pe", trx2)
                sstep("dve", lambda e, gg=gg: e.tensor_copy(obuf[:, 128 * gg:128 * gg + 128], pB[:, 0:128]))
            def sto(e, tt=tt, t0=t0):
                return e.dma_start(out=out[t0 + 128 * tt:t0 + 128 * tt + 128, :], in_=obuf[:])
            ch = p.dma("sp", sto, waits=[ch], inc="st")
    st["ffn_done"] = ch


# version 11
# speedup vs baseline: 1.0023x; 1.0023x over previous
import math
import numpy as np
import ml_dtypes
import concourse.bass as bass
import concourse.mybir as mybir
from concourse.bass_utils import run_bass_kernel_spmd

F32 = mybir.dt.float32
BF16 = mybir.dt.bfloat16
AF = mybir.ActivationFunctionType
ALU = mybir.AluOpType
AXL = mybir.AxisListType

NCORES = 8
T = 2048
HALO = 66
TIN = T + HALO
VH = 63
TV = T + VH
H = 4
DK = 256
DV = 256
CH = 128
NCH = T // CH
ST = 512
NST = T // ST
D = 1024
KT = D // 128
GH = 2048
P5 = 5

DEBUG = []
CVALS = [1e-6, 1e-5, 0.5, -1.0, 1.0, 0.0, 0.2, 2.0]


class Prog:
    ENGINES = ("pe", "dve", "act", "pool", "sp")

    def __init__(self):
        self.steps = {e: [] for e in self.ENGINES}
        self.counts = {}

    def on(self, eng, fn, waits=(), inc=None, amt=1):
        tok = None
        if inc is not None:
            self.counts[inc] = self.counts.get(inc, 0) + amt
            tok = (inc, self.counts[inc])
        self.steps[eng].append((fn, list(waits), inc, amt))
        return tok

    def dma(self, eng, fn, waits=(), inc=None):
        return self.on(eng, fn, waits, inc, amt=16)

    def bump(self, sem, amt):
        self.counts[sem] = self.counts.get(sem, 0) + amt
        return (sem, self.counts[sem])

    def emit(self, nc, sems):
        self.sems = sems
        block_attr = {"pe": "tensor", "dve": "vector", "act": "scalar",
                      "pool": "gpsimd", "sp": "sync"}
        with nc.Block() as block:
            for eng_name in self.ENGINES:
                steps = self.steps[eng_name]
                if not steps:
                    continue
                def make_body(steps):
                    def body(eng):
                        for fn, waits, inc, amt in steps:
                            for (sem, cnt) in waits:
                                eng.wait_ge(sems[sem], cnt)
                            last = fn(eng)
                            if inc is not None:
                                assert last is not None, f"step for {inc} returned no instruction"
                                last.then_inc(sems[inc], amt)
                    return body
                getattr(block, block_attr[eng_name])(make_body(steps))


def ceil_div(a, b):
    return (a + b - 1) // b


def build_nc():
    nc = bass.Bass("TRN2", target_bir_lowering=False, debug=False)
    p = Prog()

    din = {}
    def inp(name, shape, dtype=F32):
        din[name] = nc.dram_tensor(name, shape, dtype, kind="ExternalInput").ap()
        return din[name]

    hsT = inp("hsT", [D, TIN])
    wq = inp("wq", [D, D]); wk = inp("wk", [D, D]); wv = inp("wv", [D, D])
    wb = inp("wb", [D, H])
    conv_q = inp("conv_q", [D, 4]); conv_k = inp("conv_k", [D, 4]); conv_v = inp("conv_v", [D, 4])
    fir_s = inp("fir_s", [D, 3]); fir_m = inp("fir_m", [D, 15]); fir_l = inp("fir_l", [D, 64])
    gw1_slice = inp("gw1_slice", [20 * 128, GH])
    gw1_hs = inp("gw1_hs", [D, GH])
    gb1 = inp("gb1", [128, 16]); gw2 = inp("gw2", [GH, 20]); gb2 = inp("gb2", [20])
    log_tau = inp("log_tau", [H]); eps_logit = inp("eps_logit", [H, 5])
    norm_w = inp("norm_w", [128, 2]); wo = inp("wo", [D, D])
    cn_w1 = inp("cn_w1", [D, D]); cn_b1 = inp("cn_b1", [128, 8])
    cn_w2 = inp("cn_w2", [D, 512]); cn_b2 = inp("cn_b2", [128, 4])
    cn_w3 = inp("cn_w3", [512, 256]); cn_b3 = inp("cn_b3", [128, 2])
    cn_w4 = inp("cn_w4", [256, 1]); cn_b4 = inp("cn_b4", [1])
    gc_w1 = inp("gc_w1", [D, 512]); gc_b1 = inp("gc_b1", [128, 4])
    gc_w2 = inp("gc_w2", [512, 256]); gc_b2 = inp("gc_b2", [128, 2])
    gc_w3 = inp("gc_w3", [256, 1]); gc_b3 = inp("gc_b3", [1])
    adaptive_range = inp("adaptive_range", [1])
    eye16 = inp("eye16", [128, 128], BF16)
    eyef = inp("eyef", [128, 128])
    mask3 = inp("mask3", [128, 384])
    sel20 = inp("sel20", [20, 20, 128])
    sel_m = inp("sel_m", [128, 20, 80], BF16)
    sel_v = inp("sel_v", [128, 20, 80], BF16)
    sel_a = inp("sel_a", [128, 20, 80], BF16)
    ones1 = inp("ones1", [1, 128], BF16)
    ones1f = inp("ones1f", [1, 128])
    onesc = inp("onesc", [128, 1], BF16)
    onescf = inp("onescf", [128, 1])
    g0c = inp("g0c", [128, 2, 512], BF16)
    gsel = inp("gsel", [128, 20, 10])
    pref_mask = inp("pref_mask", [128, 8])
    pref_inv = inp("pref_inv", [128, 8])
    cvals = inp("cvals", [128, 8])
    cvals16 = inp("cvals16", [128, 8], BF16)

    out = nc.dram_tensor("out", [T, D], F32, kind="ExternalOutput").ap()

    dbg = {}
    for name in DEBUG:
        shp = DEBUG_SHAPES[name]
        dbg[name] = nc.dram_tensor("dbg_" + name, shp, F32, kind="ExternalOutput").ap()

    cc_g_in = nc.dram_tensor("cc_g_in", [H * 2 * 128, 512], F32).ap()
    cc_g_out = nc.dram_tensor("cc_g_out", [8 * H * 2 * 128, 512], F32).ap()
    cc_gw_in = nc.dram_tensor("cc_gw_in", [10, GH], F32).ap()
    cc_gw_out = nc.dram_tensor("cc_gw_out", [80, GH], F32).ap()
    cc_mean_in = nc.dram_tensor("cc_mean_in", [128, 8], F32).ap()
    cc_mean_out = nc.dram_tensor("cc_mean_out", [128, 8], F32).ap()
    x1_dram = nc.dram_tensor("x1_dram", [D, T], F32).ap()

    sem_names = ["ld", "pe", "dve", "act", "pool", "sp", "lw", "st", "cc", "ld2", "ld3",
                 "lda", "ldb", "lwa", "lwb", "pld"]

    import contextlib
    guards = {}
    tensors = {}

    def salloc(name, shape, dtype):
        g = nc.sbuf_tensor(name, shape, dtype)
        t = g.__enter__()
        guards[name] = g
        tensors[name] = t
        return t

    def sfree(*names):
        for n in names:
            guards.pop(n).__exit__(None, None, None)

    def palloc(name, shape, dtype=F32):
        g = nc.psum_tensor(name, shape, dtype)
        t = g.__enter__()
        guards[name] = g
        tensors[name] = t
        return t

    build_program(nc, p, din, out, dbg,
                  dict(cc_g_in=cc_g_in, cc_g_out=cc_g_out, cc_gw_in=cc_gw_in,
                       cc_gw_out=cc_gw_out, cc_mean_in=cc_mean_in,
                       cc_mean_out=cc_mean_out, x1_dram=x1_dram),
                  salloc, sfree, palloc)

    with contextlib.ExitStack() as ctx:
        sems = {s: ctx.enter_context(nc.semaphore(s)) for s in sem_names}
        p.emit(nc, sems)
    for n in reversed(list(guards)):
        guards.pop(n).__exit__(None, None, None)
    return nc


DEBUG_SHAPES = {
    "o16": [128, KT, T],
    "x1dump": [D, T],
    "probs": [NST, 128, 4, 20],
    "logitsdump": [NST, 128, 4, 20],
    "rowsdump": [1, 128],
    "gwout": [80, GH],
    "gredump": [10, GH],
    "statsdump": [NST, 128, 512],
    "epsbcdump": [128, 40],
    "gw2dump": [128, 16, 20],
    "ocomb": [NST, 128, KT, 512],
    "stats80": [128, T],
    "hgate": [128, 16, 512],
    "gout": [8 * H * 2 * 128, 512],
    "fst": [128, H * NCH, 128],
    "sinit": [128, 2 * H, 256],
    "q16": [128, KT, T],
    "k16": [128, KT, T],
    "v16": [128, KT, TV],
    "beta": [128, NCH, H],
    "rsq": [1, 8, T],
}


def build_program(nc, p, din, out, dbg, dram, salloc, sfree, palloc):
    st = {}
    st.update(nc=nc, p=p, din=din, out=out, dbg=dbg, dram=dram,
              salloc=salloc, sfree=sfree, palloc=palloc)
    phase15_gw1(st)
    phase1_front(st)
    phase2_delta(st)
    phase3_gate(st)
    phase4_ffn(st)
    finish_debug(st)


def phase1_front(st):
    nc, p, din = st["nc"], st["p"], st["din"]
    salloc, palloc = st["salloc"], st["palloc"]

    cw = salloc("cw", [128, KT, 12], F32)
    s_eye16 = salloc("s_eye16", [128, 128], BF16)
    s_eyef = salloc("s_eyef", [128, 128], F32)
    s_onesc = salloc("s_onesc", [128, 1], BF16)
    s_wb = salloc("s_wb", [128, KT, H], BF16)
    s_ones1f = salloc("s_ones1f", [1, 128], F32)
    s_ones1 = salloc("s_ones1", [1, 128], BF16)
    s_onescf = salloc("s_onescf", [128, 1], F32)
    s_mask3 = salloc("s_mask3", [128, 384], F32)
    s_cvals = salloc("s_cvals", [128, 8], F32)
    s_cvals16 = salloc("s_cvals16", [128, 8], BF16)
    for i, v in enumerate(CVALS):
        nc.const_aps.register(F32, v, s_cvals[:, i:i + 1])
        nc.const_aps.register(BF16, v, s_cvals16[:, i:i + 1])

    v16 = salloc("v16", [128, KT, TV], BF16)
    o16 = salloc("o16", [128, KT, T], BF16)
    q16 = salloc("q16", [128, KT, T], BF16)
    k16 = salloc("k16", [128, KT, T], BF16)
    beta_tm = salloc("beta_tm", [128, NCH, H], F32)
    hs16 = salloc("hs16", [128, KT, TIN], BF16)
    st["o16"] = o16
    rsq = salloc("rsq", [1, 576], F32)
    st.update(hs16=hs16, q16=q16, k16=k16, v16=v16, beta_tm=beta_tm,
              s_eye16=s_eye16, s_eyef=s_eyef, s_onesc=s_onesc, rsq=rsq,
              s_ones1f=s_ones1f, s_ones1=s_ones1, s_onescf=s_onescf,
              s_mask3=s_mask3, s_cvals=s_cvals, s_cvals16=s_cvals16)

    hsf = salloc("hsf", [128, 2, KT, 544], F32)

    pp = [palloc("pp0", [128, 512], F32), palloc("pp1", [128, 512], F32)]

    NC_IN = ceil_div(TIN, 512)
    def chunk_rng(i, total):
        lo = 512 * i
        return lo, min(512, total - lo)

    castok = {}
    for i in range(NC_IN):
        lo, ln = chunk_rng(i, TIN)
        bi = i % 2
        def ld(eng, lo=lo, ln=ln, bi=bi):
            return eng.dma_start(out=hsf[:, bi, :, 0:ln],
                                 in_=din["hsT"].rearrange("(a p) t -> p a t", p=128)[:, :, lo:lo + ln])
        waits = [castok[i - 2]] if i >= 2 else []
        ldtok = p.dma("sp", ld, waits=waits, inc="lda" if bi == 0 else "ldb")
        def cast(eng, lo=lo, ln=ln, bi=bi):
            return eng.tensor_copy(hs16[:, :, lo:lo + ln], hsf[:, bi, :, 0:ln])
        castok[i] = p.on("dve", cast, waits=[ldtok], inc="dve")
    hs16_ready = castok[NC_IN - 1]
    st["hs16_ready"] = hs16_ready

    def ld_misc(eng):
        sem = p.sems["ld2"]
        eng.dma_start(out=cw[:, :, 0:4], in_=din["conv_q"].rearrange("(a p) j -> p a j", p=128)).then_inc(sem, 16)
        eng.dma_start(out=cw[:, :, 4:8], in_=din["conv_k"].rearrange("(a p) j -> p a j", p=128)).then_inc(sem, 16)
        eng.dma_start(out=cw[:, :, 8:12], in_=din["conv_v"].rearrange("(a p) j -> p a j", p=128)).then_inc(sem, 16)
        eng.dma_start(out=s_eye16[:], in_=din["eye16"]).then_inc(sem, 16)
        eng.dma_start(out=s_eyef[:], in_=din["eyef"]).then_inc(sem, 16)
        eng.dma_start(out=s_mask3[:], in_=din["mask3"]).then_inc(sem, 16)
        eng.dma_start(out=s_ones1f[:], in_=din["ones1f"]).then_inc(sem, 16)
        eng.dma_start(out=s_ones1[:], in_=din["ones1"]).then_inc(sem, 16)
        eng.dma_start(out=s_onescf[:], in_=din["onescf"]).then_inc(sem, 16)
        eng.dma_start(out=s_cvals[:], in_=din["cvals"]).then_inc(sem, 16)
        eng.dma_start(out=s_cvals16[:], in_=din["cvals16"]).then_inc(sem, 16)
        eng.dma_start(out=s_onesc[:], in_=din["onesc"]).then_inc(sem, 16)
        return None
    p.on("sp", ld_misc, waits=[])
    misc_tok = p.bump("ld2", 16 * 12)
    st["misc_tok"] = misc_tok

    def ld_wb(eng):
        return eng.dma_start(out=hsf[:, 0, :, 512:512 + H],
                             in_=din["wb"].rearrange("(a p) h -> p a h", p=128))
    wb_tok = p.dma("sp", ld_wb, waits=[castok[NC_IN - 2]], inc="ld")
    def cast_wb(eng):
        return eng.tensor_copy(s_wb[:], hsf[:, 0, :, 512:512 + H])
    wb16_tok = p.on("dve", cast_wb, waits=[wb_tok], inc="dve")

    st["sfree"]("hsf")
    wbuf = salloc("wbuf", [128, 2, KT, 128], F32)
    wbuf16 = salloc("wbuf16", [128, 2, KT, 128], BF16)
    raw = salloc("raw", [128, 2, TIN], BF16)
    cacc = salloc("cacc", [128, 544], F32)

    projs = [("wq", 0, q16, T, 66), ("wk", 1, k16, T, 66), ("wv", 2, v16, TV, 3)]
    prev_cast = {}
    prev_silu = None
    ev_toks = []
    raw_free = {}
    step = 0
    for wi, (wname, cidx, dst, tlen, off) in enumerate(projs):
        for m in range(KT):
            gi = wi * KT + m
            bi = gi % 2
            def ldw(eng, wname=wname, m=m, bi=bi):
                return eng.dma_start(
                    out=wbuf[:, bi],
                    in_=din[wname].rearrange("(a p) n -> p a n", p=128)[:, :, 128 * m:128 * (m + 1)])
            waits = [prev_cast[bi]] if prev_cast.get(bi) else [hs16_ready, wb16_tok]
            lw_tok = p.dma("sp", ldw, waits=waits, inc="lwa" if bi == 0 else "lwb")
            def castw(eng, bi=bi):
                return eng.tensor_copy(wbuf16[:, bi], wbuf[:, bi])
            cw_tok = p.on("dve", castw, waits=[lw_tok], inc="dve")
            prev_cast[bi] = cw_tok
            rwaits = [raw_free[bi]] if bi in raw_free else []
            for i in range(NC_IN):
                lo, ln = chunk_rng(i, TIN)
                ps = pp[step % 2]
                def mm(eng, bi=bi, lo=lo, ln=ln, ps=ps):
                    ins = None
                    for kt in range(KT):
                        ins = eng.matmul(ps[:, 0:ln], wbuf16[:, bi, kt, :],
                                         hs16[:, kt, lo:lo + ln], start=(kt == 0), stop=(kt == KT - 1))
                    return ins
                waits = [cw_tok, hs16_ready]
                if step >= 2:
                    waits.append(ev_toks[step - 2])
                mm_tok = p.on("pe", mm, waits=waits, inc="pe")
                def ev(eng, bi=bi, lo=lo, ln=ln, ps=ps):
                    return eng.copy(raw[:, bi, lo:lo + ln], ps[:, 0:ln])
                ev_toks.append(p.on("act", ev, waits=[mm_tok] + (rwaits if i == 0 else []), inc="act"))
                step += 1
            raw_ready = ev_toks[-1]
            NCC = ceil_div(tlen, 512)
            cv_tok = None
            for i in range(NCC):
                lo, ln = chunk_rng(i, tlen)
                base = off - 3 + lo
                def conv(eng, m=m, bi=bi, lo=lo, ln=ln, base=base, cidx=cidx):
                    eng.tensor_scalar(out=cacc[:, 0:ln], in0=raw[:, bi, base:base + ln],
                                      scalar1=cw[:, m, 4 * cidx:4 * cidx + 1], scalar2=None,
                                      op0=ALU.mult)
                    ins = None
                    for j in range(1, 4):
                        ins = eng.scalar_tensor_tensor(
                            out=cacc[:, 0:ln], in0=raw[:, bi, base + j:base + j + ln],
                            scalar=cw[:, m, 4 * cidx + j:4 * cidx + j + 1],
                            in1=cacc[:, 0:ln], op0=ALU.mult, op1=ALU.add)
                    return ins
                cv_tok = p.on("dve", conv, waits=[raw_ready, misc_tok] + ([prev_silu] if prev_silu else []), inc="dve")
                def silu(eng, m=m, lo=lo, ln=ln, dst=dst):
                    return eng.activation(out=dst[:, m, lo:lo + ln], in_=cacc[:, 0:ln], func=AF.Silu)
                prev_silu = p.on("act", silu, waits=[cv_tok], inc="act")
            raw_free[bi] = cv_tok
    silu_all = ("act", p.counts["act"])
    st["qkv_ready"] = silu_all

    ppb = palloc("ppb", [128, H], F32)
    beta_ev = None
    for c in range(NCH):
        lo = HALO + CH * c
        def mmb(eng, lo=lo):
            ins = None
            for kt in range(KT):
                ins = eng.matmul(ppb[:], hs16[:, kt, lo:lo + CH], s_wb[:, kt, :],
                                 start=(kt == 0), stop=(kt == KT - 1))
            return ins
        tok = p.on("pe", mmb, waits=[hs16_ready, wb16_tok] + ([beta_ev] if beta_ev else []), inc="pe")
        def sig(eng, c=c):
            return eng.activation(out=beta_tm[:, c, :], in_=ppb[:], func=AF.Sigmoid)
        beta_ev = p.on("act", sig, waits=[tok], inc="act")
    st["beta_ready"] = beta_ev

    st["l2_done"] = l2norm(st, pp, q16, k16, s_onesc, silu_all)

    emit_debug(st, dict(q16=q16, k16=k16, v16=v16, beta=beta_tm))
    st["sfree"]("cacc", "raw", "wbuf16", "wbuf")
    st["sfree"]("rsq")
    for nm in ("pbc", "prs", "ppb", "pp1", "pp0"):
        st["sfree"](nm)
    st["sfree"]("hs16")


def l2norm(st, pp, q16, k16, s_onesc, qkv_ready):
    nc, p = st["nc"], st["p"]
    salloc, palloc = st["salloc"], st["palloc"]
    rsq = st["rsq"]
    sq = salloc("sq", [128, 2, 512], BF16)
    prs = palloc("prs", [1, 512], F32)
    pbc = palloc("pbc", [128, 512], F32)
    bc = salloc("bc", [128, 512], F32)
    last_sc = None
    prev_sr = None
    prev_bc = None
    for ti, tens in ((0, q16), (1, k16)):
        for h in range(H):
            for i in range(NST):
                lo = ST * i
                def sqf(eng, tens=tens, h=h, lo=lo):
                    eng.tensor_mul(sq[:, 0, :], tens[:, 2 * h, lo:lo + ST], tens[:, 2 * h, lo:lo + ST])
                    return eng.tensor_mul(sq[:, 1, :], tens[:, 2 * h + 1, lo:lo + ST], tens[:, 2 * h + 1, lo:lo + ST])
                sq_tok = p.on("dve", sqf, waits=[qkv_ready, st["misc_tok"]] + ([last_sc] if last_sc else []), inc="dve")
                def mm(eng):
                    eng.matmul(prs[:], s_onesc[:], sq[:, 0, :], start=True, stop=False)
                    return eng.matmul(prs[:], s_onesc[:], sq[:, 1, :], start=False, stop=True)
                mm_tok = p.on("pe", mm, waits=[sq_tok] + ([prev_sr] if prev_sr else []), inc="pe")
                def sqr(eng):
                    return eng.activation(out=rsq[:, 0:ST], in_=prs[:], func=AF.Sqrt, bias=1e-6)
                sr_tok = p.on("act", sqr, waits=[mm_tok] + ([prev_bc] if prev_bc else []), inc="act")
                prev_sr = sr_tok
                def rec(eng):
                    return eng.reciprocal(rsq[:, 0:ST], rsq[:, 0:ST])
                rc_tok = p.on("dve", rec, waits=[sr_tok], inc="dve")
                def bcm(eng):
                    return eng.matmul(pbc[:], st["s_ones1f"][:], rsq[:, 0:ST],
                                      start=True, stop=True)
                bc_tok = p.on("pe", bcm, waits=[rc_tok] + ([last_sc] if last_sc else []), inc="pe")
                prev_bc = bc_tok
                def evbc(eng):
                    return eng.tensor_copy(bc[:], pbc[:])
                ev_tok = p.on("dve", evbc, waits=[bc_tok], inc="dve")
                def scale(eng, tens=tens, h=h, lo=lo):
                    eng.tensor_mul(tens[:, 2 * h, lo:lo + ST], tens[:, 2 * h, lo:lo + ST], bc[:])
                    return eng.tensor_mul(tens[:, 2 * h + 1, lo:lo + ST], tens[:, 2 * h + 1, lo:lo + ST], bc[:])
                last_sc = p.on("dve", scale, waits=[ev_tok], inc="dve")
    st["sfree"]("bc", "sq")
    return last_sc


def emit_debug(st, mapping):
    p, dbg = st["p"], st["dbg"]
    for name, tens in mapping.items():
        if name in dbg:
            def store(eng, tens=tens, name=name):
                return eng.dma_start(out=dbg[name], in_=tens[:])
            waits = [(s, c) for s, c in p.counts.items()]
            p.dma("pool", store, waits=waits, inc="st")


def finish_debug(st):
    p = st["p"]
    def fin(eng):
        return eng.nop()
    waits = [(s, c) for s, c in p.counts.items()]
    p.on("sp", fin, waits=waits)



def make_consts():
    c = {}
    c["eye16"] = np.eye(128).astype(ml_dtypes.bfloat16)
    c["eyef"] = np.eye(128, dtype=np.float32)
    i = np.arange(128)
    mb = np.zeros((128, 384), np.float32)
    mb[:, 0:128] = -(i[:, None] < i[None, :]).astype(np.float32)
    mb[:, 128:256] = -(i[:, None] > i[None, :]).astype(np.float32)
    mb[:, 256:384] = (i[:, None] <= i[None, :]).astype(np.float32)
    c["mask3"] = mb
    sel = np.zeros((20, 20, 128), np.float32)
    for r in range(20):
        sel[r, r, :] = 1.0
    c["sel20"] = sel
    c["ones1"] = np.ones((1, 128), ml_dtypes.bfloat16)
    c["ones1f"] = np.ones((1, 128), np.float32)
    c["onesc"] = np.ones((128, 1), ml_dtypes.bfloat16)
    c["onescf"] = np.ones((128, 1), np.float32)
    g0 = np.zeros((128, 2, 512), np.float32)
    for t in range(2):
        g0[:, t, 128 * t:128 * (t + 1)] = np.eye(128)
    c["g0c"] = g0.astype(ml_dtypes.bfloat16)
    gs = np.zeros((128, 20, 10), np.float32)
    for t in range(20):
        gs[:, t, t // 2] = 1.0
    c["gsel"] = gs
    for nm, entries in (("sel_m", [(1, 1.0 / 256)]), ("sel_v", [(2, 1.0 / 256), (0, 1.0)]),
                        ("sel_a", [(3, 1.0 / 256)])):
        sl = np.zeros((128, 20, 80), np.float32)
        for pth in range(5):
            for hh in range(4):
                for blk, val in entries:
                    sl[:, pth * 4 + hh, 20 * blk + pth * 4 + hh] = val
        c[nm] = sl.astype(ml_dtypes.bfloat16)
    cv = np.broadcast_to(np.asarray(CVALS, np.float32), (128, 8)).copy()
    c["cvals"] = cv
    c["cvals16"] = cv.astype(ml_dtypes.bfloat16)
    return c


def kernel(**inputs):
    hs = np.ascontiguousarray(np.asarray(inputs["L0_attn_hidden_states"], dtype=np.float32).reshape(16384, D))
    consts = make_consts()
    weights = {
        "wq": "L0_attn_Wq", "wk": "L0_attn_Wk", "wv": "L0_attn_Wv", "wb": "L0_attn_Wb",
        "conv_q": "L0_attn_conv_q", "conv_k": "L0_attn_conv_k", "conv_v": "L0_attn_conv_v",
        "fir_s": "L0_attn_fir_s", "fir_m": "L0_attn_fir_m", "fir_l": "L0_attn_fir_l",
        "gb1": "L0_attn_gb1", "gw2": "L0_attn_gW2", "gb2": "L0_attn_gb2",
        "log_tau": "L0_attn_log_tau", "eps_logit": "L0_attn_eps_logit",
        "norm_w": "L0_attn_norm_w", "wo": "L0_attn_Wo",
        "cn_w1": "L0_ffn_cn_w1", "cn_b1": "L0_ffn_cn_b1", "cn_w2": "L0_ffn_cn_w2",
        "cn_b2": "L0_ffn_cn_b2", "cn_w3": "L0_ffn_cn_w3", "cn_b3": "L0_ffn_cn_b3",
        "cn_w4": "L0_ffn_cn_w4", "cn_b4": "L0_ffn_cn_b4",
        "gc_w1": "L0_ffn_gc_w1", "gc_b1": "L0_ffn_gc_b1", "gc_w2": "L0_ffn_gc_w2",
        "gc_b2": "L0_ffn_gc_b2", "gc_w3": "L0_ffn_gc_w3", "gc_b3": "L0_ffn_gc_b3",
        "adaptive_range": "L0_ffn_adaptive_range",
    }
    shared = {k: np.ascontiguousarray(np.asarray(inputs[v], dtype=np.float32)) for k, v in weights.items()}
    for nm, nt in (("gb1", 16), ("norm_w", 2), ("cn_b1", 8), ("cn_b2", 4), ("cn_b3", 2),
                   ("gc_b1", 4), ("gc_b2", 2)):
        shared[nm] = np.ascontiguousarray(shared[nm].reshape(nt, 128).T)
    gW1 = np.ascontiguousarray(np.asarray(inputs["L0_attn_gW1"], dtype=np.float32))
    shared["gw1_hs"] = gW1[0:D]

    nc = build_nc()
    in_maps = []
    for c in range(NCORES):
        m = dict(shared)
        m.update(consts)
        lo = T * c - HALO
        if lo < 0:
            sl = np.concatenate([np.zeros((HALO, D), np.float32), hs[0:T * c + T]], axis=0)
        else:
            sl = hs[lo:T * (c + 1)]
        m["hsT"] = np.ascontiguousarray(sl.T)
        m["gw1_slice"] = np.ascontiguousarray(gW1[D + 2560 * c: D + 2560 * (c + 1)])
        pm = np.zeros((128, 8), np.float32)
        pm[:, 0:c] = 1.0
        m["pref_mask"] = pm
        m["pref_inv"] = 1.0 - pm
        in_maps.append(m)
    res = run_bass_kernel_spmd(nc, in_maps, core_ids=list(range(NCORES)))
    outs = [res.results[c]["out"] for c in range(NCORES)]
    full = np.concatenate(outs, axis=0).reshape(1, 16384, D)
    if DEBUG:
        kernel.debug_results = res.results
    return full


def phase2_delta(st):
    nc, p, din, dram = st["nc"], st["p"], st["din"], st["dram"]
    salloc, sfree, palloc = st["salloc"], st["sfree"], st["palloc"]
    q16, k16, v16, beta = st["q16"], st["k16"], st["v16"], st["beta_tm"]
    eye16, eyef, ones1f, mask3 = st["s_eye16"], st["s_eyef"], st["s_ones1f"], st["s_mask3"]

    Fst = salloc("Fst", [128, H * NCH, 128], BF16)
    o16 = st["o16"]
    ktm = salloc("ktm", [128, 256], BF16)
    vtm = salloc("vtm", [128, 256], BF16)
    nkvb = salloc("nkvb", [128, 512], BF16)
    kbf = salloc("kbf", [128, 256], BF16)
    brow = salloc("brow", [1, 128], F32)
    bbc = salloc("bbc", [128, 128], BF16)
    BAL = salloc("BAL", [128, 384], BF16)
    BALf = salloc("BALf", [128, 384], F32)
    PBf = salloc("PBf", [128, 256], F32)
    ngs = salloc("ngs", [128, 256], BF16)
    lcl = salloc("lcl", [128, 128], BF16)
    us = salloc("us", [128, 512], BF16)
    Gs = salloc("Gs", [128, 2, 512], BF16)
    goutf = salloc("goutf", [128, 2, 512], F32)
    s_pref = salloc("s_pref", [128, 16], F32)
    gpref = salloc("gpref", [128, 2, 512], F32)
    gj16 = salloc("gj16", [128, 2, 512], BF16)
    mjt = salloc("mjt", [128, 2, 256], BF16)
    Sinit = salloc("Sinit", [128, 2 * H, 256], BF16)
    Ss = salloc("Ss", [128, 2, 256], BF16)
    u16s = salloc("u16s", [128, 256], BF16)
    otm = salloc("otm", [128, 256], BF16)
    g0s = salloc("g0s", [128, 2, 512], BF16)

    pG = palloc("pG", [128, 2, 512], F32)
    pU = palloc("pU", [128, 512], F32)
    pM = palloc("pM", [128, 384], F32)
    pF = palloc("pF", [128, 128], F32)
    pNG = palloc("pNG", [128, 256], F32)
    pT = palloc("pT", [128, 256], BF16)
    pR = palloc("pR", [1, 128], F32)

    ch = st["l2_done"]

    first = [True]
    def sstep(eng, fn, extra=()):
        nonlocal ch
        w = [ch] + list(extra)
        if first[0]:
            w.append(st["gw1_done"])
            first[0] = False
        ch = p.on(eng, fn, waits=w, inc=eng if eng in ("pe", "dve", "act", "pool") else "sp",
                  amt=16 if eng in () else 1)
        return ch

    def sdma(eng, fn, extra=()):
        nonlocal ch
        ch = p.dma(eng, fn, waits=[ch] + list(extra), inc="ld3")
        return ch

    def ld_pref(eng):
        sem = p.sems["ld3"]
        eng.dma_start(out=s_pref[:, 0:8], in_=din["pref_mask"]).then_inc(sem, 16)
        eng.dma_start(out=s_pref[:, 8:16], in_=din["pref_inv"]).then_inc(sem, 16)
        eng.dma_start(out=g0s[:], in_=din["g0c"]).then_inc(sem, 16)
        return None
    p.on("sp", ld_pref, waits=[ch])
    pref_tok = p.bump("ld3", 48)

    beta_ready = st["beta_ready"]

    def prep_chunk(h, c, mode):
        cb = CH * c
        fidx = h * NCH + c
        def tr_k(eng):
            eng.transpose(pT[:, 0:128], k16[:, 2 * h, cb:cb + 128], eye16[:])
            return eng.transpose(pT[:, 128:256], k16[:, 2 * h + 1, cb:cb + 128], eye16[:])
        sstep("pe", tr_k)
        sstep("act", lambda eng: eng.copy(ktm[:], pT[:]))
        def tr_v(eng):
            eng.transpose(pT[:, 0:128], v16[:, 2 * h, VH + cb:VH + cb + 128], eye16[:])
            return eng.transpose(pT[:, 128:256], v16[:, 2 * h + 1, VH + cb:VH + cb + 128], eye16[:])
        sstep("pe", tr_v)
        sstep("act", lambda eng: eng.copy(vtm[:], pT[:]))
        def mk_nkvb(eng):
            eng.tensor_scalar(out=nkvb[:, 0:256], in0=ktm[:], scalar1=beta[:, c, h:h + 1],
                              scalar2=-1.0, op0=ALU.mult, op1=ALU.mult)
            return eng.tensor_scalar(out=nkvb[:, 256:512], in0=vtm[:], scalar1=beta[:, c, h:h + 1],
                                     scalar2=None, op0=ALU.mult)
        sstep("dve", mk_nkvb, extra=[beta_ready])
        if mode == "A":
            def brow_mm(eng):
                return eng.matmul(pR[:], beta[:, c, h:h + 1], eyef[:], start=True, stop=True)
            sstep("pe", brow_mm)
            sstep("act", lambda eng: eng.copy(brow[:], pR[:]))
            def bbc_mm(eng):
                return eng.matmul(pU[:, 0:128], ones1f[:], brow[:], start=True, stop=True)
            sstep("pe", bbc_mm)
            sstep("dve", lambda eng: eng.tensor_copy(bbc[:], pU[:, 0:128]))
            def mk_kbf(eng):
                eng.tensor_mul(kbf[:, 0:128], k16[:, 2 * h, cb:cb + 128], bbc[:])
                return eng.tensor_mul(kbf[:, 128:256], k16[:, 2 * h + 1, cb:cb + 128], bbc[:])
            sstep("dve", mk_kbf)
            def grams(eng):
                for t in range(2):
                    eng.matmul(pM[:, 0:128], k16[:, 2 * h + t, cb:cb + 128], kbf[:, 128 * t:128 * t + 128],
                               start=(t == 0), stop=(t == 1))
                for t in range(2):
                    eng.matmul(pM[:, 128:256], kbf[:, 128 * t:128 * t + 128], k16[:, 2 * h + t, cb:cb + 128],
                               start=(t == 0), stop=(t == 1))
                return eng.matmul(pM[:, 256:384], eye16[:], eye16[:], start=True, stop=True)
            def grams2(eng):
                ins = None
                for t in range(2):
                    ins = eng.matmul(pM[:, 0:128], k16[:, 2 * h + t, cb:cb + 128], kbf[:, 128 * t:128 * t + 128],
                                     start=(t == 0), stop=(t == 1))
                for t in range(2):
                    ins = eng.matmul(pM[:, 128:256], kbf[:, 128 * t:128 * t + 128], k16[:, 2 * h + t, cb:cb + 128],
                                     start=(t == 0), stop=(t == 1))
                return ins
            sstep("pe", grams2)
            def maskba(eng):
                return eng.tensor_mul(BALf[:, 0:256], pM[:, 0:256], mask3[:, 0:256])
            sstep("dve", maskba)
            def f0(eng):
                return eng.tensor_add(BALf[:, 256:384], BALf[:, 0:128], eyef[:])
            sstep("dve", f0)
            def finit(eng):
                return eng.matmul(pF[:], eyef[:], BALf[:, 256:384], start=True, stop=False)
            sstep("pe", finit)
            fsb_cur = BALf[:, 256:384]
            pb_cur = BALf[:, 0:256]
            for lvl in range(1, 7):
                def sq(eng, pb_cur=pb_cur):
                    eng.matmul(pM[:, 0:128], pb_cur[:, 128:256], pb_cur[:, 0:128], start=True, stop=True)
                    return eng.matmul(pM[:, 128:256], pb_cur[:, 0:128], pb_cur[:, 128:256], start=True, stop=True)
                sstep("pe", sq)
                def evpb(eng):
                    return eng.tensor_copy(PBf[:], pM[:, 0:256])
                sstep("dve", evpb)
                def fcomb(eng, fsb_cur=fsb_cur, last=(lvl == 6)):
                    return eng.matmul(pF[:], PBf[:, 128:256], fsb_cur, start=False, stop=last)
                sstep("pe", fcomb)
                if lvl == 6:
                    def evf(eng, fidx=fidx):
                        return eng.tensor_copy(Fst[:, fidx, :], pF[:])
                    sstep("dve", evf)
                else:
                    def evf(eng):
                        return eng.tensor_copy(BALf[:, 256:384], pF[:])
                    sstep("dve", evf)
                fsb_cur = BALf[:, 256:384]
                pb_cur = PBf
        def ngmm(eng):
            eng.matmul(pNG[:, 0:128], nkvb[:, 0:128], Fst[:, fidx, :], start=True, stop=True)
            return eng.matmul(pNG[:, 128:256], nkvb[:, 128:256], Fst[:, fidx, :], start=True, stop=True)
        sstep("pe", ngmm)
        sstep("act", lambda eng: eng.copy(ngs[:], pNG[:]))
        if mode == "B":
            def gram3(eng):
                ins = None
                for t in range(2):
                    ins = eng.matmul(pM[:, 256:384], k16[:, 2 * h + t, cb:cb + 128],
                                     q16[:, 2 * h + t, cb:cb + 128], start=(t == 0), stop=(t == 1))
                return ins
            sstep("pe", gram3)
            def masklcl(eng):
                return eng.tensor_mul(lcl[:], pM[:, 256:384], mask3[:, 256:384])
            sstep("dve", masklcl)

    for h in range(H):
        for c in range(NCH):
            prep_chunk(h, c, "A")
            cb = CH * c
            fidx = h * NCH + c
            if c == 0:
                def umm(eng, fidx=fidx):
                    return eng.matmul(pU[:], Fst[:, fidx, :], nkvb[:], start=True, stop=True)
            else:
                def umm(eng, fidx=fidx):
                    eng.matmul(pU[:], ngs[:, 0:128], Gs[:, 0, :], start=True, stop=False)
                    eng.matmul(pU[:], ngs[:, 128:256], Gs[:, 1, :], start=False, stop=False)
                    return eng.matmul(pU[:, 256:512], Fst[:, fidx, :], nkvb[:, 256:512],
                                      start=False, stop=True)
            sstep("pe", umm)
            sstep("dve", lambda eng: eng.tensor_copy(us[:], pU[:]))
            if c == 0:
                def gmm(eng):
                    ins = None
                    for t in range(2):
                        eng.matmul(pG[:, t, :], ktm[:, 128 * t:128 * t + 128], us[:], start=True, stop=False)
                        ins = eng.matmul(pG[:, t, :], eye16[:], g0s[:, t, :], start=False, stop=True)
                    return ins
            else:
                def gmm(eng):
                    ins = None
                    for t in range(2):
                        ins = eng.matmul(pG[:, t, :], ktm[:, 128 * t:128 * t + 128], us[:],
                                         start=False, stop=(c == NCH - 1))
                    return ins
            sstep("pe", gmm, extra=[pref_tok] if c == 0 else [])
            if c < NCH - 1:
                sstep("dve", lambda eng: eng.tensor_copy(Gs[:], pG[:]))
            else:
                sstep("dve", lambda eng: eng.tensor_copy(goutf[:], pG[:]))
                def st_g(eng, h=h):
                    return eng.dma_start(
                        out=dram["cc_g_in"].rearrange("(a p) n -> p a n", p=128)[:, 2 * h:2 * h + 2, :],
                        in_=goutf[:])
                sdma("pool", st_g)

    def cc_fn(eng):
        return eng.collective_compute(
            "AllGather", ALU.bypass, replica_groups=[list(range(8))],
            ins=[dram["cc_g_in"].opt()], outs=[dram["cc_g_out"].opt()])
    nonlocal_ch = None
    ch = p.on("pool", cc_fn, waits=[ch], inc="cc", amt=1)

    gv = dram["cc_g_out"].rearrange("(r a p) n -> p r a n", p=128, r=8)
    for j in range(7):
        for h in range(H):
            def ld_g(eng, j=j, h=h):
                return eng.dma_start(out=gpref[:], in_=gv[:, j, 2 * h:2 * h + 2, :])
            sdma("pool", ld_g)
            if j == 0:
                def mk_s0(eng):
                    return eng.tensor_scalar(out=Sinit[:, 2 * h:2 * h + 2, :] if False else Sinit[:, 2 * h:2 * h + 2, :],
                                             in0=gpref[:, :, 256:512], scalar1=s_pref[:, 0:1],
                                             scalar2=None, op0=ALU.mult)
                def mk_s0b(eng, h=h):
                    return eng.tensor_scalar(out=Sinit[:, 2 * h:2 * h + 2, :], in0=gpref[:, :, 256:512],
                                             scalar1=s_pref[:, 0:1], scalar2=None, op0=ALU.mult)
                sstep("dve", mk_s0b)
            else:
                def mk_gj(eng, j=j):
                    eng.tensor_scalar(out=gj16[:], in0=gpref[:], scalar1=s_pref[:, j:j + 1],
                                      scalar2=None, op0=ALU.mult)
                    ins = None
                    for t in range(2):
                        ins = eng.scalar_tensor_tensor(
                            out=gj16[:, t, 128 * t:128 * t + 128], in0=eyef[:],
                            scalar=s_pref[:, 8 + j:8 + j + 1], in1=gj16[:, t, 128 * t:128 * t + 128],
                            op0=ALU.mult, op1=ALU.add)
                    return ins
                sstep("dve", mk_gj)
                def tr_m(eng):
                    ins = None
                    for a2 in range(2):
                        for b2 in range(2):
                            ins = eng.transpose(pT[:, 128 * a2:128 * a2 + 128] if b2 == 0 else pT[:, 128 * a2:128 * a2 + 128],
                                                gj16[:, a2, 128 * b2:128 * b2 + 128], eye16[:])
                    return ins
                def tr_m2(eng):
                    ins = None
                    for b2 in range(2):
                        for a2 in range(2):
                            ins = eng.transpose(pT[:, 128 * a2:128 * a2 + 128],
                                                gj16[:, a2, 128 * b2:128 * b2 + 128], eye16[:])
                    return ins
                for b2 in range(2):
                    def tr_b(eng, b2=b2):
                        ins = None
                        for a2 in range(2):
                            ins = eng.transpose(pT[:, 128 * a2:128 * a2 + 128],
                                                gj16[:, a2, 128 * b2:128 * b2 + 128], eye16[:])
                        return ins
                    sstep("pe", tr_b)
                    def ev_b(eng, b2=b2):
                        return eng.tensor_copy(mjt[:, b2, :], pT[:])
                    sstep("dve", ev_b)
                def smm(eng, h=h):
                    ins = None
                    for a2 in range(2):
                        for b2 in range(2):
                            eng.matmul(pG[:, a2, 0:256], mjt[:, b2, 128 * a2:128 * a2 + 128],
                                       Sinit[:, 2 * h + b2, :], start=(b2 == 0), stop=False)
                        ins = eng.matmul(pG[:, a2, 0:256], eye16[:], gj16[:, a2, 256:512],
                                         start=False, stop=True)
                    return ins
                sstep("pe", smm)
                def ev_s(eng, h=h):
                    return eng.tensor_copy(Sinit[:, 2 * h:2 * h + 2, :], pG[:, :, 0:256])
                sstep("dve", ev_s)

    for h in range(H):
        for c in range(NCH):
            prep_chunk(h, c, "B")
            cb = CH * c
            fidx = h * NCH + c
            sread = Sinit[:, 2 * h:2 * h + 2, :] if c == 0 else Ss
            if c == 0:
                def sinitmm(eng, h=h):
                    ins = None
                    for t in range(2):
                        ins = eng.matmul(pG[:, t, 0:256], eye16[:], Sinit[:, 2 * h + t, :],
                                         start=True, stop=False)
                    return ins
                sstep("pe", sinitmm)
            def ummb(eng, fidx=fidx, sread=sread):
                eng.matmul(pU[:, 0:256], Fst[:, fidx, :], nkvb[:, 256:512], start=True, stop=False)
                eng.matmul(pU[:, 0:256], ngs[:, 0:128], sread[:, 0, :], start=False, stop=False)
                return eng.matmul(pU[:, 0:256], ngs[:, 128:256], sread[:, 1, :], start=False, stop=True)
            sstep("pe", ummb)
            sstep("dve", lambda eng: eng.tensor_copy(u16s[:], pU[:, 0:256]))
            def omm(eng, h=h, cb=cb, sread=sread):
                eng.matmul(pU[:, 256:512], q16[:, 2 * h, cb:cb + 128], sread[:, 0, :], start=True, stop=False)
                eng.matmul(pU[:, 256:512], q16[:, 2 * h + 1, cb:cb + 128], sread[:, 1, :], start=False, stop=False)
                return eng.matmul(pU[:, 256:512], lcl[:], u16s[:], start=False, stop=True)
            sstep("pe", omm)
            sstep("act", lambda eng: eng.copy(otm[:], pU[:, 256:512]))
            def smm2(eng):
                ins = None
                for t in range(2):
                    ins = eng.matmul(pG[:, t, 0:256], ktm[:, 128 * t:128 * t + 128], u16s[:],
                                     start=False, stop=False)
                return ins
            sstep("pe", smm2)
            sstep("dve", lambda eng: eng.tensor_copy(Ss[:], pG[:, :, 0:256]))
            def tr_o(eng):
                ins = None
                for t in range(2):
                    ins = eng.transpose(pT[:, 128 * t:128 * t + 128], otm[:, 128 * t:128 * t + 128], eye16[:])
                return ins
            sstep("pe", tr_o)
            def ev_o(eng, h=h, cb=cb):
                return eng.tensor_copy(o16[:, 2 * h:2 * h + 2, cb:cb + 128],
                                       pT[:].rearrange("p (a b) -> p a b", a=2))
            sstep("dve", ev_o)

    st["delta_done"] = ch
    emit_debug(st, dict(o16=o16, sinit=Sinit, fst=Fst))
    if "gout" in st["dbg"]:
        def store_g(eng):
            return eng.dma_start(out=st["dbg"]["gout"], in_=st["dram"]["cc_g_out"])
        p.dma("pool", store_g, waits=[ch], inc="st")
    for nm in ("pR", "pT", "pNG", "pF", "pM", "pU", "pG"):
        sfree(nm)
    for nm in ("g0s", "otm", "u16s", "Ss", "Sinit", "mjt", "gj16", "gpref", "s_pref",
               "goutf", "Gs", "us", "lcl", "ngs", "PBf", "BALf", "BAL", "bbc", "brow", "kbf",
               "nkvb", "vtm", "ktm", "Fst"):
        sfree(nm)
    sfree("hs16") if False else None
    sfree("beta_tm", "k16", "q16")


def phase15_gw1(st):
    nc, p, din, dram = st["nc"], st["p"], st["din"], st["dram"]
    salloc, sfree, palloc = st["salloc"], st["sfree"], st["palloc"]
    gwb = salloc("gwb", [128, 2, GH], F32)
    gsl = salloc("gsl", [128, 20, 10], F32)
    gred = salloc("gred", [10, GH], F32)
    pRD = palloc("pRD", [10, 4, 512], F32)
    ch = None
    def sstep(eng, fn, extra=()):
        nonlocal ch
        ch = p.on(eng, fn, waits=([ch] if ch else []) + list(extra), inc=eng)
        return ch
    def sdma(eng, fn):
        nonlocal ch
        ch = p.dma(eng, fn, waits=[ch] if ch else [], inc="pld" if eng == "pool" else "ld3")
        return ch
    sdma("sp", lambda e: e.dma_start(out=gsl[:], in_=din["gsel"]))
    gv = din["gw1_slice"].rearrange("(a p) n -> p a n", p=128)
    for kt in range(20):
        bi = kt % 2
        def ld(e, kt=kt, bi=bi):
            return e.dma_start(out=gwb[:, bi, :], in_=gv[:, kt, :])
        sdma("sp", ld)
        def mm(e, kt=kt, bi=bi):
            ins = None
            for nb in range(4):
                ins = e.matmul(pRD[:, nb, :], gsl[:, kt, :], gwb[:, bi, 512 * nb:512 * nb + 512],
                               start=(kt == 0), stop=(kt == 19))
            return ins
        sstep("pe", mm)
    sstep("dve", lambda e: e.tensor_copy(gred[:], pRD[:].rearrange("p a b -> p (a b)")))
    if "gredump" in st["dbg"]:
        def dgr(e):
            return e.dma_start(out=st["dbg"]["gredump"], in_=gred[:])
        ch = p.dma("pool", dgr, waits=[ch], inc="st")
    sdma("pool", lambda e: e.dma_start(out=dram["cc_gw_in"], in_=gred[:]))
    def cc_fn(e):
        return e.collective_compute(
            "AllGather", ALU.bypass, replica_groups=[list(range(8))],
            ins=[dram["cc_gw_in"].opt()], outs=[dram["cc_gw_out"].opt()])
    ch = p.on("pool", cc_fn, waits=[ch], inc="cc", amt=1)
    st["gw1_done"] = ch
    sfree("pRD")
    sfree("gred", "gsl", "gwb")


def phase3_gate(st):
    nc, p, din, dram = st["nc"], st["p"], st["din"], st["dram"]
    salloc, sfree, palloc = st["salloc"], st["sfree"], st["palloc"]
    v16, o16 = st["v16"], st["o16"]
    eye16, eyef, ones1f, onescf = st["s_eye16"], st["s_eyef"], st["s_ones1f"], st["s_onescf"]

    rows = salloc("rows", [1, 128], F32)
    tmp512 = salloc("tmp512", [128, 512], F32)
    rowbc = salloc("rowbc", [128, 2, 512], F32)
    pacc = salloc("pacc", [128, KT], F32)
    firs = salloc("firs", [128, KT, 3 * 512], BF16)
    stats80 = salloc("stats80", [128, 512], F32)
    fw = salloc("fw", [128, KT, 82], F32)
    sel = salloc("sel", [128, 20, 80], BF16)
    s_sel20 = salloc("s_sel20", [20, 20, 128], F32)
    hsst = salloc("hsst", [128, KT, 512], F32)
    hs16st = salloc("hs16st", [128, KT, 512], BF16)
    sgw = salloc("sgw", [128, 9, 128], BF16)
    wo16 = salloc("wo16", [128, KT, D], BF16)
    hgate = salloc("hgate", [128, 16, 512], BF16)
    big512 = salloc("big512", [128, KT, 512], F32)
    sgwf = big512[:].rearrange("p a n -> p (a n)")[:, 0:1152].rearrange("p (a n) -> p a n", n=128)
    onrm = salloc("onrm", [128, KT, 512], BF16)
    gw2s = salloc("gw2s", [128, 16, 20], BF16)
    gb1s = salloc("gb1s", [128, 16], F32)
    normw = salloc("normw", [128, 2], F32)
    ltm = salloc("ltm", [128, 4, 20], F32)
    mx5 = salloc("mx5", [128, 4, 8], F32)
    prow = salloc("prow", [20, 512], F32)
    pb16 = salloc("pb16", [128, 512], F32)
    epsbc = salloc("epsbc", [128, 40], F32)
    sq512 = salloc("sq512", [128, 512], BF16)
    stats16 = salloc("stats16", [128, 512], BF16)
    gb2_16 = salloc("gb2_16", [1, 20], BF16)

    pA = palloc("pA", [128, 512], F32)
    pB = palloc("pB", [128, 512], F32)
    pC = palloc("pC", [128, 512], F32)
    pD = palloc("pD", [128, 64], F32)
    pE1 = palloc("pE1", [1, 512], F32)
    pE2 = palloc("pE2", [1, 512], F32)
    st["p3"] = dict(pA=pA, pB=pB, pC=pC, pD=pD, pE1=pE1, pE2=pE2, rowbc=rowbc,
                    tmp512=tmp512, big512=big512, sq512=sq512, pacc=pacc, rows=rows,
                    hsst=hsst, hs16st=hs16st)

    ch = st["delta_done"]
    def sstep(eng, fn, extra=()):
        nonlocal ch
        ch = p.on(eng, fn, waits=[ch] + list(extra), inc=eng)
        return ch
    def sdma(eng, fn):
        nonlocal ch
        ch = p.dma(eng, fn, waits=[ch], inc="ld3")
        return ch

    def ld1(e):
        sem = p.sems["ld3"]
        e.dma_start(out=fw[:, :, 0:3], in_=din["fir_s"].rearrange("(a p) j -> p a j", p=128)).then_inc(sem, 16)
        e.dma_start(out=fw[:, :, 3:18], in_=din["fir_m"].rearrange("(a p) j -> p a j", p=128)).then_inc(sem, 16)
        e.dma_start(out=fw[:, :, 18:82], in_=din["fir_l"].rearrange("(a p) j -> p a j", p=128)).then_inc(sem, 16)
        e.dma_start(out=s_sel20[:], in_=din["sel20"]).then_inc(sem, 16)
        e.dma_start(out=gb1s[:], in_=din["gb1"]).then_inc(sem, 16)
        e.dma_start(out=normw[:], in_=din["norm_w"]).then_inc(sem, 16)
        e.dma_start(out=rows[:, 0:4], in_=din["log_tau"].rearrange("(o h) -> o h", o=1)).then_inc(sem, 16)
        e.dma_start(out=rows[:, 8:28], in_=din["eps_logit"].rearrange("(o h) p -> o (h p)", o=1)).then_inc(sem, 16)
        e.dma_start(out=rows[:, 72:92], in_=din["gb2"].rearrange("(o j) -> o j", o=1)).then_inc(sem, 16)
        return None
    p.on("sp", ld1, waits=[ch])
    ch = p.bump("ld3", 16 * 9)
    for m in range(KT):
        def ldw(e, m=m):
            return e.dma_start(out=big512[:, :, 0:128],
                               in_=din["wo"].rearrange("(a p) n -> p a n", p=128)[:, :, 128 * m:128 * (m + 1)])
        sdma("sp", ldw)
        def cst(e, m=m):
            return e.tensor_copy(wo16[:, :, 128 * m:128 * (m + 1)], big512[:, :, 0:128])
        sstep("dve", cst)
    gw2stage = big512[:].rearrange("p a n -> p (a n)")[:, 0:320].rearrange("p (a j) -> p a j", j=20)
    sdma("sp", lambda e: e.dma_start(out=gw2stage, in_=din["gw2"].rearrange("(a p) j -> p a j", p=128)))
    sstep("dve", lambda e: e.tensor_copy(gw2s[:], gw2stage))

    sstep("dve", lambda e: e.tensor_scalar(out=rows[:, 4:8], in0=rows[:, 0:4], scalar1=-1.0,
                                           scalar2=None, op0=ALU.mult))
    sstep("dve", lambda e: e.tensor_tensor(out=rows[:, 4:8], in0=rows[:, 0:4], in1=rows[:, 4:8], op=ALU.max))
    sstep("dve", lambda e: e.tensor_single_scalar(out=rows[:, 0:4], in_=rows[:, 0:4], scalar=0.0, op=ALU.max))
    sstep("act", lambda e: e.activation(out=rows[:, 4:8], in_=rows[:, 4:8], func=AF.Exp, scale=-1.0))
    sstep("act", lambda e: e.activation(out=rows[:, 4:8], in_=rows[:, 4:8], func=AF.Ln, bias=1.0))
    sstep("dve", lambda e: e.tensor_tensor(out=rows[:, 0:4], in0=rows[:, 0:4], in1=rows[:, 4:8], op=ALU.add))
    sstep("dve", lambda e: e.tensor_scalar(out=rows[:, 0:4], in0=rows[:, 0:4], scalar1=0.5,
                                           scalar2=None, op0=ALU.add))
    sstep("dve", lambda e: e.reciprocal(rows[:, 0:4], rows[:, 0:4]))
    sstep("act", lambda e: e.activation(out=rows[:, 8:28], in_=rows[:, 8:28], func=AF.Sigmoid))
    sstep("dve", lambda e: e.tensor_scalar(out=rows[:, 8:28], in0=rows[:, 8:28], scalar1=0.2,
                                           scalar2=None, op0=ALU.mult))
    sstep("dve", lambda e: e.reduce_sum(rows[:, 28:32],
                                        rows[:, 8:28].rearrange("p (h q) -> p h q", h=4), axis=AXL.X))
    sstep("dve", lambda e: e.tensor_scalar(out=rows[:, 28:32], in0=rows[:, 28:32], scalar1=-1.0,
                                           scalar2=1.0, op0=ALU.mult, op1=ALU.add))
    def eps_fin(e):
        ins = None
        for q in range(5):
            ins = e.tensor_copy(rows[:, 32:52].rearrange("p (h q) -> p h q", q=5)[:, :, q], rows[:, 28:32])
        return ins
    sstep("dve", eps_fin)
    def itau_exp(e):
        ins = None
        for q in range(5):
            ins = e.tensor_copy(rows[:, 52:72].rearrange("p (h q) -> p h q", q=5)[:, :, q], rows[:, 0:4])
        return ins
    sstep("dve", itau_exp)
    sstep("pe", lambda e: e.matmul(pD[:, 0:20], ones1f[:], rows[:, 52:72], start=True, stop=True))
    sstep("dve", lambda e: e.tensor_copy(epsbc[:, 0:20], pD[:, 0:20]))
    def gw2_scale(e):
        ins = None
        for kt in range(16):
            ins = e.tensor_tensor(out=gw2s[:, kt, :], in0=gw2s[:, kt, :], in1=epsbc[:, 0:20], op=ALU.mult)
        return ins
    sstep("dve", gw2_scale)
    sstep("dve", lambda e: e.tensor_tensor(out=rows[:, 72:92], in0=rows[:, 72:92], in1=rows[:, 52:72], op=ALU.mult))
    sstep("dve", lambda e: e.tensor_copy(gb2_16[:], rows[:, 72:92]))
    sstep("pe", lambda e: e.matmul(pD[:, 0:20], ones1f[:], rows[:, 32:52], start=True, stop=True))
    sstep("dve", lambda e: e.tensor_copy(epsbc[:, 0:20], pD[:, 0:20]))
    sstep("pe", lambda e: e.matmul(pD[:, 0:20], ones1f[:], rows[:, 8:28], start=True, stop=True))
    sstep("dve", lambda e: e.tensor_copy(epsbc[:, 20:40], pD[:, 0:20]))
    sstep("dve", lambda e: e.memset(stats80[:], 0.0))
    sstep("dve", lambda e: e.memset(stats16[:], 0.0))
    if "gwout" in st["dbg"]:
        def dgw(e):
            return e.dma_start(out=st["dbg"]["gwout"], in_=dram["cc_gw_out"])
        ch = p.dma("pool", dgw, waits=[ch], inc="st")
    if "rowsdump" in st["dbg"]:
        def drows(e):
            return e.dma_start(out=st["dbg"]["rowsdump"], in_=rows[:])
        ch = p.dma("pool", drows, waits=[ch], inc="st")
    if "epsbcdump" in st["dbg"]:
        def depsbc(e):
            return e.dma_start(out=st["dbg"]["epsbcdump"], in_=epsbc[:])
        ch = p.dma("pool", depsbc, waits=[ch], inc="st")
    if "gw2dump" in st["dbg"]:
        def dgw2(e):
            return e.dma_start(out=st["dbg"]["gw2dump"], in_=gw2s[:])
        ch = p.dma("pool", dgw2, waits=[ch], inc="st")

    gveff = dram["cc_gw_out"].rearrange("(pp hh ss) n -> ss (pp hh) n", pp=5, hh=4)
    SBLK = [1, 2, 3, 0]
    FIRK = [(0, 3, 0), (1, 15, 3), (2, 64, 18)]
    def path_tile(pth, gg, t0):
        if pth < 3:
            return firs[:, gg, 512 * pth:512 * pth + 512]
        if pth == 3:
            return o16[:, gg, t0:t0 + ST]
        return v16[:, gg, 63 + t0:63 + t0 + ST]

    for s in range(NST):
        t0 = ST * s
        sdma("sp", lambda e, t0=t0: e.dma_start(
            out=hsst[:], in_=din["hsT"].rearrange("(a p) t -> p a t", p=128)[:, :, HALO + t0:HALO + t0 + ST]))
        sstep("dve", lambda e: e.tensor_copy(hs16st[:], hsst[:]))
        for (pth, KW, off) in FIRK:
            for gg in range(KT):
                def fir(e, pth=pth, KW=KW, off=off, gg=gg, t0=t0):
                    base = 63 + t0 - (KW - 1)
                    e.tensor_scalar(out=tmp512[:], in0=v16[:, gg, base:base + ST],
                                    scalar1=fw[:, gg, off:off + 1], scalar2=None, op0=ALU.mult)
                    for j in range(1, KW):
                        e.scalar_tensor_tensor(
                            out=tmp512[:], in0=v16[:, gg, base + j:base + j + ST],
                            scalar=fw[:, gg, off + j:off + j + 1], in1=tmp512[:],
                            op0=ALU.mult, op1=ALU.add)
                    return e.tensor_copy(firs[:, gg, 512 * pth:512 * pth + 512], tmp512[:])
                sstep("dve", fir)
        passes = [("sel_m", pB, None), ("sel_v", pA, "sq"), ("sel_a", pC, "abs")]
        for (selname, ppsum, xform) in passes:
            sdma("sp", lambda e, selname=selname: e.dma_start(out=sel[:], in_=din[selname]))
            for pth in range(P5):
                for gg in range(KT):
                    kt = pth * 8 + gg
                    if xform == "sq":
                        def xf(e, pth=pth, gg=gg, t0=t0):
                            tl = path_tile(pth, gg, t0)
                            return e.tensor_mul(sq512[:], tl, tl)
                        sstep("dve", xf)
                        rhs_t = sq512
                    elif xform == "abs":
                        def xf(e, pth=pth, gg=gg, t0=t0):
                            tl = path_tile(pth, gg, t0)
                            e.tensor_scalar(out=sq512[:], in0=tl, scalar1=-1.0, scalar2=None, op0=ALU.mult)
                            return e.tensor_tensor(out=sq512[:], in0=sq512[:], in1=tl, op=ALU.max)
                        sstep("dve", xf)
                        rhs_t = sq512
                    else:
                        rhs_t = None
                    def mmp(e, pth=pth, gg=gg, kt=kt, t0=t0, ppsum=ppsum, rhs_t=rhs_t):
                        rhs = rhs_t[:] if rhs_t is not None else path_tile(pth, gg, t0)
                        return e.matmul(ppsum[0:80, :], sel[:, pth * 4 + gg // 2, :], rhs,
                                        start=(kt == 0), stop=(kt == 39))
                    sstep("pe", mmp)
        sstep("dve", lambda e: e.tensor_copy(stats80[0:80, :], pA[0:80, :]))
        sstep("dve", lambda e: e.tensor_tensor(out=stats80[0:80, :], in0=stats80[0:80, :],
                                               in1=pB[0:80, :], op=ALU.add))
        sstep("dve", lambda e: e.tensor_tensor(out=stats80[0:80, :], in0=stats80[0:80, :],
                                               in1=pC[0:80, :], op=ALU.add))
        sstep("act", lambda e: e.activation(out=stats80[0:20, :], in_=stats80[0:20, :], func=AF.Sqrt))
        sstep("dve", lambda e: e.tensor_copy(stats16[0:80, :], stats80[0:80, :]))
        if "statsdump" in st["dbg"]:
            def dstat(e, s=s):
                return e.dma_start(out=st["dbg"]["statsdump"][s], in_=stats80[:])
            ch = p.dma("pool", dstat, waits=[ch], inc="st")
        for m in range(16):
            def ldm2(e, m=m):
                sem = p.sems["ld3"]
                e.dma_start(out=sgwf[:, 0:8, :],
                            in_=din["gw1_hs"].rearrange("(a p) n -> p a n", p=128)[:, :, 128 * m:128 * (m + 1)]).then_inc(sem, 16)
                for sv in range(4):
                    blk = SBLK[sv]
                    e.dma_start(out=sgwf[20 * blk:20 * blk + 20, 8, :],
                                in_=gveff[sv, :, 128 * m:128 * (m + 1)]).then_inc(sem, 16)
                return None
            p.on("sp", ldm2, waits=[ch])
            ch = p.bump("ld3", 16 * 5)
            sstep("dve", lambda e: e.tensor_copy(sgw[:], sgwf[:]))
            def gmm(e, m=m, t0=t0):
                for kt in range(KT):
                    e.matmul(pA[:], sgw[:, kt, :], hs16st[:, kt, :], start=(kt == 0), stop=False)
                return e.matmul(pA[:], sgw[:, 8, :], stats16[:], start=False, stop=True)
            sstep("pe", gmm)
            sstep("act", lambda e, m=m: e.activation(out=hgate[:, m, :], in_=pA[:], func=AF.Gelu,
                                                     bias=gb1s[:, m:m + 1]))
        for tt in range(4):
            def lmm(e, tt=tt):
                for kt in range(16):
                    e.matmul(pD[:, 0:20], hgate[:, kt, 128 * tt:128 * tt + 128], gw2s[:, kt, :],
                             start=(kt == 0), stop=False)
                return e.matmul(pD[:, 0:20], st["s_ones1"][:], gb2_16[:], start=False, stop=True)
            sstep("pe", lmm)
            sstep("dve", lambda e, tt=tt: e.tensor_copy(ltm[:, tt, :], pD[:, 0:20]))
        if "logitsdump" in st["dbg"]:
            def dlog(e, s=s):
                return e.dma_start(out=st["dbg"]["logitsdump"][s], in_=ltm[:])
            ch = p.dma("pool", dlog, waits=[ch], inc="st")
        if True:
            pass
        mxv = mx5[:].rearrange("p a b -> p (a b)")
        def smax1(e):
            return e.reduce_max(mxv[:, 0:16].rearrange("p (a h) -> p a h", a=4),
                                ltm[:].rearrange("p a (h q) -> p a h q", q=5), axis=AXL.X)
        sstep("dve", smax1)
        def smax2(e):
            ins = None
            lv = ltm[:].rearrange("p a (h q) -> p a h q", q=5)
            mv = mxv[:, 0:16].rearrange("p (a h) -> p a h", a=4)
            for q in range(5):
                ins = e.tensor_tensor(out=lv[:, :, :, q], in0=lv[:, :, :, q], in1=mv, op=ALU.subtract)
            return ins
        sstep("dve", smax2)
        sstep("act", lambda e: e.activation(out=ltm[:], in_=ltm[:], func=AF.Exp))
        def ssum(e):
            return e.reduce_sum(mxv[:, 16:32].rearrange("p (a h) -> p a h", a=4),
                                ltm[:].rearrange("p a (h q) -> p a h q", q=5), axis=AXL.X)
        sstep("dve", ssum)
        sstep("dve", lambda e: e.reciprocal(mxv[:, 16:32], mxv[:, 16:32]))
        def pnorm(e):
            ins = None
            lv = ltm[:].rearrange("p a (h q) -> p a h q", q=5)
            mv = mxv[:, 16:32].rearrange("p (a h) -> p a h", a=4)
            for q in range(5):
                ins = e.tensor_tensor(out=lv[:, :, :, q], in0=lv[:, :, :, q], in1=mv, op=ALU.mult)
            return ins
        sstep("dve", pnorm)
        def peps1(e):
            ins = None
            for tt in range(4):
                ins = e.tensor_tensor(out=ltm[:, tt, :], in0=ltm[:, tt, :], in1=epsbc[:, 0:20], op=ALU.mult)
            return ins
        sstep("dve", peps1)
        def peps2(e):
            ins = None
            for tt in range(4):
                ins = e.tensor_tensor(out=ltm[:, tt, :], in0=ltm[:, tt, :], in1=epsbc[:, 20:40], op=ALU.add)
            return ins
        sstep("dve", peps2)
        if "probs" in st["dbg"]:
            def dprobs(e, s=s):
                return e.dma_start(out=st["dbg"]["probs"].rearrange("a p b c -> a p b c")[s], in_=ltm[:])
            ch = p.dma("pool", dprobs, waits=[ch], inc="st")
        sstep("dve", lambda e: e.tensor_copy(tmp512[:, 0:80], ltm[:].rearrange("p a j -> p (a j)")))
        for tt in range(4):
            def trp(e, tt=tt):
                return e.transpose(pC[0:20, 128 * tt:128 * tt + 128], tmp512[:, 20 * tt:20 * tt + 20], eyef[:])
            sstep("pe", trp)
        sstep("dve", lambda e: e.tensor_copy(prow[:], pC[0:20, :]))
        for h in range(H):
            for pth in range(P5):
                j = h * P5 + pth
                def bcp(e, j=j):
                    return e.matmul(pB[:], s_sel20[:, j, :], prow[:], start=True, stop=True)
                sstep("pe", bcp)
                sstep("dve", lambda e: e.tensor_copy(pb16[:], pB[:]))
                for t2 in range(2):
                    gg = 2 * h + t2
                    def comb(e, pth=pth, gg=gg, t0=t0, first=(pth == 0)):
                        tl = path_tile(pth, gg, t0)
                        if first:
                            return e.tensor_tensor(out=big512[:, gg, :], in0=tl, in1=pb16[:], op=ALU.mult)
                        e.tensor_tensor(out=tmp512[:], in0=tl, in1=pb16[:], op=ALU.mult)
                        return e.tensor_tensor(out=big512[:, gg, :], in0=big512[:, gg, :],
                                               in1=tmp512[:], op=ALU.add)
                    sstep("dve", comb)
        if "ocomb" in st["dbg"]:
            def docomb(e, s=s):
                return e.dma_start(out=st["dbg"]["ocomb"][s], in_=big512[:])
            ch = p.dma("pool", docomb, waits=[ch], inc="st")
        for h in range(H):
            for t2 in range(2):
                gg = 2 * h + t2
                sstep("dve", lambda e, gg=gg: e.tensor_mul(tmp512[:], big512[:, gg, :], big512[:, gg, :]))
                sstep("pe", lambda e, t2=t2: e.matmul(pE1[:], onescf[:], tmp512[:],
                                                      start=(t2 == 0), stop=(t2 == 1)))
            sstep("act", lambda e: e.activation(out=tmp512[0:1, :], in_=pE1[:], func=AF.Sqrt,
                                                scale=1.0 / 256.0, bias=1e-5))
            sstep("dve", lambda e: e.reciprocal(tmp512[0:1, :], tmp512[0:1, :]))
            sstep("pe", lambda e: e.matmul(pB[:], ones1f[:], tmp512[0:1, :], start=True, stop=True))
            sstep("dve", lambda e: e.tensor_copy(rowbc[:, 0, :], pB[:]))
            for t2 in range(2):
                gg = 2 * h + t2
                def appl(e, gg=gg, t2=t2):
                    return e.scalar_tensor_tensor(out=onrm[:, gg, :], in0=big512[:, gg, :],
                                                  scalar=normw[:, t2:t2 + 1], in1=rowbc[:, 0, :],
                                                  op0=ALU.mult, op1=ALU.mult)
                sstep("dve", appl)
        for m in range(KT):
            def womm(e, m=m):
                ins = None
                for kt in range(KT):
                    ins = e.matmul(pA[:], wo16[:, kt, 128 * m:128 * (m + 1)], onrm[:, kt, :],
                                   start=(kt == 0), stop=(kt == KT - 1))
                return ins
            sstep("pe", womm)
            sstep("dve", lambda e, m=m: e.tensor_tensor(out=big512[:, m, :], in0=pA[:],
                                                        in1=hsst[:, m, :], op=ALU.add))
        ln_emit(st, sstep, big512)
        ch = st["_ln_ch"]
        def macc(e, s=s):
            ins = None
            dst = pacc if s == 0 else tmp512[:, 0:KT].rearrange("p (a o) -> p a o", o=1)
            for gg in range(KT):
                if s == 0:
                    ins = e.reduce_sum(pacc[:, gg:gg + 1], big512[:, gg, :], axis=AXL.X)
                else:
                    ins = e.reduce_sum(tmp512[:, gg:gg + 1], big512[:, gg, :], axis=AXL.X)
            return ins
        sstep("dve", macc)
        if s > 0:
            sstep("dve", lambda e: e.tensor_tensor(out=pacc[:], in0=pacc[:], in1=tmp512[:, 0:KT], op=ALU.add))
        sdma("pool", lambda e, t0=t0: e.dma_start(
            out=dram["x1_dram"].rearrange("(a p) t -> p a t", p=128)[:, :, t0:t0 + ST], in_=big512[:]))

    emit_debug(st, dict(stats80=stats80, hgate=hgate))
    if "x1dump" in st["dbg"]:
        def stx1(eng):
            return eng.dma_start(out=st["dbg"]["x1dump"], in_=dram["x1_dram"])
        ch = p.dma("pool", stx1, waits=[ch], inc="st")
    st["chain3"] = ch
    for nm in ("pE2", "pE1", "pD", "pC", "pB", "pA"):
        sfree(nm)
    for nm in ("gb2_16", "stats16", "sq512", "epsbc", "pb16", "prow", "mx5", "ltm", "normw", "gb1s",
               "gw2s", "onrm", "big512", "hgate", "wo16", "sgw", "hs16st", "hsst",
               "s_sel20", "sel", "fw", "stats80", "firs"):
        sfree(nm)


def ln_emit(st, sstep, xbuf):
    p = st["p"]
    d = st["p3"]
    pE1, pE2, tmp512, rowbc = d["pE1"], d["pE2"], d["tmp512"], d["rowbc"]
    onescf, ones1f = st["s_onescf"], st["s_ones1f"]
    for gg in range(KT):
        sstep("pe", lambda e, gg=gg: e.matmul(pE1[:], onescf[:], xbuf[:, gg, :],
                                              start=(gg == 0), stop=(gg == KT - 1)))
    for gg in range(KT):
        sstep("dve", lambda e, gg=gg: e.tensor_mul(tmp512[:], xbuf[:, gg, :], xbuf[:, gg, :]))
        sstep("pe", lambda e, gg=gg: e.matmul(pE2[:], onescf[:], tmp512[:],
                                              start=(gg == 0), stop=(gg == KT - 1)))
    sstep("dve", lambda e: e.tensor_scalar(out=tmp512[0:1, :], in0=pE1[:], scalar1=1.0 / D,
                                           scalar2=None, op0=ALU.mult))
    sstep("dve", lambda e: e.tensor_mul(rowbc[0:1, 1, :], tmp512[0:1, :], tmp512[0:1, :]))
    sstep("dve", lambda e: e.tensor_scalar(out=rowbc[0:1, 1, :], in0=rowbc[0:1, 1, :], scalar1=-1.0,
                                           scalar2=None, op0=ALU.mult))
    sstep("dve", lambda e: e.scalar_tensor_tensor(out=rowbc[0:1, 1, :], in0=pE2[:], scalar=1.0 / D,
                                                  in1=rowbc[0:1, 1, :], op0=ALU.mult, op1=ALU.add))
    sstep("act", lambda e: e.activation(out=rowbc[0:1, 1, :], in_=rowbc[0:1, 1, :], func=AF.Sqrt, bias=1e-5))
    sstep("dve", lambda e: e.reciprocal(rowbc[0:1, 1, :], rowbc[0:1, 1, :]))
    sstep("pe", lambda e: e.matmul(d["pB"][:], ones1f[:], tmp512[0:1, :], start=True, stop=True))
    sstep("dve", lambda e: e.tensor_copy(rowbc[:, 0, :], d["pB"][:]))
    sstep("pe", lambda e: e.matmul(d["pB"][:], ones1f[:], rowbc[0:1, 1, :], start=True, stop=True))
    sstep("dve", lambda e: e.tensor_copy(rowbc[:, 1, :], d["pB"][:]))
    for gg in range(KT):
        def appl(e, gg=gg):
            e.tensor_tensor(out=xbuf[:, gg, :], in0=xbuf[:, gg, :], in1=rowbc[:, 0, :], op=ALU.subtract)
            return e.tensor_tensor(out=xbuf[:, gg, :], in0=xbuf[:, gg, :], in1=rowbc[:, 1, :], op=ALU.mult)
        tok = sstep("dve", appl)
    st["_ln_ch"] = tok


def broadcast_pair(a, b):
    import concourse.bass as _b
    return _b.broadcast_tensor_aps(a, b)


def phase4_ffn(st):
    nc, p, din, dram, out = st["nc"], st["p"], st["din"], st["dram"], st["out"]
    salloc, sfree, palloc = st["salloc"], st["sfree"], st["palloc"]
    eyef, ones1f, onescf = st["s_eyef"], st["s_ones1f"], st["s_onescf"]
    d3 = st["p3"]
    rows, pacc, tmp512, rowbc, hsst = d3["rows"], d3["pacc"], d3["tmp512"], d3["rowbc"], d3["hsst"]

    ch = st["chain3"]
    def sstep(eng, fn, extra=()):
        nonlocal ch
        ch = p.on(eng, fn, waits=[ch] + list(extra), inc=eng)
        return ch
    def sdma(eng, fn):
        nonlocal ch
        ch = p.dma(eng, fn, waits=[ch], inc="pld" if eng == "pool" else "ld3")
        return ch

    w1s = salloc("w1s", [128, KT, D], BF16)
    w2s = salloc("w2s", [128, KT, 512], BF16)
    w3s = salloc("w3s", [128, 4, 256], BF16)
    w4s = salloc("w4s", [128, 2, 1], BF16)
    g1s = salloc("g1s", [128, KT, 512], BF16)
    g2s = salloc("g2s", [128, 4, 256], BF16)
    g3s = salloc("g3s", [128, 2, 1], BF16)
    bias = salloc("bias", [128, 16], F32)
    xs = salloc("xs", [128, KT, 512], F32)
    xs2 = salloc("xs2", [128, KT, 512], F32)
    xs16 = salloc("xs16", [128, KT, 512], BF16)
    t1 = salloc("t1", [128, KT, 512], BF16)
    t2b = salloc("t2b", [128, 4, 512], BF16)
    t3b = salloc("t3b", [128, 2, 512], BF16)
    gcx = salloc("gcx", [128, KT], BF16)
    gch = salloc("gch", [128, 4], BF16)
    obuf = salloc("obuf", [128, D], F32)

    pA = palloc("pA4", [128, 512], F32)
    pB = palloc("pB4", [128, 512], F32)
    pE1 = palloc("pE14", [1, 512], F32)
    pE2 = palloc("pE24", [1, 512], F32)
    pD = palloc("pD4", [128, 16], F32)
    st["p3"]["pA"] = pA
    st["p3"]["pB"] = pB
    st["p3"]["pE1"] = pE1
    st["p3"]["pE2"] = pE2

    def ldw(e):
        sem = p.sems["ld3"]
        e.dma_start(out=xs[:, :, 0:512].rearrange("p a n -> p (a n)")[:, 0:4096].rearrange("p (a n) -> p a n", a=KT),
                    in_=din["cn_w1"].rearrange("(a p) n -> p a n", p=128)[:, :, 0:512]).then_inc(sem, 16)
        return None
    def ld_piece(dst, src_name, cols, stage_cols):
        def f(e, dst=dst, cols=cols):
            sem = p.sems["ld3"]
            e.dma_start(out=xs[:, :, 0:cols[1] - cols[0]] if False else xs.ap() if False else xs[:, :, :],
                        in_=din[src_name]).then_inc(sem, 16)
            return None
        return f
    def l1(e):
        sem = p.sems["ld3"]
        e.dma_start(out=xs[:], in_=din["cn_w1"].rearrange("(a p) n -> p a n", p=128)[:, :, 0:512]).then_inc(sem, 16)
        return None
    p.on("sp", l1, waits=[ch]); ch = p.bump("ld3", 16)
    sstep("dve", lambda e: e.tensor_copy(w1s[:, :, 0:512], xs[:]))
    def l2(e):
        sem = p.sems["ld3"]
        e.dma_start(out=xs[:], in_=din["cn_w1"].rearrange("(a p) n -> p a n", p=128)[:, :, 512:1024]).then_inc(sem, 16)
        return None
    p.on("sp", l2, waits=[ch]); ch = p.bump("ld3", 16)
    sstep("dve", lambda e: e.tensor_copy(w1s[:, :, 512:1024], xs[:]))
    def l3(e):
        sem = p.sems["ld3"]
        e.dma_start(out=xs[:], in_=din["cn_w2"].rearrange("(a p) n -> p a n", p=128)).then_inc(sem, 16)
        return None
    p.on("sp", l3, waits=[ch]); ch = p.bump("ld3", 16)
    sstep("dve", lambda e: e.tensor_copy(w2s[:], xs[:]))
    def l4(e):
        sem = p.sems["ld3"]
        e.dma_start(out=xs[:, 0:4, 0:256], in_=din["cn_w3"].rearrange("(a p) n -> p a n", p=128)).then_inc(sem, 16)
        e.dma_start(out=xs[:, 4:6, 0:1], in_=din["cn_w4"].rearrange("(a p) n -> p a n", p=128),
                    allow_slow_non_contiguous=True).then_inc(sem, 16)
        e.dma_start(out=xs[:, 6:7, 0:512].rearrange("p a n -> p (a n)"), in_=din["gc_w1"].rearrange("(a p) n -> p a n", p=128).rearrange("p a n -> p (a n)") if False else din["gc_w1"].rearrange("(a p) n -> p (a n)", p=128)).then_inc(sem, 16)
        return None
    def l4b(e):
        sem = p.sems["ld3"]
        e.dma_start(out=xs[:, 0:4, 0:256], in_=din["cn_w3"].rearrange("(a p) n -> p a n", p=128)).then_inc(sem, 16)
        e.dma_start(out=xs[:, 4:6, 0:1], in_=din["cn_w4"].rearrange("(a p) n -> p a n", p=128),
                    allow_slow_non_contiguous=True).then_inc(sem, 16)
        return None
    p.on("sp", l4b, waits=[ch]); ch = p.bump("ld3", 32)
    sstep("dve", lambda e: e.tensor_copy(w3s[:], xs[:, 0:4, 0:256]))
    sstep("dve", lambda e: e.tensor_copy(w4s[:], xs[:, 4:6, 0:1]))
    def l5(e):
        sem = p.sems["ld3"]
        e.dma_start(out=xs[:, 0:8, 0:512], in_=din["gc_w1"].rearrange("(a p) n -> p a n", p=128)).then_inc(sem, 16)
        return None
    p.on("sp", l5, waits=[ch]); ch = p.bump("ld3", 16)
    sstep("dve", lambda e: e.tensor_copy(g1s[:], xs[:, 0:8, 0:512]))
    def l6(e):
        sem = p.sems["ld3"]
        e.dma_start(out=xs[:, 0:4, 0:256], in_=din["gc_w2"].rearrange("(a p) n -> p a n", p=128)).then_inc(sem, 16)
        e.dma_start(out=xs[:, 4:6, 0:1], in_=din["gc_w3"].rearrange("(a p) n -> p a n", p=128),
                    allow_slow_non_contiguous=True).then_inc(sem, 16)
        e.dma_start(out=xs[0:1, 6, 0:1], in_=din["adaptive_range"].rearrange("(o a) -> o a", o=1)).then_inc(sem, 16)
        e.dma_start(out=xs[:, 7, 0:8], in_=din["cn_b1"]).then_inc(sem, 16)
        e.dma_start(out=xs[:, 7, 8:12], in_=din["cn_b2"]).then_inc(sem, 16)
        e.dma_start(out=xs[:, 7, 12:14], in_=din["cn_b3"]).then_inc(sem, 16)
        e.dma_start(out=xs[0:1, 7, 14:15], in_=din["cn_b4"].rearrange("(o a) -> o a", o=1)).then_inc(sem, 16)
        e.dma_start(out=xs[:, 6, 8:12], in_=din["gc_b1"]).then_inc(sem, 16)
        e.dma_start(out=xs[:, 6, 12:14], in_=din["gc_b2"]).then_inc(sem, 16)
        e.dma_start(out=xs[0:1, 6, 15:16], in_=din["gc_b3"].rearrange("(o a) -> o a", o=1)).then_inc(sem, 16)
        return None
    p.on("sp", l6, waits=[ch]); ch = p.bump("ld3", 160)
    sstep("dve", lambda e: e.tensor_copy(g2s[:], xs[:, 0:4, 0:256]))
    sstep("dve", lambda e: e.tensor_copy(g3s[:], xs[:, 4:6, 0:1]))
    def cpb(e):
        e.tensor_copy(bias[:, 0:8], xs[:, 7, 0:8])
        e.tensor_copy(bias[:, 8:12], xs[:, 7, 8:12])
        e.tensor_copy(bias[:, 12:14], xs[:, 7, 12:14])
        e.tensor_copy(rows[:, 100:101], xs[0:1, 7, 14:15])
        e.tensor_copy(rows[:, 101:102], xs[0:1, 6, 15:16])
        e.tensor_copy(rows[:, 102:103], xs[0:1, 6, 0:1])
        e.tensor_copy(bias[:, 14:15].rearrange("p a -> p a") if False else rowbc[:, 0, 0:4], xs[:, 6, 8:12])
        return e.tensor_copy(rowbc[:, 1, 0:2], xs[:, 6, 12:14])
    sstep("dve", cpb)

    sdma("pool", lambda e: e.dma_start(out=dram["cc_mean_in"], in_=pacc[:]))
    def cc_fn(e):
        return e.collective_compute(
            "AllReduce", ALU.add, replica_groups=[list(range(8))],
            ins=[dram["cc_mean_in"].opt()], outs=[dram["cc_mean_out"].opt()])
    ch = p.on("pool", cc_fn, waits=[ch], inc="cc", amt=1)
    sdma("pool", lambda e: e.dma_start(out=pacc[:], in_=dram["cc_mean_out"]))
    sstep("dve", lambda e: e.tensor_scalar(out=gcx[:], in0=pacc[:], scalar1=1.0 / 16384.0,
                                           scalar2=None, op0=ALU.mult))
    def gc1(e):
        for m in range(4):
            for kt in range(KT):
                e.matmul(pD[:, m:m + 1], g1s[:, kt, 128 * m:128 * (m + 1)], gcx[:, kt:kt + 1],
                         start=(kt == 0), stop=(kt == KT - 1))
        return e.matmul(pD[:, 4:5], g1s[:, 0, 0:128], gcx[:, 0:1], start=True, stop=True)
    def gc1b(e):
        ins = None
        for m in range(4):
            for kt in range(KT):
                ins = e.matmul(pD[:, m:m + 1], g1s[:, kt, 128 * m:128 * (m + 1)], gcx[:, kt:kt + 1],
                               start=(kt == 0), stop=(kt == KT - 1))
        return ins
    sstep("pe", gc1b)
    def gc1act(e):
        ins = None
        for m in range(4):
            ins = e.activation(out=gch[:, m:m + 1], in_=pD[:, m:m + 1], func=AF.Gelu,
                               bias=rowbc[:, 0, m:m + 1])
        return ins
    sstep("act", gc1act)
    def gc2(e):
        ins = None
        for m in range(2):
            for kt in range(4):
                ins = e.matmul(pD[:, 8 + m:9 + m], g2s[:, kt, 128 * m:128 * (m + 1)], gch[:, kt:kt + 1],
                               start=(kt == 0), stop=(kt == 3))
        return ins
    sstep("pe", gc2)
    def gc2act(e):
        ins = None
        for m in range(2):
            ins = e.activation(out=gch[:, 4 + m:5 + m] if False else gch[:, m:m + 1], in_=pD[:, 8 + m:9 + m],
                               func=AF.Gelu, bias=rowbc[:, 1, m:m + 1])
        return ins
    sstep("act", gc2act)
    def gc3(e):
        ins = None
        for kt in range(2):
            ins = e.matmul(pD[0:1, 12:13], g3s[:, kt, 0:1], gch[:, kt:kt + 1],
                           start=(kt == 0), stop=(kt == 1))
        return ins
    sstep("pe", gc3)
    def gc3act(e):
        return e.activation(out=rows[:, 103:104], in_=pD[0:1, 12:13], func=AF.Sigmoid,
                            bias=rows[:, 101:102])
    sstep("act", gc3act)
    def couplc(e):
        e.tensor_scalar(out=rows[:, 104:105], in0=rows[:, 103:104], scalar1=0.7, scalar2=None, op0=ALU.mult)
        e.tensor_scalar(out=rows[:, 105:106], in0=rows[:, 102:103], scalar1=2.0, scalar2=None, op0=ALU.mult)
        return e.tensor_scalar(out=rows[:, 106:107], in0=rows[:, 102:103], scalar1=-1.0, scalar2=0.1,
                               op0=ALU.mult, op1=ALU.add)
    sstep("dve", couplc)

    twist_map = [(0, 4, 1), (1, 6, -1), (2, 0, -1), (3, 2, 1)]
    for s in range(NST):
        t0 = ST * s
        sdma("sp", lambda e, t0=t0: e.dma_start(
            out=xs[:], in_=dram["x1_dram"].rearrange("(a p) t -> p a t", p=128)[:, :, t0:t0 + ST]))
        cur, nxt = xs, xs2
        for cyc in range(3):
            sstep("dve", lambda e, cur=cur: e.tensor_copy(xs16[:], cur[:]))
            for m in range(KT):
                def m1(e, m=m):
                    ins = None
                    for kt in range(KT):
                        ins = e.matmul(pA[:], w1s[:, kt, 128 * m:128 * (m + 1)], xs16[:, kt, :],
                                       start=(kt == 0), stop=(kt == KT - 1))
                    return ins
                sstep("pe", m1)
                sstep("act", lambda e, m=m: e.activation(out=t1[:, m, :], in_=pA[:], func=AF.Gelu,
                                                         bias=bias[:, m:m + 1]))
            for m in range(4):
                def m2(e, m=m):
                    ins = None
                    for kt in range(KT):
                        ins = e.matmul(pA[:], w2s[:, kt, 128 * m:128 * (m + 1)], t1[:, kt, :],
                                       start=(kt == 0), stop=(kt == KT - 1))
                    return ins
                sstep("pe", m2)
                sstep("act", lambda e, m=m: e.activation(out=t2b[:, m, :], in_=pA[:], func=AF.Gelu,
                                                         bias=bias[:, 8 + m:9 + m]))
            for m in range(2):
                def m3(e, m=m):
                    ins = None
                    for kt in range(4):
                        ins = e.matmul(pA[:], w3s[:, kt, 128 * m:128 * (m + 1)], t2b[:, kt, :],
                                       start=(kt == 0), stop=(kt == 3))
                    return ins
                sstep("pe", m3)
                sstep("act", lambda e, m=m: e.activation(out=t3b[:, m, :], in_=pA[:], func=AF.Gelu,
                                                         bias=bias[:, 12 + m:13 + m]))
            def m4(e):
                ins = None
                for kt in range(2):
                    ins = e.matmul(pE1[:], t3b[:, kt, :].rearrange("p n -> p n") if False else w4s[:, kt, :],
                                   t3b[:, kt, :], start=(kt == 0), stop=(kt == 1))
                return ins
            sstep("pe", m4)
            sstep("act", lambda e: e.activation(out=tmp512[0:1, :], in_=pE1[:], func=AF.Sigmoid,
                                                bias=rows[:, 100:101]))
            sstep("dve", lambda e: e.tensor_scalar(out=tmp512[0:1, :], in0=tmp512[0:1, :], scalar1=0.3,
                                                   scalar2=rows[:, 104:105], op0=ALU.mult, op1=ALU.add))
            sstep("dve", lambda e: e.tensor_scalar(out=tmp512[0:1, :], in0=tmp512[0:1, :],
                                                   scalar1=rows[:, 105:106], scalar2=rows[:, 106:107],
                                                   op0=ALU.mult, op1=ALU.add))
            sstep("pe", lambda e: e.matmul(pB[:], ones1f[:], tmp512[0:1, :], start=True, stop=True))
            sstep("dve", lambda e: e.tensor_copy(rowbc[:, 0, :], pB[:]))
            for (qd, gsrc, sign) in twist_map:
                for t2_ in range(2):
                    gd = 2 * qd + t2_
                    gs = gsrc + t2_
                    def tw(e, gd=gd, gs=gs, sign=sign, cur=cur, nxt=nxt):
                        e.tensor_mul(tmp512[:], cur[:, gs, :], rowbc[:, 0, :])
                        return e.tensor_tensor(out=nxt[:, gd, :], in0=cur[:, gd, :], in1=tmp512[:],
                                               op=ALU.add if sign > 0 else ALU.subtract)
                    sstep("dve", tw)
            cur, nxt = nxt, cur
        def resln(e, cur=cur):
            ins = None
            for gg in range(KT):
                ins = e.engine_nop()
            return ins
        sdma("sp", lambda e, t0=t0, nxt=nxt: e.dma_start(
            out=nxt[:], in_=dram["x1_dram"].rearrange("(a p) t -> p a t", p=128)[:, :, t0:t0 + ST]))
        for gg in range(KT):
            sstep("dve", lambda e, gg=gg, cur=cur, nxt=nxt: e.tensor_tensor(
                out=cur[:, gg, :], in0=cur[:, gg, :], in1=nxt[:, gg, :], op=ALU.add))
        ln_emit(st, sstep, cur)
        ch = st["_ln_ch"]
        for tt in range(4):
            for gg in range(KT):
                def trx(e, gg=gg, tt=tt, cur=cur):
                    return e.transpose(pB[:, 128 * (gg % 4):128 * (gg % 4) + 128], cur[:, gg, 128 * tt:128 * tt + 128], eyef[:])
                def trx2(e, gg=gg, tt=tt, cur=cur):
                    return e.transpose(pB[:, 0:128], cur[:, gg, 128 * tt:128 * tt + 128], eyef[:])
                sstep("pe", trx2)
                sstep("dve", lambda e, gg=gg: e.tensor_copy(obuf[:, 128 * gg:128 * gg + 128], pB[:, 0:128]))
            def sto(e, tt=tt, t0=t0):
                return e.dma_start(out=out[t0 + 128 * tt:t0 + 128 * tt + 128, :], in_=obuf[:])
            ch = p.dma("sp", sto, waits=[ch], inc="st")
    st["ffn_done"] = ch


# version 12
# speedup vs baseline: 1.0247x; 1.0223x over previous
import math
import numpy as np
import ml_dtypes
import concourse.bass as bass
import concourse.mybir as mybir
from concourse.bass_utils import run_bass_kernel_spmd

F32 = mybir.dt.float32
BF16 = mybir.dt.bfloat16
AF = mybir.ActivationFunctionType
ALU = mybir.AluOpType
AXL = mybir.AxisListType

NCORES = 8
T = 2048
HALO = 66
TIN = T + HALO
VH = 63
TV = T + VH
H = 4
DK = 256
DV = 256
CH = 128
NCH = T // CH
ST = 512
NST = T // ST
D = 1024
KT = D // 128
GH = 2048
P5 = 5

DEBUG = []
PHASE_LIMIT = 4
CVALS = [1e-6, 1e-5, 0.5, -1.0, 1.0, 0.0, 0.2, 2.0]


class Prog:
    ENGINES = ("pe", "dve", "act", "pool", "sp")

    def __init__(self):
        self.steps = {e: [] for e in self.ENGINES}
        self.counts = {}

    def on(self, eng, fn, waits=(), inc=None, amt=1):
        tok = None
        if inc is not None:
            self.counts[inc] = self.counts.get(inc, 0) + amt
            tok = (inc, self.counts[inc])
        self.steps[eng].append((fn, list(waits), inc, amt))
        return tok

    def dma(self, eng, fn, waits=(), inc=None):
        return self.on(eng, fn, waits, inc, amt=16)

    def bump(self, sem, amt):
        self.counts[sem] = self.counts.get(sem, 0) + amt
        return (sem, self.counts[sem])

    def emit(self, nc, sems):
        self.sems = sems
        block_attr = {"pe": "tensor", "dve": "vector", "act": "scalar",
                      "pool": "gpsimd", "sp": "sync"}
        with nc.Block() as block:
            for eng_name in self.ENGINES:
                steps = self.steps[eng_name]
                if not steps:
                    continue
                def make_body(steps):
                    def body(eng):
                        for fn, waits, inc, amt in steps:
                            for (sem, cnt) in waits:
                                eng.wait_ge(sems[sem], cnt)
                            last = fn(eng)
                            if inc is not None:
                                assert last is not None, f"step for {inc} returned no instruction"
                                last.then_inc(sems[inc], amt)
                    return body
                getattr(block, block_attr[eng_name])(make_body(steps))


def ceil_div(a, b):
    return (a + b - 1) // b


def build_nc():
    nc = bass.Bass("TRN2", target_bir_lowering=False, debug=False)
    p = Prog()

    din = {}
    def inp(name, shape, dtype=F32):
        din[name] = nc.dram_tensor(name, shape, dtype, kind="ExternalInput").ap()
        return din[name]

    hsT = inp("hsT", [D, TIN])
    wq = inp("wq", [D, D]); wk = inp("wk", [D, D]); wv = inp("wv", [D, D])
    wb = inp("wb", [D, H])
    conv_q = inp("conv_q", [D, 4]); conv_k = inp("conv_k", [D, 4]); conv_v = inp("conv_v", [D, 4])
    fir_s = inp("fir_s", [D, 3]); fir_m = inp("fir_m", [D, 15]); fir_l = inp("fir_l", [D, 64])
    gw1_slice = inp("gw1_slice", [20 * 128, GH])
    gw1_hs = inp("gw1_hs", [D, GH])
    gb1 = inp("gb1", [128, 16]); gw2 = inp("gw2", [GH, 20]); gb2 = inp("gb2", [20])
    log_tau = inp("log_tau", [H]); eps_logit = inp("eps_logit", [H, 5])
    norm_w = inp("norm_w", [128, 2]); wo = inp("wo", [D, D])
    cn_w1 = inp("cn_w1", [D, D]); cn_b1 = inp("cn_b1", [128, 8])
    cn_w2 = inp("cn_w2", [D, 512]); cn_b2 = inp("cn_b2", [128, 4])
    cn_w3 = inp("cn_w3", [512, 256]); cn_b3 = inp("cn_b3", [128, 2])
    cn_w4 = inp("cn_w4", [256, 1]); cn_b4 = inp("cn_b4", [1])
    gc_w1 = inp("gc_w1", [D, 512]); gc_b1 = inp("gc_b1", [128, 4])
    gc_w2 = inp("gc_w2", [512, 256]); gc_b2 = inp("gc_b2", [128, 2])
    gc_w3 = inp("gc_w3", [256, 1]); gc_b3 = inp("gc_b3", [1])
    adaptive_range = inp("adaptive_range", [1])
    eye16 = inp("eye16", [128, 128], BF16)
    eyef = inp("eyef", [128, 128])
    mask3 = inp("mask3", [128, 384])
    sel20 = inp("sel20", [20, 20, 128])
    sel_m = inp("sel_m", [128, 20, 80], BF16)
    sel_v = inp("sel_v", [128, 20, 80], BF16)
    sel_a = inp("sel_a", [128, 20, 80], BF16)
    ones1 = inp("ones1", [1, 128], BF16)
    ones1f = inp("ones1f", [1, 128])
    onesc = inp("onesc", [128, 1], BF16)
    onescf = inp("onescf", [128, 1])
    g0c = inp("g0c", [128, 2, 512], BF16)
    gsel = inp("gsel", [128, 20, 10])
    pref_mask = inp("pref_mask", [128, 8])
    pref_inv = inp("pref_inv", [128, 8])
    cvals = inp("cvals", [128, 8])
    cvals16 = inp("cvals16", [128, 8], BF16)

    out = nc.dram_tensor("out", [T, D], F32, kind="ExternalOutput").ap()

    dbg = {}
    for name in DEBUG:
        shp = DEBUG_SHAPES[name]
        dbg[name] = nc.dram_tensor("dbg_" + name, shp, F32, kind="ExternalOutput").ap()

    cc_g_in = nc.dram_tensor("cc_g_in", [H * 2 * 128, 512], F32).ap()
    cc_g_out = nc.dram_tensor("cc_g_out", [8 * H * 2 * 128, 512], F32).ap()
    cc_gw_in = nc.dram_tensor("cc_gw_in", [10, GH], F32).ap()
    cc_gw_out = nc.dram_tensor("cc_gw_out", [80, GH], F32).ap()
    cc_mean_in = nc.dram_tensor("cc_mean_in", [128, 8], F32).ap()
    cc_mean_out = nc.dram_tensor("cc_mean_out", [128, 8], F32).ap()
    x1_dram = nc.dram_tensor("x1_dram", [D, T], F32).ap()

    sem_names = ["ld", "pe", "dve", "act", "pool", "sp", "lw", "st", "cc", "ld2", "ld3",
                 "lda", "ldb", "lwa", "lwb", "pld"]

    import contextlib
    guards = {}
    tensors = {}

    def salloc(name, shape, dtype):
        g = nc.sbuf_tensor(name, shape, dtype)
        t = g.__enter__()
        guards[name] = g
        tensors[name] = t
        return t

    def sfree(*names):
        for n in names:
            guards.pop(n).__exit__(None, None, None)

    def palloc(name, shape, dtype=F32):
        g = nc.psum_tensor(name, shape, dtype)
        t = g.__enter__()
        guards[name] = g
        tensors[name] = t
        return t

    build_program(nc, p, din, out, dbg,
                  dict(cc_g_in=cc_g_in, cc_g_out=cc_g_out, cc_gw_in=cc_gw_in,
                       cc_gw_out=cc_gw_out, cc_mean_in=cc_mean_in,
                       cc_mean_out=cc_mean_out, x1_dram=x1_dram),
                  salloc, sfree, palloc)

    with contextlib.ExitStack() as ctx:
        sems = {s: ctx.enter_context(nc.semaphore(s)) for s in sem_names}
        p.emit(nc, sems)
    for n in reversed(list(guards)):
        guards.pop(n).__exit__(None, None, None)
    return nc


DEBUG_SHAPES = {
    "o16": [128, KT, T],
    "x1dump": [D, T],
    "probs": [NST, 128, 4, 20],
    "logitsdump": [NST, 128, 4, 20],
    "rowsdump": [1, 128],
    "gwout": [80, GH],
    "gredump": [10, GH],
    "statsdump": [NST, 128, 512],
    "epsbcdump": [128, 40],
    "gw2dump": [128, 16, 20],
    "ocomb": [NST, 128, KT, 512],
    "stats80": [128, T],
    "hgate": [128, 16, 512],
    "gout": [8 * H * 2 * 128, 512],
    "fst": [128, H * NCH, 128],
    "sinit": [128, 2 * H, 256],
    "q16": [128, KT, T],
    "k16": [128, KT, T],
    "v16": [128, KT, TV],
    "beta": [128, NCH, H],
    "rsq": [1, 8, T],
}


def build_program(nc, p, din, out, dbg, dram, salloc, sfree, palloc):
    st = {}
    st.update(nc=nc, p=p, din=din, out=out, dbg=dbg, dram=dram,
              salloc=salloc, sfree=sfree, palloc=palloc)
    phase15_gw1(st)
    phase1_front(st)
    if PHASE_LIMIT >= 2:
        phase2_delta(st)
    if PHASE_LIMIT >= 3:
        phase3_gate(st)
    if PHASE_LIMIT >= 4:
        phase4_ffn(st)
    finish_debug(st)


def phase1_front(st):
    nc, p, din = st["nc"], st["p"], st["din"]
    salloc, palloc = st["salloc"], st["palloc"]

    cw = salloc("cw", [128, KT, 12], F32)
    s_eye16 = salloc("s_eye16", [128, 128], BF16)
    s_eyef = salloc("s_eyef", [128, 128], F32)
    s_onesc = salloc("s_onesc", [128, 1], BF16)
    s_wb = salloc("s_wb", [128, KT, H], BF16)
    s_ones1f = salloc("s_ones1f", [1, 128], F32)
    s_ones1 = salloc("s_ones1", [1, 128], BF16)
    s_onescf = salloc("s_onescf", [128, 1], F32)
    s_mask3 = salloc("s_mask3", [128, 384], F32)
    s_cvals = salloc("s_cvals", [128, 8], F32)
    s_cvals16 = salloc("s_cvals16", [128, 8], BF16)
    for i, v in enumerate(CVALS):
        nc.const_aps.register(F32, v, s_cvals[:, i:i + 1])
        nc.const_aps.register(BF16, v, s_cvals16[:, i:i + 1])

    v16 = salloc("v16", [128, KT, TV], BF16)
    o16 = salloc("o16", [128, KT, T], BF16)
    q16 = salloc("q16", [128, KT, T], BF16)
    k16 = salloc("k16", [128, KT, T], BF16)
    beta_tm = salloc("beta_tm", [128, NCH, H], F32)
    hs16 = salloc("hs16", [128, KT, TIN], BF16)
    st["o16"] = o16
    rsq = salloc("rsq", [1, 576], F32)
    st.update(hs16=hs16, q16=q16, k16=k16, v16=v16, beta_tm=beta_tm,
              s_eye16=s_eye16, s_eyef=s_eyef, s_onesc=s_onesc, rsq=rsq,
              s_ones1f=s_ones1f, s_ones1=s_ones1, s_onescf=s_onescf,
              s_mask3=s_mask3, s_cvals=s_cvals, s_cvals16=s_cvals16)

    hsf = salloc("hsf", [128, 2, KT, 544], F32)

    pp = [palloc("pp0", [128, 512], F32), palloc("pp1", [128, 512], F32)]

    NC_IN = ceil_div(TIN, 512)
    def chunk_rng(i, total):
        lo = 512 * i
        return lo, min(512, total - lo)

    castok = {}
    for i in range(NC_IN):
        lo, ln = chunk_rng(i, TIN)
        bi = i % 2
        def ld(eng, lo=lo, ln=ln, bi=bi):
            return eng.dma_start(out=hsf[:, bi, :, 0:ln],
                                 in_=din["hsT"].rearrange("(a p) t -> p a t", p=128)[:, :, lo:lo + ln])
        waits = [castok[i - 2]] if i >= 2 else []
        ldtok = p.dma("sp", ld, waits=waits, inc="lda" if bi == 0 else "ldb")
        def cast(eng, lo=lo, ln=ln, bi=bi):
            return eng.tensor_copy(hs16[:, :, lo:lo + ln], hsf[:, bi, :, 0:ln])
        castok[i] = p.on("dve", cast, waits=[ldtok], inc="dve")
    hs16_ready = castok[NC_IN - 1]
    st["hs16_ready"] = hs16_ready

    def ld_misc(eng):
        sem = p.sems["ld2"]
        eng.dma_start(out=cw[:, :, 0:4], in_=din["conv_q"].rearrange("(a p) j -> p a j", p=128)).then_inc(sem, 16)
        eng.dma_start(out=cw[:, :, 4:8], in_=din["conv_k"].rearrange("(a p) j -> p a j", p=128)).then_inc(sem, 16)
        eng.dma_start(out=cw[:, :, 8:12], in_=din["conv_v"].rearrange("(a p) j -> p a j", p=128)).then_inc(sem, 16)
        eng.dma_start(out=s_eye16[:], in_=din["eye16"]).then_inc(sem, 16)
        eng.dma_start(out=s_eyef[:], in_=din["eyef"]).then_inc(sem, 16)
        eng.dma_start(out=s_mask3[:], in_=din["mask3"]).then_inc(sem, 16)
        eng.dma_start(out=s_ones1f[:], in_=din["ones1f"]).then_inc(sem, 16)
        eng.dma_start(out=s_ones1[:], in_=din["ones1"]).then_inc(sem, 16)
        eng.dma_start(out=s_onescf[:], in_=din["onescf"]).then_inc(sem, 16)
        eng.dma_start(out=s_cvals[:], in_=din["cvals"]).then_inc(sem, 16)
        eng.dma_start(out=s_cvals16[:], in_=din["cvals16"]).then_inc(sem, 16)
        eng.dma_start(out=s_onesc[:], in_=din["onesc"]).then_inc(sem, 16)
        return None
    p.on("sp", ld_misc, waits=[])
    misc_tok = p.bump("ld2", 16 * 12)
    st["misc_tok"] = misc_tok

    def ld_wb(eng):
        return eng.dma_start(out=hsf[:, 0, :, 512:512 + H],
                             in_=din["wb"].rearrange("(a p) h -> p a h", p=128))
    wb_tok = p.dma("sp", ld_wb, waits=[castok[NC_IN - 2]], inc="ld")
    def cast_wb(eng):
        return eng.tensor_copy(s_wb[:], hsf[:, 0, :, 512:512 + H])
    wb16_tok = p.on("dve", cast_wb, waits=[wb_tok], inc="dve")

    st["sfree"]("hsf")
    wbuf = salloc("wbuf", [128, 2, KT, 128], F32)
    wbuf16 = salloc("wbuf16", [128, 2, KT, 128], BF16)
    raw = salloc("raw", [128, 2, TIN], BF16)
    cacc = salloc("cacc", [128, 544], F32)

    projs = [("wq", 0, q16, T, 66), ("wk", 1, k16, T, 66), ("wv", 2, v16, TV, 3)]
    prev_cast = {}
    prev_silu = None
    ev_toks = []
    raw_free = {}
    step = 0
    for wi, (wname, cidx, dst, tlen, off) in enumerate(projs):
        for m in range(KT):
            gi = wi * KT + m
            bi = gi % 2
            def ldw(eng, wname=wname, m=m, bi=bi):
                return eng.dma_start(
                    out=wbuf[:, bi],
                    in_=din[wname].rearrange("(a p) n -> p a n", p=128)[:, :, 128 * m:128 * (m + 1)])
            waits = [prev_cast[bi]] if prev_cast.get(bi) else [hs16_ready, wb16_tok]
            lw_tok = p.dma("sp", ldw, waits=waits, inc="lwa" if bi == 0 else "lwb")
            def castw(eng, bi=bi):
                return eng.tensor_copy(wbuf16[:, bi], wbuf[:, bi])
            cw_tok = p.on("dve", castw, waits=[lw_tok], inc="dve")
            prev_cast[bi] = cw_tok
            rwaits = [raw_free[bi]] if bi in raw_free else []
            for i in range(NC_IN):
                lo, ln = chunk_rng(i, TIN)
                ps = pp[step % 2]
                def mm(eng, bi=bi, lo=lo, ln=ln, ps=ps):
                    ins = None
                    for kt in range(KT):
                        ins = eng.matmul(ps[:, 0:ln], wbuf16[:, bi, kt, :],
                                         hs16[:, kt, lo:lo + ln], start=(kt == 0), stop=(kt == KT - 1))
                    return ins
                waits = [cw_tok, hs16_ready]
                if step >= 2:
                    waits.append(ev_toks[step - 2])
                mm_tok = p.on("pe", mm, waits=waits, inc="pe")
                def ev(eng, bi=bi, lo=lo, ln=ln, ps=ps):
                    return eng.copy(raw[:, bi, lo:lo + ln], ps[:, 0:ln])
                ev_toks.append(p.on("act", ev, waits=[mm_tok] + (rwaits if i == 0 else []), inc="act"))
                step += 1
            raw_ready = ev_toks[-1]
            NCC = ceil_div(tlen, 512)
            cv_tok = None
            for i in range(NCC):
                lo, ln = chunk_rng(i, tlen)
                base = off - 3 + lo
                def conv(eng, m=m, bi=bi, lo=lo, ln=ln, base=base, cidx=cidx):
                    eng.tensor_scalar(out=cacc[:, 0:ln], in0=raw[:, bi, base:base + ln],
                                      scalar1=cw[:, m, 4 * cidx:4 * cidx + 1], scalar2=None,
                                      op0=ALU.mult)
                    ins = None
                    for j in range(1, 4):
                        ins = eng.scalar_tensor_tensor(
                            out=cacc[:, 0:ln], in0=raw[:, bi, base + j:base + j + ln],
                            scalar=cw[:, m, 4 * cidx + j:4 * cidx + j + 1],
                            in1=cacc[:, 0:ln], op0=ALU.mult, op1=ALU.add)
                    return ins
                cv_tok = p.on("dve", conv, waits=[raw_ready, misc_tok] + ([prev_silu] if prev_silu else []), inc="dve")
                def silu(eng, m=m, lo=lo, ln=ln, dst=dst):
                    return eng.activation(out=dst[:, m, lo:lo + ln], in_=cacc[:, 0:ln], func=AF.Silu)
                prev_silu = p.on("act", silu, waits=[cv_tok], inc="act")
            raw_free[bi] = cv_tok
    silu_all = ("act", p.counts["act"])
    st["qkv_ready"] = silu_all

    ppb = palloc("ppb", [128, H], F32)
    beta_ev = None
    for c in range(NCH):
        lo = HALO + CH * c
        def mmb(eng, lo=lo):
            ins = None
            for kt in range(KT):
                ins = eng.matmul(ppb[:], hs16[:, kt, lo:lo + CH], s_wb[:, kt, :],
                                 start=(kt == 0), stop=(kt == KT - 1))
            return ins
        tok = p.on("pe", mmb, waits=[hs16_ready, wb16_tok] + ([beta_ev] if beta_ev else []), inc="pe")
        def sig(eng, c=c):
            return eng.activation(out=beta_tm[:, c, :], in_=ppb[:], func=AF.Sigmoid)
        beta_ev = p.on("act", sig, waits=[tok], inc="act")
    st["beta_ready"] = beta_ev

    st["l2_done"] = l2norm(st, pp, q16, k16, s_onesc, silu_all)

    emit_debug(st, dict(q16=q16, k16=k16, v16=v16, beta=beta_tm))
    st["sfree"]("cacc", "raw", "wbuf16", "wbuf")
    st["sfree"]("rsq")
    for nm in ("pbc", "prs", "ppb", "pp1", "pp0"):
        st["sfree"](nm)
    st["sfree"]("hs16")


def l2norm(st, pp, q16, k16, s_onesc, qkv_ready):
    nc, p = st["nc"], st["p"]
    salloc, palloc = st["salloc"], st["palloc"]
    rsq = st["rsq"]
    sq = salloc("sq", [128, 2, 512], BF16)
    prs = palloc("prs", [1, 512], F32)
    pbc = palloc("pbc", [128, 512], F32)
    bc = salloc("bc", [128, 512], F32)
    last_sc = None
    prev_sr = None
    prev_bc = None
    for ti, tens in ((0, q16), (1, k16)):
        for h in range(H):
            for i in range(NST):
                lo = ST * i
                def sqf(eng, tens=tens, h=h, lo=lo):
                    eng.tensor_mul(sq[:, 0, :], tens[:, 2 * h, lo:lo + ST], tens[:, 2 * h, lo:lo + ST])
                    return eng.tensor_mul(sq[:, 1, :], tens[:, 2 * h + 1, lo:lo + ST], tens[:, 2 * h + 1, lo:lo + ST])
                sq_tok = p.on("dve", sqf, waits=[qkv_ready, st["misc_tok"]] + ([last_sc] if last_sc else []), inc="dve")
                def mm(eng):
                    eng.matmul(prs[:], s_onesc[:], sq[:, 0, :], start=True, stop=False)
                    return eng.matmul(prs[:], s_onesc[:], sq[:, 1, :], start=False, stop=True)
                mm_tok = p.on("pe", mm, waits=[sq_tok] + ([prev_sr] if prev_sr else []), inc="pe")
                def sqr(eng):
                    return eng.activation(out=rsq[:, 0:ST], in_=prs[:], func=AF.Sqrt, bias=1e-6)
                sr_tok = p.on("act", sqr, waits=[mm_tok] + ([prev_bc] if prev_bc else []), inc="act")
                prev_sr = sr_tok
                def rec(eng):
                    return eng.reciprocal(rsq[:, 0:ST], rsq[:, 0:ST])
                rc_tok = p.on("dve", rec, waits=[sr_tok], inc="dve")
                def bcm(eng):
                    return eng.matmul(pbc[:], st["s_ones1f"][:], rsq[:, 0:ST],
                                      start=True, stop=True)
                bc_tok = p.on("pe", bcm, waits=[rc_tok] + ([last_sc] if last_sc else []), inc="pe")
                prev_bc = bc_tok
                def evbc(eng):
                    return eng.tensor_copy(bc[:], pbc[:])
                ev_tok = p.on("dve", evbc, waits=[bc_tok], inc="dve")
                def scale(eng, tens=tens, h=h, lo=lo):
                    eng.tensor_mul(tens[:, 2 * h, lo:lo + ST], tens[:, 2 * h, lo:lo + ST], bc[:])
                    return eng.tensor_mul(tens[:, 2 * h + 1, lo:lo + ST], tens[:, 2 * h + 1, lo:lo + ST], bc[:])
                last_sc = p.on("dve", scale, waits=[ev_tok], inc="dve")
    st["sfree"]("bc", "sq")
    return last_sc


def emit_debug(st, mapping):
    p, dbg = st["p"], st["dbg"]
    for name, tens in mapping.items():
        if name in dbg:
            def store(eng, tens=tens, name=name):
                return eng.dma_start(out=dbg[name], in_=tens[:])
            waits = [(s, c) for s, c in p.counts.items()]
            p.dma("pool", store, waits=waits, inc="st")


def finish_debug(st):
    p = st["p"]
    def fin(eng):
        return eng.nop()
    waits = [(s, c) for s, c in p.counts.items()]
    p.on("sp", fin, waits=waits)



def make_consts():
    c = {}
    c["eye16"] = np.eye(128).astype(ml_dtypes.bfloat16)
    c["eyef"] = np.eye(128, dtype=np.float32)
    i = np.arange(128)
    mb = np.zeros((128, 384), np.float32)
    mb[:, 0:128] = -(i[:, None] < i[None, :]).astype(np.float32)
    mb[:, 128:256] = -(i[:, None] > i[None, :]).astype(np.float32)
    mb[:, 256:384] = (i[:, None] <= i[None, :]).astype(np.float32)
    c["mask3"] = mb
    sel = np.zeros((20, 20, 128), np.float32)
    for r in range(20):
        sel[r, r, :] = 1.0
    c["sel20"] = sel
    c["ones1"] = np.ones((1, 128), ml_dtypes.bfloat16)
    c["ones1f"] = np.ones((1, 128), np.float32)
    c["onesc"] = np.ones((128, 1), ml_dtypes.bfloat16)
    c["onescf"] = np.ones((128, 1), np.float32)
    g0 = np.zeros((128, 2, 512), np.float32)
    for t in range(2):
        g0[:, t, 128 * t:128 * (t + 1)] = np.eye(128)
    c["g0c"] = g0.astype(ml_dtypes.bfloat16)
    gs = np.zeros((128, 20, 10), np.float32)
    for t in range(20):
        gs[:, t, t // 2] = 1.0
    c["gsel"] = gs
    for nm, entries in (("sel_m", [(1, 1.0 / 256)]), ("sel_v", [(2, 1.0 / 256), (0, 1.0)]),
                        ("sel_a", [(3, 1.0 / 256)])):
        sl = np.zeros((128, 20, 80), np.float32)
        for pth in range(5):
            for hh in range(4):
                for blk, val in entries:
                    sl[:, pth * 4 + hh, 20 * blk + pth * 4 + hh] = val
        c[nm] = sl.astype(ml_dtypes.bfloat16)
    cv = np.broadcast_to(np.asarray(CVALS, np.float32), (128, 8)).copy()
    c["cvals"] = cv
    c["cvals16"] = cv.astype(ml_dtypes.bfloat16)
    return c


def kernel(**inputs):
    hs = np.ascontiguousarray(np.asarray(inputs["L0_attn_hidden_states"], dtype=np.float32).reshape(16384, D))
    consts = make_consts()
    weights = {
        "wq": "L0_attn_Wq", "wk": "L0_attn_Wk", "wv": "L0_attn_Wv", "wb": "L0_attn_Wb",
        "conv_q": "L0_attn_conv_q", "conv_k": "L0_attn_conv_k", "conv_v": "L0_attn_conv_v",
        "fir_s": "L0_attn_fir_s", "fir_m": "L0_attn_fir_m", "fir_l": "L0_attn_fir_l",
        "gb1": "L0_attn_gb1", "gw2": "L0_attn_gW2", "gb2": "L0_attn_gb2",
        "log_tau": "L0_attn_log_tau", "eps_logit": "L0_attn_eps_logit",
        "norm_w": "L0_attn_norm_w", "wo": "L0_attn_Wo",
        "cn_w1": "L0_ffn_cn_w1", "cn_b1": "L0_ffn_cn_b1", "cn_w2": "L0_ffn_cn_w2",
        "cn_b2": "L0_ffn_cn_b2", "cn_w3": "L0_ffn_cn_w3", "cn_b3": "L0_ffn_cn_b3",
        "cn_w4": "L0_ffn_cn_w4", "cn_b4": "L0_ffn_cn_b4",
        "gc_w1": "L0_ffn_gc_w1", "gc_b1": "L0_ffn_gc_b1", "gc_w2": "L0_ffn_gc_w2",
        "gc_b2": "L0_ffn_gc_b2", "gc_w3": "L0_ffn_gc_w3", "gc_b3": "L0_ffn_gc_b3",
        "adaptive_range": "L0_ffn_adaptive_range",
    }
    shared = {k: np.ascontiguousarray(np.asarray(inputs[v], dtype=np.float32)) for k, v in weights.items()}
    for nm, nt in (("gb1", 16), ("norm_w", 2), ("cn_b1", 8), ("cn_b2", 4), ("cn_b3", 2),
                   ("gc_b1", 4), ("gc_b2", 2)):
        shared[nm] = np.ascontiguousarray(shared[nm].reshape(nt, 128).T)
    gW1 = np.ascontiguousarray(np.asarray(inputs["L0_attn_gW1"], dtype=np.float32))
    shared["gw1_hs"] = gW1[0:D]

    nc = build_nc()
    in_maps = []
    for c in range(NCORES):
        m = dict(shared)
        m.update(consts)
        lo = T * c - HALO
        if lo < 0:
            sl = np.concatenate([np.zeros((HALO, D), np.float32), hs[0:T * c + T]], axis=0)
        else:
            sl = hs[lo:T * (c + 1)]
        m["hsT"] = np.ascontiguousarray(sl.T)
        m["gw1_slice"] = np.ascontiguousarray(gW1[D + 2560 * c: D + 2560 * (c + 1)])
        pm = np.zeros((128, 8), np.float32)
        pm[:, 0:c] = 1.0
        m["pref_mask"] = pm
        m["pref_inv"] = 1.0 - pm
        in_maps.append(m)
    res = run_bass_kernel_spmd(nc, in_maps, core_ids=list(range(NCORES)))
    outs = [res.results[c]["out"] for c in range(NCORES)]
    full = np.concatenate(outs, axis=0).reshape(1, 16384, D)
    if DEBUG:
        kernel.debug_results = res.results
    return full


def phase2_delta(st):
    nc, p, din, dram = st["nc"], st["p"], st["din"], st["dram"]
    salloc, sfree, palloc = st["salloc"], st["sfree"], st["palloc"]
    q16, k16, v16, beta = st["q16"], st["k16"], st["v16"], st["beta_tm"]
    eye16, eyef, ones1f, mask3 = st["s_eye16"], st["s_eyef"], st["s_ones1f"], st["s_mask3"]

    Fst = salloc("Fst", [128, H * NCH, 128], BF16)
    o16 = st["o16"]
    ktm = salloc("ktm", [128, 256], BF16)
    vtm = salloc("vtm", [128, 256], BF16)
    nkvb = salloc("nkvb", [128, 512], BF16)
    kbf = salloc("kbf", [128, 256], BF16)
    brow = salloc("brow", [1, 128], F32)
    bbc = salloc("bbc", [128, 128], BF16)
    BAL = salloc("BAL", [128, 384], BF16)
    BALf = salloc("BALf", [128, 384], F32)
    PBf = salloc("PBf", [128, 256], F32)
    ngs = salloc("ngs", [128, 256], BF16)
    lcl = salloc("lcl", [128, 128], BF16)
    us = salloc("us", [128, 512], BF16)
    Gs = salloc("Gs", [128, 2, 512], BF16)
    goutf = salloc("goutf", [128, 2, 512], F32)
    s_pref = salloc("s_pref", [128, 16], F32)
    gpref = salloc("gpref", [128, 2, 512], F32)
    gj16 = salloc("gj16", [128, 2, 512], BF16)
    mjt = salloc("mjt", [128, 2, 256], BF16)
    Sinit = salloc("Sinit", [128, 2 * H, 256], BF16)
    Ss = salloc("Ss", [128, 2, 256], BF16)
    u16s = salloc("u16s", [128, 256], BF16)
    otm = salloc("otm", [128, 256], BF16)
    g0s = salloc("g0s", [128, 2, 512], BF16)

    pG = palloc("pG", [128, 2, 512], F32)
    pU = palloc("pU", [128, 512], F32)
    pM = palloc("pM", [128, 384], F32)
    pF = palloc("pF", [128, 128], F32)
    pNG = palloc("pNG", [128, 256], F32)
    pT = palloc("pT", [128, 256], BF16)
    pR = palloc("pR", [1, 128], F32)

    ch = st["l2_done"]

    first = [True]
    def sstep(eng, fn, extra=()):
        nonlocal ch
        w = [ch] + list(extra)
        if first[0]:
            w.append(st["gw1_done"])
            first[0] = False
        ch = p.on(eng, fn, waits=w, inc=eng if eng in ("pe", "dve", "act", "pool") else "sp",
                  amt=16 if eng in () else 1)
        return ch

    def sdma(eng, fn, extra=()):
        nonlocal ch
        ch = p.dma(eng, fn, waits=[ch] + list(extra), inc="ld3")
        return ch

    def ld_pref(eng):
        sem = p.sems["ld3"]
        eng.dma_start(out=s_pref[:, 0:8], in_=din["pref_mask"]).then_inc(sem, 16)
        eng.dma_start(out=s_pref[:, 8:16], in_=din["pref_inv"]).then_inc(sem, 16)
        eng.dma_start(out=g0s[:], in_=din["g0c"]).then_inc(sem, 16)
        return None
    p.on("sp", ld_pref, waits=[ch])
    pref_tok = p.bump("ld3", 48)

    beta_ready = st["beta_ready"]

    def prep_chunk(h, c, mode):
        cb = CH * c
        fidx = h * NCH + c
        def tr_k(eng):
            eng.transpose(pT[:, 0:128], k16[:, 2 * h, cb:cb + 128], eye16[:])
            return eng.transpose(pT[:, 128:256], k16[:, 2 * h + 1, cb:cb + 128], eye16[:])
        sstep("pe", tr_k)
        sstep("act", lambda eng: eng.copy(ktm[:], pT[:]))
        def tr_v(eng):
            eng.transpose(pT[:, 0:128], v16[:, 2 * h, VH + cb:VH + cb + 128], eye16[:])
            return eng.transpose(pT[:, 128:256], v16[:, 2 * h + 1, VH + cb:VH + cb + 128], eye16[:])
        sstep("pe", tr_v)
        sstep("act", lambda eng: eng.copy(vtm[:], pT[:]))
        def mk_nkvb(eng):
            eng.tensor_scalar(out=nkvb[:, 0:256], in0=ktm[:], scalar1=beta[:, c, h:h + 1],
                              scalar2=-1.0, op0=ALU.mult, op1=ALU.mult)
            return eng.tensor_scalar(out=nkvb[:, 256:512], in0=vtm[:], scalar1=beta[:, c, h:h + 1],
                                     scalar2=None, op0=ALU.mult)
        sstep("dve", mk_nkvb, extra=[beta_ready])
        if mode == "A":
            def brow_mm(eng):
                return eng.matmul(pR[:], beta[:, c, h:h + 1], eyef[:], start=True, stop=True)
            sstep("pe", brow_mm)
            sstep("act", lambda eng: eng.copy(brow[:], pR[:]))
            def bbc_mm(eng):
                return eng.matmul(pU[:, 0:128], ones1f[:], brow[:], start=True, stop=True)
            sstep("pe", bbc_mm)
            sstep("dve", lambda eng: eng.tensor_copy(bbc[:], pU[:, 0:128]))
            def mk_kbf(eng):
                eng.tensor_mul(kbf[:, 0:128], k16[:, 2 * h, cb:cb + 128], bbc[:])
                return eng.tensor_mul(kbf[:, 128:256], k16[:, 2 * h + 1, cb:cb + 128], bbc[:])
            sstep("dve", mk_kbf)
            def grams(eng):
                for t in range(2):
                    eng.matmul(pM[:, 0:128], k16[:, 2 * h + t, cb:cb + 128], kbf[:, 128 * t:128 * t + 128],
                               start=(t == 0), stop=(t == 1))
                for t in range(2):
                    eng.matmul(pM[:, 128:256], kbf[:, 128 * t:128 * t + 128], k16[:, 2 * h + t, cb:cb + 128],
                               start=(t == 0), stop=(t == 1))
                return eng.matmul(pM[:, 256:384], eye16[:], eye16[:], start=True, stop=True)
            def grams2(eng):
                ins = None
                for t in range(2):
                    ins = eng.matmul(pM[:, 0:128], k16[:, 2 * h + t, cb:cb + 128], kbf[:, 128 * t:128 * t + 128],
                                     start=(t == 0), stop=(t == 1))
                for t in range(2):
                    ins = eng.matmul(pM[:, 128:256], kbf[:, 128 * t:128 * t + 128], k16[:, 2 * h + t, cb:cb + 128],
                                     start=(t == 0), stop=(t == 1))
                return ins
            sstep("pe", grams2)
            def maskba(eng):
                return eng.tensor_mul(BALf[:, 0:256], pM[:, 0:256], mask3[:, 0:256])
            sstep("dve", maskba)
            def f0(eng):
                return eng.tensor_add(BALf[:, 256:384], BALf[:, 0:128], eyef[:])
            sstep("dve", f0)
            def finit(eng):
                return eng.matmul(pF[:], eyef[:], BALf[:, 256:384], start=True, stop=False)
            sstep("pe", finit)
            fsb_cur = BALf[:, 256:384]
            pb_cur = BALf[:, 0:256]
            for lvl in range(1, 7):
                def sq(eng, pb_cur=pb_cur):
                    eng.matmul(pM[:, 0:128], pb_cur[:, 128:256], pb_cur[:, 0:128], start=True, stop=True)
                    return eng.matmul(pM[:, 128:256], pb_cur[:, 0:128], pb_cur[:, 128:256], start=True, stop=True)
                sstep("pe", sq)
                def evpb(eng):
                    return eng.tensor_copy(PBf[:], pM[:, 0:256])
                sstep("dve", evpb)
                def fcomb(eng, fsb_cur=fsb_cur, last=(lvl == 6)):
                    return eng.matmul(pF[:], PBf[:, 128:256], fsb_cur, start=False, stop=last)
                sstep("pe", fcomb)
                if lvl == 6:
                    def evf(eng, fidx=fidx):
                        return eng.tensor_copy(Fst[:, fidx, :], pF[:])
                    sstep("dve", evf)
                else:
                    def evf(eng):
                        return eng.tensor_copy(BALf[:, 256:384], pF[:])
                    sstep("dve", evf)
                fsb_cur = BALf[:, 256:384]
                pb_cur = PBf
        def ngmm(eng):
            eng.matmul(pNG[:, 0:128], nkvb[:, 0:128], Fst[:, fidx, :], start=True, stop=True)
            return eng.matmul(pNG[:, 128:256], nkvb[:, 128:256], Fst[:, fidx, :], start=True, stop=True)
        sstep("pe", ngmm)
        sstep("act", lambda eng: eng.copy(ngs[:], pNG[:]))
        if mode == "B":
            def gram3(eng):
                ins = None
                for t in range(2):
                    ins = eng.matmul(pM[:, 256:384], k16[:, 2 * h + t, cb:cb + 128],
                                     q16[:, 2 * h + t, cb:cb + 128], start=(t == 0), stop=(t == 1))
                return ins
            sstep("pe", gram3)
            def masklcl(eng):
                return eng.tensor_mul(lcl[:], pM[:, 256:384], mask3[:, 256:384])
            sstep("dve", masklcl)

    for h in range(H):
        for c in range(NCH):
            prep_chunk(h, c, "A")
            cb = CH * c
            fidx = h * NCH + c
            if c == 0:
                def umm(eng, fidx=fidx):
                    return eng.matmul(pU[:], Fst[:, fidx, :], nkvb[:], start=True, stop=True)
            else:
                def umm(eng, fidx=fidx):
                    eng.matmul(pU[:], ngs[:, 0:128], Gs[:, 0, :], start=True, stop=False)
                    eng.matmul(pU[:], ngs[:, 128:256], Gs[:, 1, :], start=False, stop=False)
                    return eng.matmul(pU[:, 256:512], Fst[:, fidx, :], nkvb[:, 256:512],
                                      start=False, stop=True)
            sstep("pe", umm)
            sstep("dve", lambda eng: eng.tensor_copy(us[:], pU[:]))
            if c == 0:
                def gmm(eng):
                    ins = None
                    for t in range(2):
                        eng.matmul(pG[:, t, :], ktm[:, 128 * t:128 * t + 128], us[:], start=True, stop=False)
                        ins = eng.matmul(pG[:, t, :], eye16[:], g0s[:, t, :], start=False, stop=True)
                    return ins
            else:
                def gmm(eng):
                    ins = None
                    for t in range(2):
                        ins = eng.matmul(pG[:, t, :], ktm[:, 128 * t:128 * t + 128], us[:],
                                         start=False, stop=(c == NCH - 1))
                    return ins
            sstep("pe", gmm, extra=[pref_tok] if c == 0 else [])
            if c < NCH - 1:
                sstep("dve", lambda eng: eng.tensor_copy(Gs[:], pG[:]))
            else:
                sstep("dve", lambda eng: eng.tensor_copy(goutf[:], pG[:]))
                def st_g(eng, h=h):
                    return eng.dma_start(
                        out=dram["cc_g_in"].rearrange("(a p) n -> p a n", p=128)[:, 2 * h:2 * h + 2, :],
                        in_=goutf[:])
                sdma("pool", st_g)

    def cc_fn(eng):
        return eng.collective_compute(
            "AllGather", ALU.bypass, replica_groups=[list(range(8))],
            ins=[dram["cc_g_in"].opt()], outs=[dram["cc_g_out"].opt()])
    nonlocal_ch = None
    ch = p.on("pool", cc_fn, waits=[ch], inc="cc", amt=1)

    gv = dram["cc_g_out"].rearrange("(r a p) n -> p r a n", p=128, r=8)
    for j in range(7):
        for h in range(H):
            def ld_g(eng, j=j, h=h):
                return eng.dma_start(out=gpref[:], in_=gv[:, j, 2 * h:2 * h + 2, :])
            sdma("pool", ld_g)
            if j == 0:
                def mk_s0(eng):
                    return eng.tensor_scalar(out=Sinit[:, 2 * h:2 * h + 2, :] if False else Sinit[:, 2 * h:2 * h + 2, :],
                                             in0=gpref[:, :, 256:512], scalar1=s_pref[:, 0:1],
                                             scalar2=None, op0=ALU.mult)
                def mk_s0b(eng, h=h):
                    return eng.tensor_scalar(out=Sinit[:, 2 * h:2 * h + 2, :], in0=gpref[:, :, 256:512],
                                             scalar1=s_pref[:, 0:1], scalar2=None, op0=ALU.mult)
                sstep("dve", mk_s0b)
            else:
                def mk_gj(eng, j=j):
                    eng.tensor_scalar(out=gj16[:], in0=gpref[:], scalar1=s_pref[:, j:j + 1],
                                      scalar2=None, op0=ALU.mult)
                    ins = None
                    for t in range(2):
                        ins = eng.scalar_tensor_tensor(
                            out=gj16[:, t, 128 * t:128 * t + 128], in0=eyef[:],
                            scalar=s_pref[:, 8 + j:8 + j + 1], in1=gj16[:, t, 128 * t:128 * t + 128],
                            op0=ALU.mult, op1=ALU.add)
                    return ins
                sstep("dve", mk_gj)
                def tr_m(eng):
                    ins = None
                    for a2 in range(2):
                        for b2 in range(2):
                            ins = eng.transpose(pT[:, 128 * a2:128 * a2 + 128] if b2 == 0 else pT[:, 128 * a2:128 * a2 + 128],
                                                gj16[:, a2, 128 * b2:128 * b2 + 128], eye16[:])
                    return ins
                def tr_m2(eng):
                    ins = None
                    for b2 in range(2):
                        for a2 in range(2):
                            ins = eng.transpose(pT[:, 128 * a2:128 * a2 + 128],
                                                gj16[:, a2, 128 * b2:128 * b2 + 128], eye16[:])
                    return ins
                for b2 in range(2):
                    def tr_b(eng, b2=b2):
                        ins = None
                        for a2 in range(2):
                            ins = eng.transpose(pT[:, 128 * a2:128 * a2 + 128],
                                                gj16[:, a2, 128 * b2:128 * b2 + 128], eye16[:])
                        return ins
                    sstep("pe", tr_b)
                    def ev_b(eng, b2=b2):
                        return eng.tensor_copy(mjt[:, b2, :], pT[:])
                    sstep("dve", ev_b)
                def smm(eng, h=h):
                    ins = None
                    for a2 in range(2):
                        for b2 in range(2):
                            eng.matmul(pG[:, a2, 0:256], mjt[:, b2, 128 * a2:128 * a2 + 128],
                                       Sinit[:, 2 * h + b2, :], start=(b2 == 0), stop=False)
                        ins = eng.matmul(pG[:, a2, 0:256], eye16[:], gj16[:, a2, 256:512],
                                         start=False, stop=True)
                    return ins
                sstep("pe", smm)
                def ev_s(eng, h=h):
                    return eng.tensor_copy(Sinit[:, 2 * h:2 * h + 2, :], pG[:, :, 0:256])
                sstep("dve", ev_s)

    for h in range(H):
        for c in range(NCH):
            prep_chunk(h, c, "B")
            cb = CH * c
            fidx = h * NCH + c
            sread = Sinit[:, 2 * h:2 * h + 2, :] if c == 0 else Ss
            if c == 0:
                def sinitmm(eng, h=h):
                    ins = None
                    for t in range(2):
                        ins = eng.matmul(pG[:, t, 0:256], eye16[:], Sinit[:, 2 * h + t, :],
                                         start=True, stop=False)
                    return ins
                sstep("pe", sinitmm)
            def ummb(eng, fidx=fidx, sread=sread):
                eng.matmul(pU[:, 0:256], Fst[:, fidx, :], nkvb[:, 256:512], start=True, stop=False)
                eng.matmul(pU[:, 0:256], ngs[:, 0:128], sread[:, 0, :], start=False, stop=False)
                return eng.matmul(pU[:, 0:256], ngs[:, 128:256], sread[:, 1, :], start=False, stop=True)
            sstep("pe", ummb)
            sstep("dve", lambda eng: eng.tensor_copy(u16s[:], pU[:, 0:256]))
            def omm(eng, h=h, cb=cb, sread=sread):
                eng.matmul(pU[:, 256:512], q16[:, 2 * h, cb:cb + 128], sread[:, 0, :], start=True, stop=False)
                eng.matmul(pU[:, 256:512], q16[:, 2 * h + 1, cb:cb + 128], sread[:, 1, :], start=False, stop=False)
                return eng.matmul(pU[:, 256:512], lcl[:], u16s[:], start=False, stop=True)
            sstep("pe", omm)
            sstep("act", lambda eng: eng.copy(otm[:], pU[:, 256:512]))
            def smm2(eng):
                ins = None
                for t in range(2):
                    ins = eng.matmul(pG[:, t, 0:256], ktm[:, 128 * t:128 * t + 128], u16s[:],
                                     start=False, stop=False)
                return ins
            sstep("pe", smm2)
            sstep("dve", lambda eng: eng.tensor_copy(Ss[:], pG[:, :, 0:256]))
            def tr_o(eng):
                ins = None
                for t in range(2):
                    ins = eng.transpose(pT[:, 128 * t:128 * t + 128], otm[:, 128 * t:128 * t + 128], eye16[:])
                return ins
            sstep("pe", tr_o)
            def ev_o(eng, h=h, cb=cb):
                return eng.tensor_copy(o16[:, 2 * h:2 * h + 2, cb:cb + 128],
                                       pT[:].rearrange("p (a b) -> p a b", a=2))
            sstep("dve", ev_o)

    st["delta_done"] = ch
    emit_debug(st, dict(o16=o16, sinit=Sinit, fst=Fst))
    if "gout" in st["dbg"]:
        def store_g(eng):
            return eng.dma_start(out=st["dbg"]["gout"], in_=st["dram"]["cc_g_out"])
        p.dma("pool", store_g, waits=[ch], inc="st")
    for nm in ("pR", "pT", "pNG", "pF", "pM", "pU", "pG"):
        sfree(nm)
    for nm in ("g0s", "otm", "u16s", "Ss", "Sinit", "mjt", "gj16", "gpref", "s_pref",
               "goutf", "Gs", "us", "lcl", "ngs", "PBf", "BALf", "BAL", "bbc", "brow", "kbf",
               "nkvb", "vtm", "ktm", "Fst"):
        sfree(nm)
    sfree("hs16") if False else None
    sfree("beta_tm", "k16", "q16")


def phase15_gw1(st):
    nc, p, din, dram = st["nc"], st["p"], st["din"], st["dram"]
    salloc, sfree, palloc = st["salloc"], st["sfree"], st["palloc"]
    gwb = salloc("gwb", [128, 2, GH], F32)
    gsl = salloc("gsl", [128, 20, 10], F32)
    gred = salloc("gred", [10, GH], F32)
    pRD = palloc("pRD", [10, 4, 512], F32)
    ch = None
    def sstep(eng, fn, extra=()):
        nonlocal ch
        ch = p.on(eng, fn, waits=([ch] if ch else []) + list(extra), inc=eng)
        return ch
    def sdma(eng, fn):
        nonlocal ch
        ch = p.dma(eng, fn, waits=[ch] if ch else [], inc="pld" if eng == "pool" else "ld3")
        return ch
    sdma("sp", lambda e: e.dma_start(out=gsl[:], in_=din["gsel"]))
    gv = din["gw1_slice"].rearrange("(a p) n -> p a n", p=128)
    for kt in range(20):
        bi = kt % 2
        def ld(e, kt=kt, bi=bi):
            return e.dma_start(out=gwb[:, bi, :], in_=gv[:, kt, :])
        sdma("sp", ld)
        def mm(e, kt=kt, bi=bi):
            ins = None
            for nb in range(4):
                ins = e.matmul(pRD[:, nb, :], gsl[:, kt, :], gwb[:, bi, 512 * nb:512 * nb + 512],
                               start=(kt == 0), stop=(kt == 19))
            return ins
        sstep("pe", mm)
    sstep("dve", lambda e: e.tensor_copy(gred[:], pRD[:].rearrange("p a b -> p (a b)")))
    if "gredump" in st["dbg"]:
        def dgr(e):
            return e.dma_start(out=st["dbg"]["gredump"], in_=gred[:])
        ch = p.dma("pool", dgr, waits=[ch], inc="st")
    sdma("pool", lambda e: e.dma_start(out=dram["cc_gw_in"], in_=gred[:]))
    def cc_fn(e):
        return e.collective_compute(
            "AllGather", ALU.bypass, replica_groups=[list(range(8))],
            ins=[dram["cc_gw_in"].opt()], outs=[dram["cc_gw_out"].opt()])
    ch = p.on("pool", cc_fn, waits=[ch], inc="cc", amt=1)
    st["gw1_done"] = ch
    sfree("pRD")
    sfree("gred", "gsl", "gwb")


def phase3_gate(st):
    nc, p, din, dram = st["nc"], st["p"], st["din"], st["dram"]
    salloc, sfree, palloc = st["salloc"], st["sfree"], st["palloc"]
    v16, o16 = st["v16"], st["o16"]
    eye16, eyef, ones1f, onescf = st["s_eye16"], st["s_eyef"], st["s_ones1f"], st["s_onescf"]

    rows = salloc("rows", [1, 128], F32)
    tmp512 = salloc("tmp512", [128, 512], F32)
    rowbc = salloc("rowbc", [128, 2, 512], F32)
    pacc = salloc("pacc", [128, KT], F32)
    firs = salloc("firs", [128, KT, 3 * 512], BF16)
    stats80 = salloc("stats80", [128, 512], F32)
    fw = salloc("fw", [128, KT, 82], F32)
    sel = salloc("sel", [128, 20, 80], BF16)
    s_sel20 = salloc("s_sel20", [20, 20, 128], F32)
    hsst = salloc("hsst", [128, KT, 512], F32)
    hs16st = salloc("hs16st", [128, KT, 512], BF16)
    sgw = salloc("sgw", [128, 9, 128], BF16)
    wo16 = salloc("wo16", [128, KT, D], BF16)
    hgate = salloc("hgate", [128, 16, 512], BF16)
    big512 = salloc("big512", [128, KT, 512], F32)
    sgwf = big512[:].rearrange("p a n -> p (a n)")[:, 0:1152].rearrange("p (a n) -> p a n", n=128)
    onrm = salloc("onrm", [128, KT, 512], BF16)
    gw2s = salloc("gw2s", [128, 16, 20], BF16)
    gb1s = salloc("gb1s", [128, 16], F32)
    normw = salloc("normw", [128, 2], F32)
    ltm = salloc("ltm", [128, 4, 20], F32)
    mx5 = salloc("mx5", [128, 4, 8], F32)
    prow = salloc("prow", [20, 512], F32)
    pb16 = salloc("pb16", [128, 512], F32)
    epsbc = salloc("epsbc", [128, 40], F32)
    sq512 = salloc("sq512", [128, 512], BF16)
    stats16 = salloc("stats16", [128, 512], BF16)
    gb2_16 = salloc("gb2_16", [1, 20], BF16)

    pA = palloc("pA", [128, 512], F32)
    pB = palloc("pB", [128, 512], F32)
    pC = palloc("pC", [128, 512], F32)
    pD = palloc("pD", [128, 64], F32)
    pE1 = palloc("pE1", [1, 512], F32)
    pE2 = palloc("pE2", [1, 512], F32)
    st["p3"] = dict(pA=pA, pB=pB, pC=pC, pD=pD, pE1=pE1, pE2=pE2, rowbc=rowbc,
                    tmp512=tmp512, big512=big512, sq512=sq512, pacc=pacc, rows=rows,
                    hsst=hsst, hs16st=hs16st)

    ch = st["delta_done"]
    def sstep(eng, fn, extra=()):
        nonlocal ch
        ch = p.on(eng, fn, waits=[ch] + list(extra), inc=eng)
        return ch
    def sdma(eng, fn):
        nonlocal ch
        ch = p.dma(eng, fn, waits=[ch], inc="ld3")
        return ch

    def ld1(e):
        sem = p.sems["ld3"]
        e.dma_start(out=fw[:, :, 0:3], in_=din["fir_s"].rearrange("(a p) j -> p a j", p=128)).then_inc(sem, 16)
        e.dma_start(out=fw[:, :, 3:18], in_=din["fir_m"].rearrange("(a p) j -> p a j", p=128)).then_inc(sem, 16)
        e.dma_start(out=fw[:, :, 18:82], in_=din["fir_l"].rearrange("(a p) j -> p a j", p=128)).then_inc(sem, 16)
        e.dma_start(out=s_sel20[:], in_=din["sel20"]).then_inc(sem, 16)
        e.dma_start(out=gb1s[:], in_=din["gb1"]).then_inc(sem, 16)
        e.dma_start(out=normw[:], in_=din["norm_w"]).then_inc(sem, 16)
        e.dma_start(out=rows[:, 0:4], in_=din["log_tau"].rearrange("(o h) -> o h", o=1)).then_inc(sem, 16)
        e.dma_start(out=rows[:, 8:28], in_=din["eps_logit"].rearrange("(o h) p -> o (h p)", o=1)).then_inc(sem, 16)
        e.dma_start(out=rows[:, 72:92], in_=din["gb2"].rearrange("(o j) -> o j", o=1)).then_inc(sem, 16)
        return None
    p.on("sp", ld1, waits=[ch])
    ch = p.bump("ld3", 16 * 9)
    for m in range(KT):
        def ldw(e, m=m):
            return e.dma_start(out=big512[:, :, 0:128],
                               in_=din["wo"].rearrange("(a p) n -> p a n", p=128)[:, :, 128 * m:128 * (m + 1)])
        sdma("sp", ldw)
        def cst(e, m=m):
            return e.tensor_copy(wo16[:, :, 128 * m:128 * (m + 1)], big512[:, :, 0:128])
        sstep("dve", cst)
    gw2stage = big512[:].rearrange("p a n -> p (a n)")[:, 0:320].rearrange("p (a j) -> p a j", j=20)
    sdma("sp", lambda e: e.dma_start(out=gw2stage, in_=din["gw2"].rearrange("(a p) j -> p a j", p=128)))
    sstep("dve", lambda e: e.tensor_copy(gw2s[:], gw2stage))

    sstep("dve", lambda e: e.tensor_scalar(out=rows[:, 4:8], in0=rows[:, 0:4], scalar1=-1.0,
                                           scalar2=None, op0=ALU.mult))
    sstep("dve", lambda e: e.tensor_tensor(out=rows[:, 4:8], in0=rows[:, 0:4], in1=rows[:, 4:8], op=ALU.max))
    sstep("dve", lambda e: e.tensor_single_scalar(out=rows[:, 0:4], in_=rows[:, 0:4], scalar=0.0, op=ALU.max))
    sstep("act", lambda e: e.activation(out=rows[:, 4:8], in_=rows[:, 4:8], func=AF.Exp, scale=-1.0))
    sstep("act", lambda e: e.activation(out=rows[:, 4:8], in_=rows[:, 4:8], func=AF.Ln, bias=1.0))
    sstep("dve", lambda e: e.tensor_tensor(out=rows[:, 0:4], in0=rows[:, 0:4], in1=rows[:, 4:8], op=ALU.add))
    sstep("dve", lambda e: e.tensor_scalar(out=rows[:, 0:4], in0=rows[:, 0:4], scalar1=0.5,
                                           scalar2=None, op0=ALU.add))
    sstep("dve", lambda e: e.reciprocal(rows[:, 0:4], rows[:, 0:4]))
    sstep("act", lambda e: e.activation(out=rows[:, 8:28], in_=rows[:, 8:28], func=AF.Sigmoid))
    sstep("dve", lambda e: e.tensor_scalar(out=rows[:, 8:28], in0=rows[:, 8:28], scalar1=0.2,
                                           scalar2=None, op0=ALU.mult))
    sstep("dve", lambda e: e.reduce_sum(rows[:, 28:32],
                                        rows[:, 8:28].rearrange("p (h q) -> p h q", h=4), axis=AXL.X))
    sstep("dve", lambda e: e.tensor_scalar(out=rows[:, 28:32], in0=rows[:, 28:32], scalar1=-1.0,
                                           scalar2=1.0, op0=ALU.mult, op1=ALU.add))
    def eps_fin(e):
        ins = None
        for q in range(5):
            ins = e.tensor_copy(rows[:, 32:52].rearrange("p (h q) -> p h q", q=5)[:, :, q], rows[:, 28:32])
        return ins
    sstep("dve", eps_fin)
    def itau_exp(e):
        ins = None
        for q in range(5):
            ins = e.tensor_copy(rows[:, 52:72].rearrange("p (h q) -> p h q", q=5)[:, :, q], rows[:, 0:4])
        return ins
    sstep("dve", itau_exp)
    sstep("pe", lambda e: e.matmul(pD[:, 0:20], ones1f[:], rows[:, 52:72], start=True, stop=True))
    sstep("dve", lambda e: e.tensor_copy(epsbc[:, 0:20], pD[:, 0:20]))
    def gw2_scale(e):
        ins = None
        for kt in range(16):
            ins = e.tensor_tensor(out=gw2s[:, kt, :], in0=gw2s[:, kt, :], in1=epsbc[:, 0:20], op=ALU.mult)
        return ins
    sstep("dve", gw2_scale)
    sstep("dve", lambda e: e.tensor_tensor(out=rows[:, 72:92], in0=rows[:, 72:92], in1=rows[:, 52:72], op=ALU.mult))
    sstep("dve", lambda e: e.tensor_copy(gb2_16[:], rows[:, 72:92]))
    sstep("pe", lambda e: e.matmul(pD[:, 0:20], ones1f[:], rows[:, 32:52], start=True, stop=True))
    sstep("dve", lambda e: e.tensor_copy(epsbc[:, 0:20], pD[:, 0:20]))
    sstep("pe", lambda e: e.matmul(pD[:, 0:20], ones1f[:], rows[:, 8:28], start=True, stop=True))
    sstep("dve", lambda e: e.tensor_copy(epsbc[:, 20:40], pD[:, 0:20]))
    sstep("dve", lambda e: e.memset(stats80[:], 0.0))
    sstep("dve", lambda e: e.memset(stats16[:], 0.0))
    if "gwout" in st["dbg"]:
        def dgw(e):
            return e.dma_start(out=st["dbg"]["gwout"], in_=dram["cc_gw_out"])
        ch = p.dma("pool", dgw, waits=[ch], inc="st")
    if "rowsdump" in st["dbg"]:
        def drows(e):
            return e.dma_start(out=st["dbg"]["rowsdump"], in_=rows[:])
        ch = p.dma("pool", drows, waits=[ch], inc="st")
    if "epsbcdump" in st["dbg"]:
        def depsbc(e):
            return e.dma_start(out=st["dbg"]["epsbcdump"], in_=epsbc[:])
        ch = p.dma("pool", depsbc, waits=[ch], inc="st")
    if "gw2dump" in st["dbg"]:
        def dgw2(e):
            return e.dma_start(out=st["dbg"]["gw2dump"], in_=gw2s[:])
        ch = p.dma("pool", dgw2, waits=[ch], inc="st")

    gveff = dram["cc_gw_out"].rearrange("(pp hh ss) n -> ss (pp hh) n", pp=5, hh=4)
    SBLK = [1, 2, 3, 0]
    FIRK = [(0, 3, 0), (1, 15, 3), (2, 64, 18)]
    def path_tile(pth, gg, t0):
        if pth < 3:
            return firs[:, gg, 512 * pth:512 * pth + 512]
        if pth == 3:
            return o16[:, gg, t0:t0 + ST]
        return v16[:, gg, 63 + t0:63 + t0 + ST]

    for s in range(NST):
        t0 = ST * s
        sdma("sp", lambda e, t0=t0: e.dma_start(
            out=hsst[:], in_=din["hsT"].rearrange("(a p) t -> p a t", p=128)[:, :, HALO + t0:HALO + t0 + ST]))
        sstep("dve", lambda e: e.tensor_copy(hs16st[:], hsst[:]))
        for (pth, KW, off) in FIRK:
            for gg in range(KT):
                def fir(e, pth=pth, KW=KW, off=off, gg=gg, t0=t0):
                    base = 63 + t0 - (KW - 1)
                    e.tensor_scalar(out=tmp512[:], in0=v16[:, gg, base:base + ST],
                                    scalar1=fw[:, gg, off:off + 1], scalar2=None, op0=ALU.mult)
                    for j in range(1, KW):
                        e.scalar_tensor_tensor(
                            out=tmp512[:], in0=v16[:, gg, base + j:base + j + ST],
                            scalar=fw[:, gg, off + j:off + j + 1], in1=tmp512[:],
                            op0=ALU.mult, op1=ALU.add)
                    return e.tensor_copy(firs[:, gg, 512 * pth:512 * pth + 512], tmp512[:])
                sstep("dve", fir)
        passes = [("sel_m", pB, None), ("sel_v", pA, "sq"), ("sel_a", pC, "abs")]
        for (selname, ppsum, xform) in passes:
            sdma("sp", lambda e, selname=selname: e.dma_start(out=sel[:], in_=din[selname]))
            for pth in range(P5):
                for gg in range(KT):
                    kt = pth * 8 + gg
                    if xform == "sq":
                        def xf(e, pth=pth, gg=gg, t0=t0):
                            tl = path_tile(pth, gg, t0)
                            return e.tensor_mul(sq512[:], tl, tl)
                        sstep("dve", xf)
                        rhs_t = sq512
                    elif xform == "abs":
                        def xf(e, pth=pth, gg=gg, t0=t0):
                            tl = path_tile(pth, gg, t0)
                            e.tensor_scalar(out=sq512[:], in0=tl, scalar1=-1.0, scalar2=None, op0=ALU.mult)
                            return e.tensor_tensor(out=sq512[:], in0=sq512[:], in1=tl, op=ALU.max)
                        sstep("dve", xf)
                        rhs_t = sq512
                    else:
                        rhs_t = None
                    def mmp(e, pth=pth, gg=gg, kt=kt, t0=t0, ppsum=ppsum, rhs_t=rhs_t):
                        rhs = rhs_t[:] if rhs_t is not None else path_tile(pth, gg, t0)
                        return e.matmul(ppsum[0:80, :], sel[:, pth * 4 + gg // 2, :], rhs,
                                        start=(kt == 0), stop=(kt == 39))
                    sstep("pe", mmp)
        sstep("dve", lambda e: e.tensor_copy(stats80[0:80, :], pA[0:80, :]))
        sstep("dve", lambda e: e.tensor_tensor(out=stats80[0:80, :], in0=stats80[0:80, :],
                                               in1=pB[0:80, :], op=ALU.add))
        sstep("dve", lambda e: e.tensor_tensor(out=stats80[0:80, :], in0=stats80[0:80, :],
                                               in1=pC[0:80, :], op=ALU.add))
        sstep("act", lambda e: e.activation(out=stats80[0:20, :], in_=stats80[0:20, :], func=AF.Sqrt))
        sstep("dve", lambda e: e.tensor_copy(stats16[0:80, :], stats80[0:80, :]))
        if "statsdump" in st["dbg"]:
            def dstat(e, s=s):
                return e.dma_start(out=st["dbg"]["statsdump"][s], in_=stats80[:])
            ch = p.dma("pool", dstat, waits=[ch], inc="st")
        for m in range(16):
            def ldm2(e, m=m):
                sem = p.sems["ld3"]
                e.dma_start(out=sgwf[:, 0:8, :],
                            in_=din["gw1_hs"].rearrange("(a p) n -> p a n", p=128)[:, :, 128 * m:128 * (m + 1)]).then_inc(sem, 16)
                for sv in range(4):
                    blk = SBLK[sv]
                    e.dma_start(out=sgwf[20 * blk:20 * blk + 20, 8, :],
                                in_=gveff[sv, :, 128 * m:128 * (m + 1)]).then_inc(sem, 16)
                return None
            p.on("sp", ldm2, waits=[ch])
            ch = p.bump("ld3", 16 * 5)
            sstep("dve", lambda e: e.tensor_copy(sgw[:], sgwf[:]))
            def gmm(e, m=m, t0=t0):
                for kt in range(KT):
                    e.matmul(pA[:], sgw[:, kt, :], hs16st[:, kt, :], start=(kt == 0), stop=False)
                return e.matmul(pA[:], sgw[:, 8, :], stats16[:], start=False, stop=True)
            sstep("pe", gmm)
            sstep("act", lambda e, m=m: e.activation(out=hgate[:, m, :], in_=pA[:], func=AF.Gelu,
                                                     bias=gb1s[:, m:m + 1]))
        for tt in range(4):
            def lmm(e, tt=tt):
                for kt in range(16):
                    e.matmul(pD[:, 0:20], hgate[:, kt, 128 * tt:128 * tt + 128], gw2s[:, kt, :],
                             start=(kt == 0), stop=False)
                return e.matmul(pD[:, 0:20], st["s_ones1"][:], gb2_16[:], start=False, stop=True)
            sstep("pe", lmm)
            sstep("dve", lambda e, tt=tt: e.tensor_copy(ltm[:, tt, :], pD[:, 0:20]))
        if "logitsdump" in st["dbg"]:
            def dlog(e, s=s):
                return e.dma_start(out=st["dbg"]["logitsdump"][s], in_=ltm[:])
            ch = p.dma("pool", dlog, waits=[ch], inc="st")
        if True:
            pass
        mxv = mx5[:].rearrange("p a b -> p (a b)")
        def smax1(e):
            return e.reduce_max(mxv[:, 0:16].rearrange("p (a h) -> p a h", a=4),
                                ltm[:].rearrange("p a (h q) -> p a h q", q=5), axis=AXL.X)
        sstep("dve", smax1)
        def smax2(e):
            ins = None
            lv = ltm[:].rearrange("p a (h q) -> p a h q", q=5)
            mv = mxv[:, 0:16].rearrange("p (a h) -> p a h", a=4)
            for q in range(5):
                ins = e.tensor_tensor(out=lv[:, :, :, q], in0=lv[:, :, :, q], in1=mv, op=ALU.subtract)
            return ins
        sstep("dve", smax2)
        sstep("act", lambda e: e.activation(out=ltm[:], in_=ltm[:], func=AF.Exp))
        def ssum(e):
            return e.reduce_sum(mxv[:, 16:32].rearrange("p (a h) -> p a h", a=4),
                                ltm[:].rearrange("p a (h q) -> p a h q", q=5), axis=AXL.X)
        sstep("dve", ssum)
        sstep("dve", lambda e: e.reciprocal(mxv[:, 16:32], mxv[:, 16:32]))
        def pnorm(e):
            ins = None
            lv = ltm[:].rearrange("p a (h q) -> p a h q", q=5)
            mv = mxv[:, 16:32].rearrange("p (a h) -> p a h", a=4)
            for q in range(5):
                ins = e.tensor_tensor(out=lv[:, :, :, q], in0=lv[:, :, :, q], in1=mv, op=ALU.mult)
            return ins
        sstep("dve", pnorm)
        def peps1(e):
            ins = None
            for tt in range(4):
                ins = e.tensor_tensor(out=ltm[:, tt, :], in0=ltm[:, tt, :], in1=epsbc[:, 0:20], op=ALU.mult)
            return ins
        sstep("dve", peps1)
        def peps2(e):
            ins = None
            for tt in range(4):
                ins = e.tensor_tensor(out=ltm[:, tt, :], in0=ltm[:, tt, :], in1=epsbc[:, 20:40], op=ALU.add)
            return ins
        sstep("dve", peps2)
        if "probs" in st["dbg"]:
            def dprobs(e, s=s):
                return e.dma_start(out=st["dbg"]["probs"].rearrange("a p b c -> a p b c")[s], in_=ltm[:])
            ch = p.dma("pool", dprobs, waits=[ch], inc="st")
        sstep("dve", lambda e: e.tensor_copy(tmp512[:, 0:80], ltm[:].rearrange("p a j -> p (a j)")))
        for tt in range(4):
            def trp(e, tt=tt):
                return e.transpose(pC[0:20, 128 * tt:128 * tt + 128], tmp512[:, 20 * tt:20 * tt + 20], eyef[:])
            sstep("pe", trp)
        sstep("dve", lambda e: e.tensor_copy(prow[:], pC[0:20, :]))
        for h in range(H):
            for pth in range(P5):
                j = h * P5 + pth
                def bcp(e, j=j):
                    return e.matmul(pB[:], s_sel20[:, j, :], prow[:], start=True, stop=True)
                sstep("pe", bcp)
                sstep("dve", lambda e: e.tensor_copy(pb16[:], pB[:]))
                for t2 in range(2):
                    gg = 2 * h + t2
                    def comb(e, pth=pth, gg=gg, t0=t0, first=(pth == 0)):
                        tl = path_tile(pth, gg, t0)
                        if first:
                            return e.tensor_tensor(out=big512[:, gg, :], in0=tl, in1=pb16[:], op=ALU.mult)
                        e.tensor_tensor(out=tmp512[:], in0=tl, in1=pb16[:], op=ALU.mult)
                        return e.tensor_tensor(out=big512[:, gg, :], in0=big512[:, gg, :],
                                               in1=tmp512[:], op=ALU.add)
                    sstep("dve", comb)
        if "ocomb" in st["dbg"]:
            def docomb(e, s=s):
                return e.dma_start(out=st["dbg"]["ocomb"][s], in_=big512[:])
            ch = p.dma("pool", docomb, waits=[ch], inc="st")
        for h in range(H):
            for t2 in range(2):
                gg = 2 * h + t2
                sstep("dve", lambda e, gg=gg: e.tensor_mul(tmp512[:], big512[:, gg, :], big512[:, gg, :]))
                sstep("pe", lambda e, t2=t2: e.matmul(pE1[:], onescf[:], tmp512[:],
                                                      start=(t2 == 0), stop=(t2 == 1)))
            sstep("act", lambda e: e.activation(out=tmp512[0:1, :], in_=pE1[:], func=AF.Sqrt,
                                                scale=1.0 / 256.0, bias=1e-5))
            sstep("dve", lambda e: e.reciprocal(tmp512[0:1, :], tmp512[0:1, :]))
            sstep("pe", lambda e: e.matmul(pB[:], ones1f[:], tmp512[0:1, :], start=True, stop=True))
            sstep("dve", lambda e: e.tensor_copy(rowbc[:, 0, :], pB[:]))
            for t2 in range(2):
                gg = 2 * h + t2
                def appl(e, gg=gg, t2=t2):
                    return e.scalar_tensor_tensor(out=onrm[:, gg, :], in0=big512[:, gg, :],
                                                  scalar=normw[:, t2:t2 + 1], in1=rowbc[:, 0, :],
                                                  op0=ALU.mult, op1=ALU.mult)
                sstep("dve", appl)
        for m in range(KT):
            def womm(e, m=m):
                ins = None
                for kt in range(KT):
                    ins = e.matmul(pA[:], wo16[:, kt, 128 * m:128 * (m + 1)], onrm[:, kt, :],
                                   start=(kt == 0), stop=(kt == KT - 1))
                return ins
            sstep("pe", womm)
            sstep("dve", lambda e, m=m: e.tensor_tensor(out=big512[:, m, :], in0=pA[:],
                                                        in1=hsst[:, m, :], op=ALU.add))
        ln_emit(st, sstep, big512)
        ch = st["_ln_ch"]
        def macc(e, s=s):
            ins = None
            dst = pacc if s == 0 else tmp512[:, 0:KT].rearrange("p (a o) -> p a o", o=1)
            for gg in range(KT):
                if s == 0:
                    ins = e.reduce_sum(pacc[:, gg:gg + 1], big512[:, gg, :], axis=AXL.X)
                else:
                    ins = e.reduce_sum(tmp512[:, gg:gg + 1], big512[:, gg, :], axis=AXL.X)
            return ins
        sstep("dve", macc)
        if s > 0:
            sstep("dve", lambda e: e.tensor_tensor(out=pacc[:], in0=pacc[:], in1=tmp512[:, 0:KT], op=ALU.add))
        sdma("pool", lambda e, t0=t0: e.dma_start(
            out=dram["x1_dram"].rearrange("(a p) t -> p a t", p=128)[:, :, t0:t0 + ST], in_=big512[:]))

    emit_debug(st, dict(stats80=stats80, hgate=hgate))
    if "x1dump" in st["dbg"]:
        def stx1(eng):
            return eng.dma_start(out=st["dbg"]["x1dump"], in_=dram["x1_dram"])
        ch = p.dma("pool", stx1, waits=[ch], inc="st")
    sdma("pool", lambda e: e.dma_start(out=dram["cc_mean_in"], in_=pacc[:]))
    def ccm(e):
        return e.collective_compute(
            "AllReduce", ALU.add, replica_groups=[list(range(8))],
            ins=[dram["cc_mean_in"].opt()], outs=[dram["cc_mean_out"].opt()])
    ch = p.on("pool", ccm, waits=[ch], inc="cc", amt=1)
    st["chain3"] = ch
    for nm in ("pE2", "pE1", "pD", "pC", "pB", "pA"):
        sfree(nm)
    for nm in ("gb2_16", "stats16", "sq512", "epsbc", "pb16", "prow", "mx5", "ltm", "normw", "gb1s",
               "gw2s", "onrm", "big512", "hgate", "wo16", "sgw", "hs16st", "hsst",
               "s_sel20", "sel", "fw", "stats80", "firs"):
        sfree(nm)
    for nm in ("pacc", "rowbc", "tmp512", "rows"):
        sfree(nm)
    sfree("o16", "v16")


def ln_emit(st, sstep, xbuf):
    p = st["p"]
    d = st["p3"]
    pE1, pE2, tmp512, rowbc = d["pE1"], d["pE2"], d["tmp512"], d["rowbc"]
    onescf, ones1f = st["s_onescf"], st["s_ones1f"]
    for gg in range(KT):
        sstep("pe", lambda e, gg=gg: e.matmul(pE1[:], onescf[:], xbuf[:, gg, :],
                                              start=(gg == 0), stop=(gg == KT - 1)))
    for gg in range(KT):
        sstep("dve", lambda e, gg=gg: e.tensor_mul(tmp512[:], xbuf[:, gg, :], xbuf[:, gg, :]))
        sstep("pe", lambda e, gg=gg: e.matmul(pE2[:], onescf[:], tmp512[:],
                                              start=(gg == 0), stop=(gg == KT - 1)))
    sstep("dve", lambda e: e.tensor_scalar(out=tmp512[0:1, :], in0=pE1[:], scalar1=1.0 / D,
                                           scalar2=None, op0=ALU.mult))
    sstep("dve", lambda e: e.tensor_mul(rowbc[0:1, 1, :], tmp512[0:1, :], tmp512[0:1, :]))
    sstep("dve", lambda e: e.tensor_scalar(out=rowbc[0:1, 1, :], in0=rowbc[0:1, 1, :], scalar1=-1.0,
                                           scalar2=None, op0=ALU.mult))
    sstep("dve", lambda e: e.scalar_tensor_tensor(out=rowbc[0:1, 1, :], in0=pE2[:], scalar=1.0 / D,
                                                  in1=rowbc[0:1, 1, :], op0=ALU.mult, op1=ALU.add))
    sstep("act", lambda e: e.activation(out=rowbc[0:1, 1, :], in_=rowbc[0:1, 1, :], func=AF.Sqrt, bias=1e-5))
    sstep("dve", lambda e: e.reciprocal(rowbc[0:1, 1, :], rowbc[0:1, 1, :]))
    sstep("pe", lambda e: e.matmul(d["pB"][:], ones1f[:], tmp512[0:1, :], start=True, stop=True))
    sstep("dve", lambda e: e.tensor_copy(rowbc[:, 0, :], d["pB"][:]))
    sstep("pe", lambda e: e.matmul(d["pB"][:], ones1f[:], rowbc[0:1, 1, :], start=True, stop=True))
    sstep("dve", lambda e: e.tensor_copy(rowbc[:, 1, :], d["pB"][:]))
    for gg in range(KT):
        def appl(e, gg=gg):
            e.tensor_tensor(out=xbuf[:, gg, :], in0=xbuf[:, gg, :], in1=rowbc[:, 0, :], op=ALU.subtract)
            return e.tensor_tensor(out=xbuf[:, gg, :], in0=xbuf[:, gg, :], in1=rowbc[:, 1, :], op=ALU.mult)
        tok = sstep("dve", appl)
    st["_ln_ch"] = tok


def broadcast_pair(a, b):
    import concourse.bass as _b
    return _b.broadcast_tensor_aps(a, b)


def phase4_ffn(st):
    nc, p, din, dram, out = st["nc"], st["p"], st["din"], st["dram"], st["out"]
    salloc, sfree, palloc = st["salloc"], st["sfree"], st["palloc"]
    eyef, ones1f, onescf = st["s_eyef"], st["s_ones1f"], st["s_onescf"]

    w1s = salloc("w1s", [128, KT, D], BF16)
    w2s = salloc("w2s", [128, KT, 512], BF16)
    w3s = salloc("w3s", [128, 4, 256], BF16)
    w4s = salloc("w4s", [128, 2, 1], BF16)
    g1s = salloc("g1s", [128, KT, 512], BF16)
    g2s = salloc("g2s", [128, 4, 256], BF16)
    g3s = salloc("g3s", [128, 2, 1], BF16)
    bias = salloc("bias", [128, 16], F32)
    rows4 = salloc("rows4", [1, 128], F32)
    gcb = salloc("gcb", [128, 8], F32)
    gcx = salloc("gcx", [128, KT], BF16)
    gch = salloc("gch", [128, 4], BF16)
    pacc4 = salloc("pacc4", [128, KT], F32)
    xstage = salloc("xstage", [128, KT, D], F32)
    S2 = 2
    xs_ = [salloc(f"xs{i}", [128, KT, 512], F32) for i in range(S2)]
    xo_ = [salloc(f"xo{i}", [128, KT, 512], F32) for i in range(S2)]
    x16_ = [salloc(f"x16{i}", [128, KT, 512], BF16) for i in range(S2)]
    t1_ = [salloc(f"t1_{i}", [128, KT, 512], BF16) for i in range(S2)]
    t2_ = [salloc(f"t2_{i}", [128, 4, 512], BF16) for i in range(S2)]
    t3_ = [salloc(f"t3_{i}", [128, 2, 512], BF16) for i in range(S2)]
    tmp_ = [salloc(f"tmp4_{i}", [128, 512], F32) for i in range(S2)]
    cbc_ = [salloc(f"cbc{i}", [128, 512], F32) for i in range(S2)]
    lnr_ = [salloc(f"lnr{i}", [128, 2, 512], F32) for i in range(S2)]
    ob_ = [salloc(f"ob{i}", [128, D], F32) for i in range(S2)]

    pA_ = [palloc(f"pA4{i}", [128, 2, 512], F32) for i in range(S2)]
    pB = palloc("pB4", [128, 512], F32)
    pE1 = palloc("pE14", [1, 512], F32)
    pE2 = palloc("pE24", [1, 512], F32)

    ch = st["chain3"]
    def sstep(eng, fn, extra=()):
        nonlocal ch
        ch = p.on(eng, fn, waits=[ch] + list(extra), inc=eng)
        return ch
    def sdma(eng, fn):
        nonlocal ch
        ch = p.dma(eng, fn, waits=[ch], inc="pld" if eng == "pool" else "ld3")
        return ch

    def l1(e):
        p.sems and e.dma_start(out=xstage[:], in_=din["cn_w1"].rearrange("(a p) n -> p a n", p=128)[:, :, 0:512]).then_inc(p.sems["ld3"], 16)
        return None
    def mkload(srcap, dst_cols=512):
        def f(e):
            e.dma_start(out=xstage[:, :, 0:dst_cols], in_=srcap).then_inc(p.sems["ld3"], 16)
            return None
        return f
    p.on("sp", mkload(din["cn_w1"].rearrange("(a p) n -> p a n", p=128)[:, :, 0:512]), waits=[ch]); ch = p.bump("ld3", 16)
    sstep("dve", lambda e: e.tensor_copy(w1s[:, :, 0:512], xstage[:, :, 0:512]))
    p.on("sp", mkload(din["cn_w1"].rearrange("(a p) n -> p a n", p=128)[:, :, 512:1024]), waits=[ch]); ch = p.bump("ld3", 16)
    sstep("dve", lambda e: e.tensor_copy(w1s[:, :, 512:1024], xstage[:, :, 0:512]))
    p.on("sp", mkload(din["cn_w2"].rearrange("(a p) n -> p a n", p=128)), waits=[ch]); ch = p.bump("ld3", 16)
    sstep("dve", lambda e: e.tensor_copy(w2s[:], xstage[:, :, 0:512]))
    def l4b(e):
        sem = p.sems["ld3"]
        e.dma_start(out=xstage[:, 0:4, 0:256], in_=din["cn_w3"].rearrange("(a p) n -> p a n", p=128)).then_inc(sem, 16)
        e.dma_start(out=xstage[:, 4:6, 0:1], in_=din["cn_w4"].rearrange("(a p) n -> p a n", p=128),
                    allow_slow_non_contiguous=True).then_inc(sem, 16)
        e.dma_start(out=xstage[:, 6:7, 0:8].rearrange("p a n -> p (a n)"), in_=din["cn_b1"]).then_inc(sem, 16)
        e.dma_start(out=xstage[:, 7:8, 0:4].rearrange("p a n -> p (a n)"), in_=din["cn_b2"]).then_inc(sem, 16)
        e.dma_start(out=xstage[:, 7:8, 4:6].rearrange("p a n -> p (a n)"), in_=din["cn_b3"]).then_inc(sem, 16)
        e.dma_start(out=xstage[0:1, 7, 6:7], in_=din["cn_b4"].rearrange("(o a) -> o a", o=1)).then_inc(sem, 16)
        return None
    p.on("sp", l4b, waits=[ch]); ch = p.bump("ld3", 16 * 6)
    def cpw34(e):
        e.tensor_copy(w3s[:], xstage[:, 0:4, 0:256])
        e.tensor_copy(w4s[:], xstage[:, 4:6, 0:1])
        e.tensor_copy(bias[:, 0:8], xstage[:, 6, 0:8])
        e.tensor_copy(bias[:, 8:12], xstage[:, 7, 0:4])
        e.tensor_copy(bias[:, 12:14], xstage[:, 7, 4:6])
        return e.tensor_copy(rows4[:, 100:101], xstage[0:1, 7, 6:7])
    sstep("dve", cpw34)
    def l5(e):
        sem = p.sems["ld3"]
        e.dma_start(out=xstage[:, 0:8, 0:512], in_=din["gc_w1"].rearrange("(a p) n -> p a n", p=128)).then_inc(sem, 16)
        e.dma_start(out=xstage[:, 0:4, 512:768], in_=din["gc_w2"].rearrange("(a p) n -> p a n", p=128)).then_inc(sem, 16)
        e.dma_start(out=xstage[:, 4:6, 512:513], in_=din["gc_w3"].rearrange("(a p) n -> p a n", p=128),
                    allow_slow_non_contiguous=True).then_inc(sem, 16)
        e.dma_start(out=xstage[0:1, 6, 512:513], in_=din["adaptive_range"].rearrange("(o a) -> o a", o=1)).then_inc(sem, 16)
        e.dma_start(out=xstage[:, 6:7, 513:517].rearrange("p a n -> p (a n)"), in_=din["gc_b1"]).then_inc(sem, 16)
        e.dma_start(out=xstage[:, 6:7, 517:519].rearrange("p a n -> p (a n)"), in_=din["gc_b2"]).then_inc(sem, 16)
        e.dma_start(out=xstage[0:1, 6, 519:520], in_=din["gc_b3"].rearrange("(o a) -> o a", o=1)).then_inc(sem, 16)
        return None
    p.on("sp", l5, waits=[ch]); ch = p.bump("ld3", 16 * 7)
    def cpg(e):
        e.tensor_copy(g1s[:], xstage[:, 0:8, 0:512])
        e.tensor_copy(g2s[:], xstage[:, 0:4, 512:768])
        e.tensor_copy(g3s[:], xstage[:, 4:6, 512:513])
        e.tensor_copy(rows4[:, 102:103], xstage[0:1, 6, 512:513])
        e.tensor_copy(gcb[:, 0:4], xstage[:, 6, 513:517])
        e.tensor_copy(gcb[:, 4:6], xstage[:, 6, 517:519])
        return e.tensor_copy(rows4[:, 101:102], xstage[0:1, 6, 519:520])
    sstep("dve", cpg)

    sdma("pool", lambda e: e.dma_start(out=pacc4[:], in_=dram["cc_mean_out"]))
    sstep("dve", lambda e: e.tensor_scalar(out=gcx[:], in0=pacc4[:], scalar1=1.0 / 16384.0,
                                           scalar2=None, op0=ALU.mult))
    def gc1b(e):
        ins = None
        for m in range(4):
            for kt in range(KT):
                ins = e.matmul(pB[:, m:m + 1], g1s[:, kt, 128 * m:128 * (m + 1)], gcx[:, kt:kt + 1],
                               start=(kt == 0), stop=(kt == KT - 1))
        return ins
    sstep("pe", gc1b)
    def gc1act(e):
        ins = None
        for m in range(4):
            ins = e.activation(out=gch[:, m:m + 1], in_=pB[:, m:m + 1], func=AF.Gelu,
                               bias=gcb[:, m:m + 1])
        return ins
    sstep("act", gc1act)
    def gc2(e):
        ins = None
        for m in range(2):
            for kt in range(4):
                ins = e.matmul(pB[:, 8 + m:9 + m], g2s[:, kt, 128 * m:128 * (m + 1)], gch[:, kt:kt + 1],
                               start=(kt == 0), stop=(kt == 3))
        return ins
    sstep("pe", gc2)
    def gc2act(e):
        ins = None
        for m in range(2):
            ins = e.activation(out=gch[:, m:m + 1], in_=pB[:, 8 + m:9 + m],
                               func=AF.Gelu, bias=gcb[:, 4 + m:5 + m])
        return ins
    sstep("act", gc2act)
    def gc3(e):
        ins = None
        for kt in range(2):
            ins = e.matmul(pB[0:1, 12:13], g3s[:, kt, 0:1], gch[:, kt:kt + 1],
                           start=(kt == 0), stop=(kt == 1))
        return ins
    sstep("pe", gc3)
    sstep("act", lambda e: e.activation(out=rows4[:, 103:104], in_=pB[0:1, 12:13], func=AF.Sigmoid,
                                        bias=rows4[:, 101:102]))
    def couplc(e):
        e.tensor_scalar(out=rows4[:, 104:105], in0=rows4[:, 103:104], scalar1=0.7, scalar2=None, op0=ALU.mult)
        e.tensor_scalar(out=rows4[:, 105:106], in0=rows4[:, 102:103], scalar1=2.0, scalar2=None, op0=ALU.mult)
        return e.tensor_scalar(out=rows4[:, 106:107], in0=rows4[:, 102:103], scalar1=-1.0, scalar2=0.1,
                               op0=ALU.mult, op1=ALU.add)
    sstep("dve", couplc)
    consts_tok = ch

    twist_map = [(0, 4, 1), (1, 6, -1), (2, 0, -1), (3, 2, 1)]
    x1v = dram["x1_dram"].rearrange("(a p) t -> p a t", p=128)
    slot_tok = [consts_tok, consts_tok]
    rE = consts_tok

    def sl_on(sig, eng, fn, extra=()):
        slot_tok[sig] = p.on(eng, fn, waits=[slot_tok[sig]] + list(extra), inc=eng)
        return slot_tok[sig]
    def sl_dma(sig, eng, fn, extra=()):
        slot_tok[sig] = p.dma(eng, fn, waits=[slot_tok[sig]] + list(extra),
                              inc="pld" if eng == "pool" else "ld3")
        return slot_tok[sig]

    def stage_load(s, sig):
        t0 = ST * s
        sl_dma(sig, "sp", lambda e: e.dma_start(out=xs_[sig][:], in_=x1v[:, :, t0:t0 + ST]))
        sl_on(sig, "dve", lambda e: e.tensor_copy(x16_[sig][:], xs_[sig][:]))

    def stage_cycle(s, sig, cyc):
        nonlocal rE
        cur = xs_[sig] if cyc % 2 == 0 else xo_[sig]
        nxt = xo_[sig] if cyc % 2 == 0 else xs_[sig]
        x16 = x16_[sig]
        pA = pA_[sig]
        gtok = {}
        specs = [(w1s, x16, KT, KT, t1_[sig], 0), (w2s, t1_[sig], KT, 4, t2_[sig], 8),
                 (w3s, t2_[sig], 4, 2, t3_[sig], 12)]
        for (wt, rhs, nk, nm, dst, boff) in specs:
            for m in range(nm):
                bi = m % 2
                def mmf(e, wt=wt, rhs=rhs, nk=nk, m=m, bi=bi, pA=pA):
                    ins = None
                    for kt in range(nk):
                        ins = e.matmul(pA[:, bi, :], wt[:, kt, 128 * m:128 * (m + 1)], rhs[:, kt, :],
                                       start=(kt == 0), stop=(kt == nk - 1))
                    return ins
                extra = [gtok[m - 2]] if (m - 2) in gtok else []
                sl_on(sig, "pe", mmf, extra=extra)
                def gelf(e, m=m, bi=bi, dst=dst, boff=boff, pA=pA):
                    return e.activation(out=dst[:, m, :], in_=pA[:, bi, :], func=AF.Gelu,
                                        bias=bias[:, boff + m:boff + m + 1])
                gtok[m] = p.on("act", gelf, waits=[slot_tok[sig]], inc="act")
            gtok = {}
            slot_tok[sig] = p.on("pe", lambda e: e.nop(),
                                 waits=[slot_tok[sig], ("act", p.counts["act"])], inc="pe")
        def m4(e):
            ins = None
            for kt in range(2):
                ins = e.matmul(pE1[:], w4s[:, kt, :], t3_[sig][:, kt, :], start=(kt == 0), stop=(kt == 1))
            return ins
        tokm4 = p.on("pe", m4, waits=[slot_tok[sig], rE], inc="pe")
        slot_tok[sig] = tokm4
        sl_on(sig, "act", lambda e: e.activation(out=tmp_[sig][0:1, :], in_=pE1[:], func=AF.Sigmoid,
                                                 bias=rows4[:, 100:101]))
        rE = slot_tok[sig]
        sl_on(sig, "dve", lambda e: e.tensor_scalar(out=tmp_[sig][0:1, :], in0=tmp_[sig][0:1, :], scalar1=0.3,
                                                    scalar2=rows4[:, 104:105], op0=ALU.mult, op1=ALU.add))
        sl_on(sig, "dve", lambda e: e.tensor_scalar(out=tmp_[sig][0:1, :], in0=tmp_[sig][0:1, :],
                                                    scalar1=rows4[:, 105:106], scalar2=rows4[:, 106:107],
                                                    op0=ALU.mult, op1=ALU.add))
        def bcc(e):
            return e.matmul(pA[:, 0, :], ones1f[:], tmp_[sig][0:1, :], start=True, stop=True)
        sl_on(sig, "pe", bcc)
        sl_on(sig, "dve", lambda e: e.tensor_copy(cbc_[sig][:], pA[:, 0, :]))
        for (qd, gsrc, sign) in twist_map:
            for t2i in range(2):
                gd = 2 * qd + t2i
                gs = gsrc + t2i
                def tw(e, gd=gd, gs=gs, sign=sign, cur=cur, nxt=nxt):
                    e.tensor_mul(tmp_[sig][:], cur[:, gs, :], cbc_[sig][:])
                    return e.tensor_tensor(out=nxt[:, gd, :], in0=cur[:, gd, :], in1=tmp_[sig][:],
                                           op=ALU.add if sign > 0 else ALU.subtract)
                sl_on(sig, "dve", tw)
        if cyc < 2:
            sl_on(sig, "dve", lambda e: e.tensor_copy(x16_[sig][:], nxt[:]))

    def stage_end(s, sig):
        nonlocal rE
        t0 = ST * s
        cur = xo_[sig] if 3 % 2 == 1 else xs_[sig]
        other = xs_[sig]
        fre = xs_[sig] if cur is xo_[sig] else xo_[sig]
        sl_dma(sig, "sp", lambda e: e.dma_start(out=fre[:], in_=x1v[:, :, t0:t0 + ST]))
        for gg in range(KT):
            sl_on(sig, "dve", lambda e, gg=gg: e.tensor_tensor(out=cur[:, gg, :], in0=cur[:, gg, :],
                                                               in1=fre[:, gg, :], op=ALU.add))
        def lnstep(eng, fn, extra=()):
            nonlocal rE
            tok = p.on(eng, fn, waits=[slot_tok[sig], rE] + list(extra), inc=eng)
            slot_tok[sig] = tok
            rE = tok
            return tok
        for gg in range(KT):
            lnstep("pe", lambda e, gg=gg: e.matmul(pE1[:], onescf[:], cur[:, gg, :],
                                                   start=(gg == 0), stop=(gg == KT - 1)))
        for gg in range(KT):
            lnstep("dve", lambda e, gg=gg: e.tensor_mul(tmp_[sig][:], cur[:, gg, :], cur[:, gg, :]))
            lnstep("pe", lambda e, gg=gg: e.matmul(pE2[:], onescf[:], tmp_[sig][:],
                                                   start=(gg == 0), stop=(gg == KT - 1)))
        lnstep("dve", lambda e: e.tensor_scalar(out=tmp_[sig][0:1, :], in0=pE1[:], scalar1=1.0 / D,
                                                scalar2=None, op0=ALU.mult))
        lnstep("dve", lambda e: e.tensor_mul(lnr_[sig][0:1, 1, :], tmp_[sig][0:1, :], tmp_[sig][0:1, :]))
        lnstep("dve", lambda e: e.tensor_scalar(out=lnr_[sig][0:1, 1, :], in0=lnr_[sig][0:1, 1, :],
                                                scalar1=-1.0, scalar2=None, op0=ALU.mult))
        lnstep("dve", lambda e: e.scalar_tensor_tensor(out=lnr_[sig][0:1, 1, :], in0=pE2[:], scalar=1.0 / D,
                                                       in1=lnr_[sig][0:1, 1, :], op0=ALU.mult, op1=ALU.add))
        lnstep("act", lambda e: e.activation(out=lnr_[sig][0:1, 1, :], in_=lnr_[sig][0:1, 1, :],
                                             func=AF.Sqrt, bias=1e-5))
        lnstep("dve", lambda e: e.reciprocal(lnr_[sig][0:1, 1, :], lnr_[sig][0:1, 1, :]))
        lnstep("pe", lambda e: e.matmul(pB[:], ones1f[:], tmp_[sig][0:1, :], start=True, stop=True))
        lnstep("dve", lambda e: e.tensor_copy(lnr_[sig][:, 0, :], pB[:]))
        lnstep("pe", lambda e: e.matmul(pB[:], ones1f[:], lnr_[sig][0:1, 1, :], start=True, stop=True))
        lnstep("dve", lambda e: e.tensor_copy(lnr_[sig][:, 1, :], pB[:]))
        for gg in range(KT):
            def appl(e, gg=gg):
                e.tensor_tensor(out=cur[:, gg, :], in0=cur[:, gg, :], in1=lnr_[sig][:, 0, :], op=ALU.subtract)
                return e.tensor_tensor(out=cur[:, gg, :], in0=cur[:, gg, :], in1=lnr_[sig][:, 1, :], op=ALU.mult)
            sl_on(sig, "dve", appl)
        for tt in range(4):
            for gg in range(KT):
                def trx2(e, gg=gg, tt=tt):
                    return e.transpose(pA_[sig][:, 0, 0:128], cur[:, gg, 128 * tt:128 * tt + 128], eyef[:])
                sl_on(sig, "pe", trx2)
                sl_on(sig, "dve", lambda e, gg=gg: e.tensor_copy(ob_[sig][:, 128 * gg:128 * gg + 128],
                                                                 pA_[sig][:, 0, 0:128]))
            def sto(e, tt=tt, t0=t0):
                return e.dma_start(out=out[t0 + 128 * tt:t0 + 128 * tt + 128, :], in_=ob_[sig][:])
            slot_tok[sig] = p.dma("sp", sto, waits=[slot_tok[sig]], inc="st")

    for pair in range(2):
        sa, sb = 2 * pair, 2 * pair + 1
        stage_load(sa, 0)
        stage_load(sb, 1)
        for cyc in range(3):
            stage_cycle(sa, 0, cyc)
            stage_cycle(sb, 1, cyc)
        stage_end(sa, 0)
        stage_end(sb, 1)

    def fin(e):
        return e.nop()
    p.on("sp", fin, waits=[slot_tok[0], slot_tok[1]])
    st["ffn_done"] = slot_tok[1]
